# version 10
# baseline (speedup 1.0000x reference)
import numpy as np
import ml_dtypes
from contextlib import ExitStack

import concourse.bass as bass
import concourse.mybir as mybir
from concourse.bass_utils import run_bass_kernel_spmd

F32 = mybir.dt.float32
BF16 = mybir.dt.bfloat16
AF = mybir.ActivationFunctionType
ALU = mybir.AluOpType
AX = mybir.AxisListType
BF16NP = ml_dtypes.bfloat16

NCORES = 8
BPC = 4
S = 2048
C = 1024
NT = C // 128
CH = 512
NCH = S // CH
NH = 16
HD = 64
A1COLS = NH * 16 * 65



class Prog:
    ENGS = ("tensor", "vector", "scalar", "gpsimd", "sync")

    def __init__(self):
        self.nc = bass.Bass("TRN2", target_bir_lowering=False)
        self.ops = {e: [] for e in self.ENGS}
        self.ctx = ExitStack()
        self.ctrs = {}
        for e in self.ENGS:
            self._mkctr(e)
        for d in ("gdma", "sdma", "odma", "xdma", "c8d"):
            self._mkctr(d)
        self.ring_busy = {}

    def _mkctr(self, name):
        h = self.ctx.enter_context(self.nc.semaphore(name))
        self.ctrs[name] = [h, 0]

    def ctr(self, name):
        if name not in self.ctrs:
            self._mkctr(name)
        return name

    def sbuf(self, name, shape, dt):
        return self.ctx.enter_context(self.nc.sbuf_tensor(name, shape, dt))

    def psum(self, name, shape, dt=F32):
        return self.ctx.enter_context(self.nc.psum_tensor(name, shape, dt))

    def add(self, eng, build, deps=(), ctr=None, n=1, fifo=False):
        cname = ctr or eng
        h, cnt = self.ctrs[cname]
        waits = {}
        for d in deps:
            if d is None:
                continue
            dc, dv = d
            if dv <= 0 or (fifo and dc == eng):
                continue
            waits[dc] = max(waits.get(dc, 0), dv)
        wl = [(self.ctrs[dc][0], dv) for dc, dv in waits.items()]
        self.ctrs[cname][1] = cnt + n
        self.ops[eng].append((wl, build, (h, n)))
        return (cname, cnt + n)

    def realize(self):
        finals = [(self.ctrs[n][0], self.ctrs[n][1]) for n in self.ctrs
                  if n not in self.ENGS and self.ctrs[n][1] > 0]
        blk = self.ctx.enter_context(self.nc.Block())
        for eng in self.ENGS:
            ops = self.ops[eng]

            def body(e, ops=ops, eng=eng):
                for waits, build, inc in ops:
                    for (h, v) in waits:
                        e.wait_ge(h, v)
                    r = build(e)
                    if r is not None:
                        r.then_inc(*inc)
                if eng == "sync":
                    for h, v in finals:
                        e.wait_ge(h, v)
            getattr(blk, eng)(body)
        return self.nc


def joined(*tok_lists):
    best = {}
    for tl in tok_lists:
        for t in tl:
            if t is None:
                continue
            c, v = t
            best[c] = max(best.get(c, 0), v)
    return [(c, v) for c, v in best.items()]


class PsumRot:
    def __init__(self, views):
        self.views = views
        self.free = [None] * len(views)
        self.i = 0

    def next(self):
        i = self.i
        self.i = (self.i + 1) % len(self.views)
        return i, self.views[i], ([self.free[i]] if self.free[i] else [])

    def done(self, i, tok):
        self.free[i] = tok



def host_consts():
    cs = {}
    cs["c_idn"] = np.eye(128, dtype=BF16NP)
    cs["c_ones2"] = np.ones((128, 128), dtype=BF16NP)
    cs["c_onesc"] = np.ones((128, 1), dtype=BF16NP)
    cs["c_onesf"] = np.ones((1, 128), dtype=np.float32)
    cs["c_onescf"] = np.ones((128, 1), dtype=np.float32)
    kp = np.arange(128)[:, None]
    q = np.arange(512)[None, :]
    m = np.stack([np.where(q >= 128 * r + kp, 0.0, -1e30) for r in range(4)], 1)
    cs["c_cmskt"] = np.ascontiguousarray(m.reshape(128, 2048)).astype(BF16NP)
    es = np.zeros((16, 16, 128), dtype=np.float32)
    for e in range(16):
        es[e, e, :] = 1.0
    cs["c_esel"] = np.ascontiguousarray(es.transpose(0, 1, 2).reshape(16, 2048)
                                        ).astype(BF16NP)
    R = np.eye(C, dtype=np.float64)
    inv = 10000.0 ** (-np.arange(0, 16, 2) / 64.0)
    for h in range(NH):
        for j in range(8):
            ct, st = np.cos(h * inv[j]), np.sin(h * inv[j])
            a, b = 64 * h + 2 * j, 64 * h + 2 * j + 1
            R[a, a] = ct
            R[a, b] = -st
            R[b, a] = st
            R[b, b] = ct
    cs["c_rt"] = np.ascontiguousarray(R.T).astype(np.float32)
    return cs


def featvec(v):
    return np.ascontiguousarray(np.asarray(v, np.float32).reshape(NT, 128).T)



def make_bufs(P):
    b = {}
    b["X"] = P.sbuf("X", [128, NT * S], F32)
    b["A1"] = P.sbuf("A1", [128, A1COLS], BF16)
    b["A2"] = P.sbuf("A2", [128, NT * S], BF16)
    b["A3"] = P.sbuf("A3", [128, NT * S], BF16)
    b["A4"] = P.sbuf("A4", [128, NT * S], BF16)
    b["F32A"] = P.sbuf("F32A", [128, 1024], F32)
    b["ROWS"] = P.sbuf("ROWS", [1, 1536], F32)
    b["C8B"] = P.sbuf("C8B", [128, 2048], BF16)
    b["COL"] = P.sbuf("COL", [128, 40], F32)
    b["CBC"] = P.sbuf("CBC", [128, 16], F32)
    b["IDN"] = P.sbuf("IDN", [128, 128], BF16)
    b["ONESC"] = P.sbuf("ONESC", [128, 1], BF16)
    b["ONESF"] = P.sbuf("ONESF", [1, 128], F32)
    b["ONESCF"] = P.sbuf("ONESCF", [128, 1], F32)
    b["NORMW"] = P.sbuf("NORMW", [128, 24], F32)
    b["EPSC"] = P.sbuf("EPSC", [1, 2], F32)
    b["PSA"] = P.psum("PSA", [128, 2048], F32)
    b["PSB"] = P.psum("PSB", [128, 512], F32)
    b["PSC"] = P.psum("PSC", [128, 512], F32)
    b["PSD"] = P.psum("PSD", [128, 512], F32)
    b["PTP"] = P.psum("PTP", [128, 512], BF16)
    return b


def xt(b, j):
    return b["X"][:, j * S:(j + 1) * S]


def emit_const_loads(P, b, d, extra=()):
    def g(e):
        sem = P.ctrs["sdma"][0]
        e.dma_start(b["IDN"][:, :], d["c_idn"]).then_inc(sem, 16)
        e.dma_start(b["ONESC"][:, :], d["c_onesc"]).then_inc(sem, 16)
        e.dma_start(b["ONESF"][:, :], d["c_onesf"]).then_inc(sem, 16)
        e.dma_start(b["ONESCF"][:, :], d["c_onescf"]).then_inc(sem, 16)
        for dst, src in extra:
            e.dma_start(dst, src).then_inc(sem, 16)
        return None
    t1 = P.add("sync", g, ctr="sdma", n=16 * (4 + len(extra)))

    def ge(e):
        e.memset(b["EPSC"][0:1, 0:1], 1e-5)
        return e.memset(b["EPSC"][0:1, 1:2], 1e-6)
    t2 = P.add("vector", ge)
    return [t1, t2]


def emit_load_x(P, b, x_ap, bi, deps=()):
    src = x_ap[bi].rearrange("(j p) t -> p j t", p=128)
    dst = b["X"][:, :].rearrange("p (j t) -> p j t", t=S)
    return [P.add("sync", lambda e: e.dma_start(dst, src), deps=deps,
                  ctr="xdma", n=16)]


def emit_store_x(P, b, xo_ap, bi, deps=()):
    dst = xo_ap[bi].rearrange("(j p) t -> p j t", p=128)
    src = b["X"][:, :].rearrange("p (j t) -> p j t", t=S)
    return [P.add("sync", lambda e: e.dma_start(dst, src), deps=deps,
                  ctr="odma", n=16)]



class WRing:

    def __init__(self, P, arena, base_col, nslots, cols, tag="w"):
        self.P = P
        self.tag = tag
        self.slots = [arena[:, base_col + i * cols: base_col + (i + 1) * cols]
                      for i in range(nslots)]
        self.n = nslots
        self.i = 0

    def load(self, dram_slice, ktiles, deps=()):
        P = self.P
        i = self.i
        self.i = (self.i + 1) % self.n
        sl = self.slots[i]
        cast = dram_slice.tensor.dtype != BF16
        cname = P.ctr(f"{self.tag}{'g' if cast else 's'}{i}")
        key = (self.tag, i)
        d = list(deps) + [P.ring_busy.get(key)]
        mcols = dram_slice.shape[-1]
        src = dram_slice.rearrange("(kt p) m -> p kt m", p=128)
        dst = sl[:, 0:ktiles * mcols].rearrange("p (kt m) -> p kt m", kt=ktiles)
        eng = "gpsimd" if cast else "sync"
        tok = P.add(eng, (lambda e: e.dma_start(dst, src)), deps=d,
                    ctr=cname, n=16)
        return sl, key, tok

    def used(self, key, tok):
        self.P.ring_busy[key] = tok



def emit_proj(P, b, wring, wt_dram, src_buf, kdim, mdim, prot, evac,
              deps_src=(), src_stride=S, src_col0=0, chunks=None, wdeps=()):
    ktiles = kdim // 128
    mtiles = mdim // 128
    chunks = list(range(NCH)) if chunks is None else chunks
    etoks = []
    mmtok = None
    for mi in range(mtiles):
        wsl, wi, wtok = wring.load(wt_dram[:, mi * 128:(mi + 1) * 128], ktiles,
                                   deps=wdeps)
        for ci in chunks:
            si, pv, ftoks = prot.next()

            def g(e, wsl=wsl, pv=pv, ci=ci, ktiles=ktiles):
                last = None
                for kt in range(ktiles):
                    c0 = src_col0 + kt * src_stride + ci * CH
                    last = e.matmul(pv, wsl[:, kt * 128:(kt + 1) * 128],
                                    src_buf[:, c0:c0 + CH],
                                    start=(kt == 0), stop=(kt == ktiles - 1))
                return last
            mmtok = P.add("tensor", g, deps=[wtok] + list(deps_src) + ftoks,
                          fifo=True)
            et = evac(mi, ci, pv, [mmtok])
            prot.done(si, et)
            etoks.append(et)
        wring.used(wi, mmtok)
    return etoks


def mk_evac_to(P, dst_buf, eng="scalar", dst_stride=S, dst_col0=0):
    def evac(mi, ci, pv, deps):
        c0 = dst_col0 + mi * dst_stride + ci * CH
        dst = dst_buf[:, c0:c0 + CH]
        if eng == "scalar":
            return P.add("scalar", lambda e: e.activation(dst, pv, AF.Copy),
                         deps=deps)
        return P.add("vector", lambda e: e.tensor_copy(dst, pv), deps=deps)
    return evac


def mk_evac_resid(P, b, scale=None, extra=()):
    def evac(mi, ci, pv, deps):
        dst = b["X"][:, mi * S + ci * CH: mi * S + ci * CH + CH]
        if scale is None:
            return P.add("vector",
                         lambda e: e.tensor_tensor(dst, dst, pv, ALU.add),
                         deps=list(deps) + list(extra))
        return P.add("vector",
                     lambda e: e.scalar_tensor_tensor(dst, pv, float(scale),
                                                      dst, ALU.mult, ALU.add),
                     deps=list(deps) + list(extra))
    return evac



def emit_layernorm(P, b, dst_buf, arena, deps, eps=1e-5):
    xq = [arena[:, 1024:1536], arena[:, 1536:2048]]
    negmu = b["ROWS"][0:1, 0:512]
    invr = b["ROWS"][0:1, 512:1024]
    msh = b["ROWS"][0:1, 1024:1536]
    out = []
    row_free = []
    xq_free = {0: None, 1: None}
    for ci in range(NCH):
        sumx = b["PSB"][0:1, 0:512]
        sumq = b["PSC"][0:1, 0:512]
        mmx = mmq = None
        for j in range(NT):
            sl = j % 2
            mmx = P.add("tensor",
                        (lambda e, j=j, ci=ci:
                         e.matmul(sumx, b["ONESCF"][:, 0:1],
                                  xt(b, j)[:, ci * CH:ci * CH + CH],
                                  start=(j == 0), stop=(j == NT - 1))),
                        deps=list(deps) + row_free, fifo=True)
            sq = P.add("scalar",
                       (lambda e, sl=sl, j=j, ci=ci:
                        e.activation(xq[sl],
                                     xt(b, j)[:, ci * CH:ci * CH + CH],
                                     AF.Square)),
                       deps=list(deps) + [xq_free[sl]] + row_free)
            mmq = P.add("tensor",
                        (lambda e, sl=sl, j=j:
                         e.matmul(sumq, b["ONESC"][:, 0:1], xq[sl],
                                  start=(j == 0), stop=(j == NT - 1))),
                        deps=[sq] + row_free, fifo=True)
            xq_free[sl] = mmq
        t1 = P.add("vector",
                   lambda e: e.tensor_scalar(negmu, sumx, -1.0 / C, None,
                                             ALU.mult),
                   deps=[mmx])
        t2 = P.add("vector",
                   lambda e: e.tensor_tensor(msh, negmu, negmu, ALU.mult),
                   deps=[t1])
        t3 = P.add("vector",
                   lambda e: e.scalar_tensor_tensor(invr, sumq, 1.0 / C, msh,
                                                    ALU.mult, ALU.subtract),
                   deps=[t2, mmq])
        t4 = P.add("scalar",
                   lambda e: e.activation(msh, invr, AF.Ln,
                                          bias=b["EPSC"][0:1, 0:1]),
                   deps=[t3])
        t5 = P.add("scalar",
                   lambda e: e.activation(invr, msh, AF.Exp, scale=-0.5),
                   deps=[t4])
        t6 = P.add("vector",
                   lambda e: e.tensor_tensor(msh, negmu, invr, ALU.mult),
                   deps=[t5])
        bcA = P.add("tensor",
                    lambda e: e.matmul(b["PSB"][:, 0:512], b["ONESF"][0:1, :],
                                       invr, start=True, stop=True),
                    deps=[t5, t1])
        bcB = P.add("tensor",
                    lambda e: e.matmul(b["PSC"][:, 0:512], b["ONESF"][0:1, :],
                                       msh, start=True, stop=True),
                    deps=[t6])
        rf = []
        for j in range(NT):
            tt1 = P.add("vector",
                        (lambda e, j=j, ci=ci:
                         e.tensor_tensor(b["F32A"][:, 512:1024],
                                         xt(b, j)[:, ci * CH:ci * CH + CH],
                                         b["PSB"][:, 0:512], ALU.mult)),
                        deps=[bcA] + ([rf[-1]] if rf else []))
            tt2 = P.add("vector",
                        (lambda e, j=j, ci=ci:
                         e.tensor_tensor(
                             dst_buf[:, j * S + ci * CH:j * S + ci * CH + CH],
                             b["F32A"][:, 512:1024], b["PSC"][:, 0:512],
                             ALU.add)),
                        deps=[bcB, tt1])
            rf.append(tt2)
            out.append(tt2)
        row_free = [rf[-1], bcA, bcB]
    return out


def emit_rmsnorm(P, b, src_buf, dst_buf, nw_col, arena, deps, eps=1e-6):
    xq = [arena[:, 0:512], arena[:, 512:1024]]
    invr = b["ROWS"][0:1, 512:1024]
    msh = b["ROWS"][0:1, 1024:1536]
    out = []
    row_free = []
    xq_free = {0: None, 1: None}
    for ci in range(NCH):
        sumq = b["PSC"][0:1, 0:512]
        mmq = None
        for j in range(NT):
            sl = j % 2
            sq = P.add("scalar",
                       (lambda e, j=j, ci=ci, sl=sl:
                        e.activation(
                            xq[sl],
                            src_buf[:, j * S + ci * CH:j * S + ci * CH + CH],
                            AF.Square)),
                       deps=list(deps) + [xq_free[sl]] + row_free)
            mmq = P.add("tensor",
                        (lambda e, sl=sl, j=j:
                         e.matmul(sumq, b["ONESC"][:, 0:1], xq[sl],
                                  start=(j == 0), stop=(j == NT - 1))),
                        deps=[sq] + row_free, fifo=True)
            xq_free[sl] = mmq
        t1 = P.add("vector",
                   lambda e: e.tensor_scalar(invr, sumq, 1.0 / C, None, ALU.mult),
                   deps=[mmq])
        t2 = P.add("scalar",
                   lambda e: e.activation(msh, invr, AF.Ln,
                                          bias=b["EPSC"][0:1, 1:2]),
                   deps=[t1])
        t3 = P.add("scalar",
                   lambda e: e.activation(invr, msh, AF.Exp, scale=-0.5),
                   deps=[t2])
        bcA = P.add("tensor",
                    lambda e: e.matmul(b["PSC"][:, 0:512], b["ONESF"][0:1, :],
                                       invr, start=True, stop=True),
                    deps=[t3, t1])
        rf = []
        for j in range(NT):
            tt = P.add("vector",
                       (lambda e, j=j, ci=ci:
                        e.scalar_tensor_tensor(
                            dst_buf[:, j * S + ci * CH:j * S + ci * CH + CH],
                            src_buf[:, j * S + ci * CH:j * S + ci * CH + CH],
                            nw_col(j), b["PSC"][:, 0:512],
                            ALU.mult, ALU.mult)),
                       deps=[bcA] + ([rf[-1]] if rf else []))
            rf.append(tt)
            out.append(tt)
        row_free = [rf[-1], bcA]
    return out



def mk_evac_vtok(P, b, vt_buf, pair_heads, vs_view):
    state = {"ptp_free": None, "vs_free": None}

    def evac(mi, ci, pv, deps):
        hA, hB = pair_heads(mi)
        cp = P.add("scalar", lambda e: e.activation(vs_view, pv, AF.Copy),
                   deps=list(deps) + [state["vs_free"]])
        trs = []
        for blk in range(4):
            tr = P.add("tensor",
                       (lambda e, blk=blk:
                        e.transpose(b["PTP"][:, blk * 128:blk * 128 + 128],
                                    vs_view[:, blk * 128:blk * 128 + 128],
                                    b["IDN"][:, :])),
                       deps=[cp, state["ptp_free"]], fifo=True)
            trs.append(tr)
        state["vs_free"] = trs[-1]
        kt0 = ci * 4
        last = None
        for blk in range(4):
            kt = kt0 + blk
            cA = 65 * (hA * 16 + kt)
            cB = 65 * (hB * 16 + kt)

            def gv(e, cA=cA, cB=cB, blk=blk):
                e.tensor_copy(vt_buf[:, cA:cA + 64],
                              b["PTP"][:, blk * 128:blk * 128 + 64])
                return e.tensor_copy(vt_buf[:, cB:cB + 64],
                                     b["PTP"][:, blk * 128 + 64:blk * 128 + 128])
            last = P.add("vector", gv, deps=[trs[3]])
        state["ptp_free"] = last
        return last
    return evac


def emit_attn(P, b, q_buf, k_buf, vt_buf, o_buf, qslot, kslot, vblk, causal,
              deps, arena, cmskt=None):
    sqst = arena[:, 0:2048]
    ptst = arena[:, 2048:4096]
    rcrow = b["ROWS"][0:1, 512:1024]
    srot = PsumRot([b["PSA"][:, i * 512:(i + 1) * 512] for i in range(4)])
    orot = PsumRot([b["PSB"][:, 0:512], b["PSD"][:, 0:512]])
    out = []
    rows = [b["PSC"][0:1, 0:512], b["PSD"][0:1, 0:512]]
    rowtok = [[], []]
    sq_free = []
    kvs = {}
    for h in range(NH):
        kvs.setdefault(kslot(h), []).append(h)
    kcol = {}
    acc = []
    prev_kx = []
    for kv, heads in kvs.items():
        kt_, kp_ = kv
        ksl = k_buf[64 * kp_:64 * kp_ + 64, kt_ * S:(kt_ + 1) * S]
        sqk = sqst[64 * kp_:64 * kp_ + 64, 0:2048]
        sq1 = P.add("scalar",
                    (lambda e, sqk=sqk, ksl=ksl:
                     e.activation(sqk, ksl, AF.Square)),
                    deps=list(deps) + sq_free)
        kms = []
        cell = b["COL"][0:1, 1 + len(kcol):2 + len(kcol)]
        for ci in range(NCH):
            rv = rows[ci % 2]
            km = P.add("tensor",
                       (lambda e, ci=ci, sqk=sqk, kp_=kp_, rv=rv:
                        e.matmul(rv, b["ONESC"][64 * kp_:64 * kp_ + 64, 0:1],
                                 sqk[:, ci * 512:ci * 512 + 512],
                                 start=True, stop=True)),
                       deps=[sq1] + rowtok[ci % 2])
            kc = P.add("vector",
                       (lambda e, ci=ci, rv=rv:
                        e.tensor_reduce(b["COL"][0:1, 20 + ci:21 + ci], rv,
                                        AX.X, ALU.max)),
                       deps=[km] + prev_kx)
            rowtok[ci % 2] = [kc]
            kms.append(kc)
        kx = P.add("vector",
                   (lambda e, cell=cell:
                    e.tensor_reduce(cell, b["COL"][0:1, 20:24], AX.X, ALU.max)),
                   deps=kms)
        kcol[kv] = (cell, kx)
        sq_free = kms
        prev_kx = [kx]
        acc.append(kx)
    qcell = b["COL"][0:1, 0:1]
    cbc_toks = {}
    for h in range(NH):
        qt, qp = qslot(h)
        qsl = q_buf[64 * qp:64 * qp + 64, qt * S:(qt + 1) * S]
        sqq = sqst[64 * qp:64 * qp + 64, 0:2048]
        sq2 = P.add("scalar",
                    (lambda e, sqq=sqq, qsl=qsl:
                     e.activation(sqq, qsl, AF.Square)),
                    deps=list(deps) + sq_free)
        qms = []
        for ci in range(NCH):
            rv = rows[ci % 2]
            qm = P.add("tensor",
                       (lambda e, ci=ci, sqq=sqq, qp=qp, rv=rv:
                        e.matmul(rv, b["ONESC"][64 * qp:64 * qp + 64, 0:1],
                                 sqq[:, ci * 512:ci * 512 + 512],
                                 start=True, stop=True)),
                       deps=[sq2] + rowtok[ci % 2])
            qc = P.add("vector",
                       (lambda e, ci=ci, rv=rv:
                        e.tensor_reduce(b["COL"][0:1, 24 + ci:25 + ci], rv,
                                        AX.X, ALU.max)),
                       deps=[qm, cbc_toks.get(h - 1)])
            rowtok[ci % 2] = [qc]
            qms.append(qc)
        sq_free = qms
        kcell, kxt = kcol[kslot(h)]
        t1 = P.add("vector",
                   (lambda e: e.tensor_reduce(qcell, b["COL"][0:1, 24:28],
                                              AX.X, ALU.max)),
                   deps=qms + [cbc_toks.get(h - 1)])
        t2 = P.add("vector",
                   (lambda e, kcell=kcell:
                    e.tensor_tensor(qcell, qcell, kcell, ALU.mult)),
                   deps=[t1, kxt])
        t3 = P.add("scalar",
                   lambda e: e.activation(qcell, qcell, AF.Sqrt),
                   deps=[t2])
        t4 = P.add("vector",
                   (lambda e: e.tensor_scalar(qcell, qcell, -0.125, None,
                                              ALU.mult)),
                   deps=[t3])
        t5 = P.add("tensor",
                   (lambda e: e.matmul(b["PSC"][:, 0:1], b["ONESF"][0:1, :],
                                       qcell, start=True, stop=True)),
                   deps=[t4] + rowtok[0] + rowtok[1])
        cb = P.add("vector",
                   (lambda e, h=h: e.tensor_copy(b["CBC"][:, h:h + 1],
                                                 b["PSC"][:, 0:1])),
                   deps=[t5])
        rowtok[0] = rowtok[1] = [cb]
        cbc_toks[h] = cb
        acc.append(cb)
    prep = joined(acc, [cbc_toks[NH - 1]])
    pend = None
    pt_free = {0: [], 1: []}

    def emit_pv(prevp):
        ex, kts, pp, h, ov, nkt, ofree = prevp
        last = None
        for i, kt in enumerate(kts):
            vcol = 65 * (vblk(h) * 16 + kt)
            pt = ptst[:, pp * 1024 + i * 512: pp * 1024 + i * 512 + 512]
            last = P.add("tensor",
                         (lambda e, kt=kt, nkt=nkt, vcol=vcol, ov=ov, pt=pt:
                          e.matmul(ov[0:65, :], vt_buf[:, vcol:vcol + 65],
                                   pt, start=(kt == 0),
                                   stop=(kt == nkt - 1))),
                         deps=[ex] + (ofree if kt == 0 else []), fifo=True)
        pt_free[pp] = [last]
        return last

    norm_state = {"prev": []}

    def emit_norm(pend):
        ov, oi, qt, qp, ci, _pv, _nf = pend
        rc = P.add("vector",
                   (lambda e, ov=ov: e.reciprocal(rcrow, ov[64:65, :])),
                   deps=[pend[5]] + pend[6] + norm_state["prev"])
        bc = P.add("tensor",
                   lambda e: e.matmul(b["PSC"][0:64, 0:512],
                                      b["ONESF"][0:1, 0:64], rcrow,
                                      start=True, stop=True),
                   deps=[rc])
        cbe = P.add("vector",
                    lambda e: e.tensor_copy(b["F32A"][0:64, 0:512],
                                            b["PSC"][0:64, 0:512]),
                    deps=[bc] + norm_state["prev"])
        oc0 = qt * S + ci * CH
        fin = P.add("vector",
                    (lambda e, ov=ov, oc0=oc0, qp=qp:
                     e.tensor_tensor(
                         o_buf[64 * qp:64 * qp + 64, oc0:oc0 + CH],
                         ov[0:64, :], b["F32A"][0:64, 0:512], ALU.mult)),
                    deps=[cbe])
        orot.done(oi, fin)
        out.append(fin)
        norm_state["prev"] = [rc, fin]
        return [cbe, fin]

    nfree = []
    for h in range(NH):
        qt, qp = qslot(h)
        kt_, kp_ = kslot(h)
        qsl = q_buf[64 * qp:64 * qp + 64, qt * S:(qt + 1) * S]
        ksl = k_buf[64 * kp_:64 * kp_ + 64, kt_ * S:(kt_ + 1) * S]
        for ci in range(NCH):
            nkt = (4 * ci + 4) if causal else 16
            oi, ov, ofree = orot.next()
            pv_last = None
            prevp = None
            for p in range(nkt // 2):
                kts = (2 * p, 2 * p + 1)
                mdep = []
                svs = []
                s0 = None
                for kt in kts:
                    si, sv, sfree = srot.next()
                    if s0 is None:
                        s0 = (si, sv)
                    svs.append(si)

                    def gmm(e, sv=sv, kt=kt, ci=ci, qsl=qsl, ksl=ksl):
                        return e.matmul(sv, ksl[:, kt * 128:kt * 128 + 128],
                                        qsl[:, ci * CH:ci * CH + CH],
                                        start=True, stop=True)
                    mm = P.add("tensor", gmm, deps=sfree + list(deps) + prep,
                               fifo=True)
                    if causal and kt >= 4 * ci:
                        r = kt - 4 * ci
                        mk = P.add("vector",
                                   (lambda e, sv=sv, r=r:
                                    e.tensor_tensor(
                                        sv, sv,
                                        cmskt[:, r * 512:r * 512 + 512],
                                        ALU.add)),
                                   deps=[mm])
                        mdep.append(mk)
                    else:
                        mdep.append(mm)
                pp = p % 2
                pt2 = ptst[:, pp * 1024:pp * 1024 + 1024]
                sv2 = b["PSA"][:, s0[0] * 512:s0[0] * 512 + 1024]
                ex = P.add("scalar",
                           (lambda e, sv2=sv2, pt2=pt2, h=h:
                            e.activation(pt2, sv2, AF.Exp, scale=0.125,
                                         bias=b["CBC"][:, h:h + 1])),
                           deps=mdep + pt_free[pp] + prep)
                for si in svs:
                    srot.done(si, ex)
                if prevp is not None:
                    pv_last = emit_pv(prevp)
                prevp = (ex, kts, pp, h, ov, nkt, ofree)
            pv_last = emit_pv(prevp)
            this = (ov, oi, qt, qp, ci, pv_last, nfree)
            if pend is not None:
                nfree = emit_norm(pend)
            pend = this
    if pend is not None:
        emit_norm(pend)
    return out



def emit_gate8(P, b, f_buf, gw_view, gsc, cmb_view, combt_view, deps,
               scale=0.5):
    out = []
    L = gsc[:, 0:8]
    T = gsc[:, 8:16]
    E = gsc[:, 16:24]
    m1 = b["COL"][:, 1:2]
    nm1 = b["COL"][:, 2:3]
    m2 = b["COL"][:, 3:4]
    den = b["COL"][:, 4:5]
    rc = b["COL"][:, 5:6]
    prev = []
    for tc in range(16):
        def gmm(e, tc=tc):
            last = None
            for j in range(NT):
                last = e.matmul(b["PSA"][:, 1024:1040][:, 0:8],
                                f_buf[:, j * S + tc * 128:j * S + tc * 128 + 128],
                                gw_view[:, j * 8:(j + 1) * 8],
                                start=(j == 0), stop=(j == NT - 1))
            return last
        mm = P.add("tensor", gmm, deps=list(deps) + prev, fifo=True)
        t0 = P.add("vector",
                   lambda e: e.tensor_copy(L, b["PSA"][:, 1024:1032]),
                   deps=[mm])
        t1 = P.add("vector", lambda e: e.tensor_reduce(m1, L, AX.X, ALU.max),
                   deps=[t0])
        t1b = P.add("vector",
                    lambda e: e.tensor_scalar(nm1, m1, -1.0, None, ALU.mult),
                    deps=[t1])
        t2 = P.add("vector",
                   lambda e: e.tensor_scalar(T, L, m1[0:128, 0:1], None,
                                             ALU.is_equal),
                   deps=[t1])
        t3 = P.add("vector",
                   lambda e: e.scalar_tensor_tensor(T, T, -1e30, L, ALU.mult,
                                                    ALU.add),
                   deps=[t2])
        t4 = P.add("vector", lambda e: e.tensor_reduce(m2, T, AX.X, ALU.max),
                   deps=[t3])
        t5 = P.add("vector",
                   lambda e: e.tensor_scalar(T, L, m2[0:128, 0:1], None,
                                             ALU.is_ge),
                   deps=[t4])
        t6 = P.add("scalar",
                   lambda e: e.activation(E, L, AF.Exp, bias=nm1[0:128, 0:1]),
                   deps=[t1b, t0])
        t7 = P.add("vector", lambda e: e.tensor_tensor(E, E, T, ALU.mult),
                   deps=[t5, t6])
        t8 = P.add("vector", lambda e: e.tensor_reduce(den, E, AX.X, ALU.add),
                   deps=[t7])
        t9 = P.add("vector", lambda e: e.reciprocal(rc, den), deps=[t8])
        t10 = P.add("vector",
                    (lambda e: e.tensor_scalar(cmb_view[:, 0:8], E,
                                               rc[0:128, 0:1], float(scale),
                                               ALU.mult, ALU.mult)),
                    deps=[t9])
        tr = P.add("tensor",
                   (lambda e: e.transpose(b["PTP"][0:8, 0:128],
                                          cmb_view[:, 0:8], b["IDN"][:, :])),
                   deps=[t10] + prev)
        ev = P.add("vector",
                   (lambda e, tc=tc:
                    e.tensor_copy(combt_view[0:8, tc * 128:tc * 128 + 128],
                                  b["PTP"][0:8, 0:128])),
                   deps=[tr])
        prev = [ev]
        out.append(ev)
    return out


def emit_gate16(P, b, f_buf, gw_view, gbc_view, gsc, cmb_view, combt_view,
                deps, route_scale=2.5, sub_scale=0.5):
    out = []
    L = gsc[:, 0:16]
    Bv = gsc[:, 16:32]
    T = gsc[:, 32:48]
    E = gsc[:, 48:64]
    mx = b["COL"][:, 1:2]
    den = b["COL"][:, 4:5]
    rc = b["COL"][:, 5:6]
    prev = []
    for tc in range(16):
        def gmm(e, tc=tc):
            last = None
            for j in range(NT):
                last = e.matmul(b["PSA"][:, 1024:1040],
                                f_buf[:, j * S + tc * 128:j * S + tc * 128 + 128],
                                gw_view[:, j * 16:(j + 1) * 16],
                                start=(j == 0), stop=(j == NT - 1))
            return last
        mm = P.add("tensor", gmm, deps=list(deps) + prev, fifo=True)
        t0 = P.add("scalar",
                   lambda e: e.activation(L, b["PSA"][:, 1024:1040], AF.Sigmoid),
                   deps=[mm])
        t1 = P.add("vector", lambda e: e.tensor_tensor(Bv, L, gbc_view, ALU.add),
                   deps=[t0])
        tsel = t1
        for r in range(4):
            ta = P.add("vector",
                       lambda e: e.tensor_reduce(mx, Bv, AX.X, ALU.max),
                       deps=[tsel])
            if r < 3:
                tb = P.add("vector",
                           (lambda e:
                            e.tensor_scalar(T, Bv, mx[0:128, 0:1], None,
                                            ALU.is_equal)),
                           deps=[ta])
                tsel = P.add("vector",
                             (lambda e:
                              e.scalar_tensor_tensor(Bv, T, -1e30, Bv,
                                                     ALU.mult, ALU.add)),
                             deps=[tb])
            else:
                tsel = ta
        t2 = P.add("vector", lambda e: e.tensor_tensor(Bv, L, gbc_view, ALU.add),
                   deps=[tsel])
        t3 = P.add("vector",
                   lambda e: e.tensor_scalar(T, Bv, mx[0:128, 0:1], None,
                                             ALU.is_ge),
                   deps=[t2])
        t4 = P.add("vector", lambda e: e.tensor_tensor(E, L, T, ALU.mult),
                   deps=[t3])
        t5 = P.add("vector", lambda e: e.tensor_reduce(den, E, AX.X, ALU.add),
                   deps=[t4])
        t6 = P.add("vector", lambda e: e.reciprocal(rc, den), deps=[t5])
        t7 = P.add("vector",
                   (lambda e:
                    e.tensor_scalar(cmb_view[:, 0:16], E, rc[0:128, 0:1],
                                    float(route_scale * sub_scale),
                                    ALU.mult, ALU.mult)),
                   deps=[t6])
        tr = P.add("tensor",
                   (lambda e: e.transpose(b["PTP"][0:16, 0:128],
                                          cmb_view[:, 0:16], b["IDN"][:, :])),
                   deps=[t7] + prev)
        ev = P.add("vector",
                   (lambda e, tc=tc:
                    e.tensor_copy(combt_view[0:16, tc * 128:tc * 128 + 128],
                                  b["PTP"][0:16, 0:128])),
                   deps=[tr])
        prev = [ev]
        out.append(ev)
    return out


def emit_comb_bc(P, b, esel_view, combt_view, nexp, e_idx, cbe4_view, deps,
                 ps_view=None):
    pv = b["PSD"][:, 0:512] if ps_view is None else ps_view
    toks = []
    for ci in range(NCH):
        mm = P.add("tensor",
                   (lambda e, ci=ci:
                    e.matmul(pv,
                             esel_view[0:nexp, e_idx * 128:e_idx * 128 + 128],
                             combt_view[0:nexp, ci * CH:ci * CH + CH],
                             start=True, stop=True)),
                   deps=list(deps) + toks[-1:])
        ev = P.add("scalar",
                   (lambda e, ci=ci:
                    e.activation(cbe4_view[:, ci * CH:ci * CH + CH],
                                 pv, AF.Copy)),
                   deps=[mm])
        toks.append(ev)
    return toks


def emit_moe8(P, b, f_buf, expw_ap, esel_view, combt_view, cbe4_view, wring,
              deps, gate_deps, fe_buf=None, wdeps=()):
    out = []
    orot = PsumRot([b["PSB"][:, 0:512], b["PSC"][:, 0:512]])
    t1 = b["F32A"][:, 512:1024]
    cb_tok = []
    prev_t1 = []
    cbe4_readers = []
    for e_ in range(8):
        cb_tok = emit_comb_bc(P, b, esel_view, combt_view, 8, e_, cbe4_view,
                              joined(gate_deps, cb_tok, cbe4_readers))
        cbe4_readers = []
        for mo in range(NT):
            wsl, wi, wtok = wring.load(expw_ap[e_][:, mo * 128:(mo + 1) * 128],
                                       NT, deps=wdeps)
            last = None
            for ci in range(NCH):
                si, pv, sfree = orot.next()

                def gmm(e, wsl=wsl, pv=pv, ci=ci):
                    r = None
                    for kt in range(NT):
                        r = e.matmul(pv, wsl[:, kt * 128:(kt + 1) * 128],
                                     f_buf[:, kt * S + ci * CH:
                                           kt * S + ci * CH + CH],
                                     start=(kt == 0), stop=(kt == NT - 1))
                    return r
                mm = P.add("tensor", gmm,
                           deps=[wtok] + list(deps) + sfree, fifo=True)
                tta = P.add("vector",
                            (lambda e, pv=pv, ci=ci:
                             e.tensor_tensor(t1, pv,
                                             cbe4_view[:, ci * CH:ci * CH + CH],
                                             ALU.mult)),
                            deps=[mm] + prev_t1 + cb_tok)
                ttb = P.add("vector",
                            (lambda e, mo=mo, ci=ci:
                             e.tensor_tensor(
                                 b["X"][:, mo * S + ci * CH:mo * S + ci * CH + CH],
                                 b["X"][:, mo * S + ci * CH:mo * S + ci * CH + CH],
                                 t1, ALU.add)),
                            deps=[tta])
                prev_t1 = [ttb]
                cbe4_readers = [tta]
                orot.done(si, tta)
                last = ttb
                out.append(ttb)
            wring.used(wi, last)
    return out


def emit_moe16(P, b, y_buf, w13_ap, w2_ap, w13s_ap, w2s_ap, esel_view,
               combt_view, cbe4_view, sg_view, h_buf, wring, w2ring, deps,
               gate_deps):
    out = []
    t1 = b["F32A"][:, 512:1024]
    grot = PsumRot([b["PSB"][:, 0:512], b["PSC"][:, 0:512],
                    b["PSD"][:, 0:512], b["PSA"][:, 1536:2048]])
    orot = PsumRot([b["PSA"][:, 0:512], b["PSA"][:, 512:1024]])
    prev_t1 = []
    prev_sg = {0: [], 1: []}
    cbe4_readers = []
    groups = [(2 * g, 2 * g + 1) for g in range(8)] + [(16,)]
    for grp in groups:
        for slot, e_ in enumerate(grp):
            if e_ < 16:
                cb = emit_comb_bc(P, b, esel_view, combt_view, 16, e_,
                                  cbe4_view,
                                  joined(gate_deps, cbe4_readers),
                                  ps_view=b["PSA"][:, 1024:1536])
            else:
                cb = [P.add("vector",
                            lambda e: e.memset(cbe4_view[:, 0:2048], 0.5),
                            deps=joined(gate_deps, cbe4_readers))]
            cbe4_readers = []
            w13 = w13_ap[e_] if e_ < 16 else w13s_ap
            for mg in range(4):
                wg, wgi, wgt = wring.load(w13[:, mg * 128:(mg + 1) * 128], NT)
                wu, wui, wut = wring.load(
                    w13[:, (mg + 4) * 128:(mg + 5) * 128], NT)
                lastg = lastu = None
                for ci in range(NCH):
                    gi, gv, gfree = grot.next()
                    ui, uv, ufree = grot.next()

                    def gmm(e, wsl=wg, pv=gv, ci=ci):
                        r = None
                        for kt in range(NT):
                            r = e.matmul(pv, wsl[:, kt * 128:(kt + 1) * 128],
                                         y_buf[:, kt * S + ci * CH:
                                               kt * S + ci * CH + CH],
                                         start=(kt == 0), stop=(kt == NT - 1))
                        return r
                    mmg = P.add("tensor", gmm, deps=[wgt] + list(deps) + gfree,
                                fifo=True)

                    def umm(e, wsl=wu, pv=uv, ci=ci):
                        r = None
                        for kt in range(NT):
                            r = e.matmul(pv, wsl[:, kt * 128:(kt + 1) * 128],
                                         y_buf[:, kt * S + ci * CH:
                                               kt * S + ci * CH + CH],
                                         start=(kt == 0), stop=(kt == NT - 1))
                        return r
                    mmu = P.add("tensor", umm, deps=[wut] + list(deps) + ufree,
                                fifo=True)
                    sgv = sg_view[:, (ci % 2) * 512:(ci % 2) * 512 + 512]
                    sg = P.add("scalar",
                               (lambda e, gv=gv, sgv=sgv:
                                e.activation(sgv, gv, AF.Silu)),
                               deps=[mmg] + prev_sg[ci % 2])
                    up = P.add("vector",
                               (lambda e, uv=uv, ci=ci:
                                e.tensor_tensor(
                                    t1, uv,
                                    cbe4_view[:, ci * CH:ci * CH + CH],
                                    ALU.mult)),
                               deps=[mmu] + prev_t1 + cb)
                    hc0 = slot * (4 * S) + mg * S + ci * CH
                    hh = P.add("vector",
                               (lambda e, hc0=hc0, sgv=sgv:
                                e.tensor_tensor(h_buf[:, hc0:hc0 + CH], t1,
                                                sgv, ALU.mult)),
                               deps=[up, sg])
                    prev_t1 = [hh]
                    prev_sg[ci % 2] = [hh]
                    cbe4_readers.append(up)
                    grot.done(gi, sg)
                    grot.done(ui, up)
                    lastg, lastu = hh, hh
                wring.used(wgi, lastg)
                wring.used(wui, lastu)
        w2s = []
        for slot, e_ in enumerate(grp):
            w2s.append(w2_ap[e_] if e_ < 16 else w2s_ap)
        for mo in range(NT):
            wts = []
            for slot, e_ in enumerate(grp):
                wsl, wi, wtok = w2ring.load(w2s[slot][:, mo * 128:(mo + 1) * 128],
                                            4)
                wts.append((wsl, wi, wtok))
            last = None
            for ci in range(NCH):
                oi, pv, ofree = orot.next()

                def gmm(e, pv=pv, ci=ci, wts=wts, grp=grp):
                    r = None
                    n = len(grp) * 4
                    i = 0
                    for slot in range(len(grp)):
                        wsl = wts[slot][0]
                        for kt in range(4):
                            hc0 = slot * (4 * S) + kt * S + ci * CH
                            r = e.matmul(pv, wsl[:, kt * 128:(kt + 1) * 128],
                                         h_buf[:, hc0:hc0 + CH],
                                         start=(i == 0), stop=(i == n - 1))
                            i += 1
                    return r
                hdeps = [prev_t1[0]] if prev_t1 else []
                mm = P.add("tensor", gmm,
                           deps=[w[2] for w in wts] + hdeps + ofree, fifo=True)
                ev = P.add("vector",
                           (lambda e, mo=mo, ci=ci, pv=pv:
                            e.tensor_tensor(
                                b["X"][:, mo * S + ci * CH:mo * S + ci * CH + CH],
                                b["X"][:, mo * S + ci * CH:mo * S + ci * CH + CH],
                                pv, ALU.add)),
                           deps=[mm])
                orot.done(oi, ev)
                last = ev
                out.append(ev)
            for wsl, wi, wtok in wts:
                w2ring.used(wi, last)
        prev_t1 = [last]
        prev_sg = {0: [last], 1: [last]}
    return out



def emit_rwkv(P, b, y_buf, wk_ap, wv_ap, wr_ap, td_view, r_buf, rs_buf,
              wring, wbc_view, deps):
    out = []
    wcol = b["COL"][:, 8:16]
    t0 = P.add("scalar", lambda e: e.activation(wcol, td_view, AF.Exp),
               deps=list(deps))
    t1 = P.add("vector",
               lambda e: e.tensor_scalar(wcol, wcol, -1.0, None, ALU.mult),
               deps=[t0])
    lastc = b["COL"][:, 6:7]
    rs_toks = []
    for j in range(NT):
        wsl, wi, wtok = wring.load(wk_ap[:, j * 128:(j + 1) * 128], NT)

        def gk(e, wsl=wsl):
            r = None
            for ci in range(NCH):
                for kt in range(NT):
                    r = e.matmul(b["PSA"][:, ci * CH:ci * CH + CH],
                                 wsl[:, kt * 128:(kt + 1) * 128],
                                 y_buf[:, kt * S + ci * CH:kt * S + ci * CH + CH],
                                 start=(kt == 0), stop=(kt == NT - 1))
            return r
        kdep = [rs_toks[-1]] if rs_toks else []
        kmm = P.add("tensor", gk, deps=[wtok] + list(deps) + kdep, fifo=True)
        wring.used(wi, kmm)
        wb0 = P.add("vector", lambda e: e.memset(wbc_view, 1.0),
                    deps=kdep)
        wb1 = P.add("vector",
                    (lambda e, j=j:
                     e.tensor_scalar(wbc_view, wbc_view, wcol[:, j:j + 1],
                                     None, ALU.mult)),
                    deps=[wb0, t1])
        wsl2, wi2, wtok2 = wring.load(wr_ap[:, j * 128:(j + 1) * 128], NT)
        rtoks = []
        for ci in range(NCH):
            def gr(e, wsl2=wsl2, ci=ci):
                r = None
                for kt in range(NT):
                    r = e.matmul(b["PSD"][:, 0:512],
                                 wsl2[:, kt * 128:(kt + 1) * 128],
                                 y_buf[:, kt * S + ci * CH:kt * S + ci * CH + CH],
                                 start=(kt == 0), stop=(kt == NT - 1))
                return r
            rdep = rtoks[-1:] if rtoks else (rs_toks[-1:] if rs_toks else [])
            rmm = P.add("tensor", gr, deps=[wtok2] + list(deps) + rdep,
                        fifo=True)
            rt = P.add("scalar",
                       (lambda e, j=j, ci=ci:
                        e.activation(r_buf[:, j * S + ci * CH:j * S + ci * CH + CH],
                                     b["PSD"][:, 0:512], AF.Sigmoid)),
                       deps=[rmm])
            rtoks.append(rt)
        wring.used(wi2, rtoks[-1])
        sc_prev = None
        for ci in range(NCH):
            sv = b["PSB"] if ci % 2 == 0 else b["PSC"]
            init = 0.0 if ci == 0 else lastc[0:128, 0:1]
            sc = P.add("vector",
                       (lambda e, sv=sv, ci=ci, init=init:
                        e.tensor_tensor_scan(sv[:, 0:512], wbc_view,
                                             b["PSA"][:, ci * CH:ci * CH + CH],
                                             init, ALU.mult, ALU.add)),
                       deps=[kmm, wb1] + ([sc_prev] if sc_prev else [])
                       + (rs_toks[-2:] if rs_toks else []))
            cp = P.add("vector",
                       (lambda e, sv=sv:
                        e.tensor_copy(lastc, sv[:, 511:512])),
                       deps=[sc])
            rs = P.add("vector",
                       (lambda e, sv=sv, j=j, ci=ci:
                        e.tensor_tensor(
                            rs_buf[:, j * S + ci * CH:j * S + ci * CH + CH],
                            sv[:, 0:512],
                            r_buf[:, j * S + ci * CH:j * S + ci * CH + CH],
                            ALU.mult)),
                       deps=[sc, rtoks[ci], cp])
            sc_prev = cp
            rs_toks.append(rs)
    orot = PsumRot([b["PSA"][:, i * 512:(i + 1) * 512] for i in range(4)])
    for mo in range(NT):
        wsl, wi, wtok = wring.load(wv_ap[:, mo * 128:(mo + 1) * 128], NT)
        last = None
        for ci in range(NCH):
            oi, pv, ofree = orot.next()

            def gmm(e, wsl=wsl, pv=pv, ci=ci):
                r = None
                for kt in range(NT):
                    r = e.matmul(pv, wsl[:, kt * 128:(kt + 1) * 128],
                                 rs_buf[:, kt * S + ci * CH:kt * S + ci * CH + CH],
                                 start=(kt == 0), stop=(kt == NT - 1))
                return r
            mm = P.add("tensor", gmm,
                       deps=[wtok] + rs_toks[-1:] + ofree, fifo=True)
            ev = P.add("vector",
                       (lambda e, mo=mo, ci=ci, pv=pv:
                        e.tensor_tensor(
                            b["X"][:, mo * S + ci * CH:mo * S + ci * CH + CH],
                            b["X"][:, mo * S + ci * CH:mo * S + ci * CH + CH],
                            pv, ALU.add)),
                       deps=[mm])
            orot.done(oi, ev)
            last = ev
            out.append(ev)
        wring.used(wi, last)
    return out



def emit_fold(P, b, w_dram, rtb_view, fst_view, scratch_ap, itiles, wring,
              deps):
    toks = []
    for i in range(itiles):
        wsl, wi, wtok = wring.load(w_dram[:, i * 128:(i + 1) * 128], NT)
        evs = []
        for n in range(2):
            ps = b["PSB"] if n == 0 else b["PSC"]

            def gmm(e, wsl=wsl, n=n, ps=ps):
                r = None
                for kt in range(NT):
                    r = e.matmul(ps[:, 0:512],
                                 wsl[:, kt * 128:(kt + 1) * 128],
                                 rtb_view[:, kt * 1024 + n * 512:
                                          kt * 1024 + n * 512 + 512],
                                 start=(kt == 0), stop=(kt == NT - 1))
                return r
            mm = P.add("tensor", gmm, deps=[wtok] + list(deps) + toks[-1:]
                       + evs[-2:-1], fifo=True)
            ev = P.add("scalar",
                       (lambda e, n=n, ps=ps:
                        e.activation(fst_view[:, n * 512:n * 512 + 512],
                                     ps[:, 0:512], AF.Copy)),
                       deps=[mm])
            evs.append(ev)
        st = P.add("sync",
                   (lambda e, i=i:
                    e.dma_start(scratch_ap[i * 128:(i + 1) * 128, :],
                                fst_view[:, 0:1024])),
                   deps=evs, ctr=P.ctr("fsd"), n=16)
        toks.append(st)
        wring.used(wi, st)
    return toks



def emit_sl_l0ffn(P, b, dr, pfx, deps_in):
    def qslot(h):
        return ((h % 4) + 4 * (h // 8), (h // 4) % 2)

    def kslot(h):
        return ((h // 4) // 2, (h // 4) % 2)

    def vblk(h):
        return h // 4

    def pair_heads(mi):
        return (2 * mi, 2 * mi + 1)

    ln = emit_layernorm(P, b, b["A4"], b["A1"][:, 0:2048], deps_in)
    rm1 = emit_rmsnorm(P, b, b["A4"], b["A1"],
                       lambda j: b["NORMW"][:, j:j + 1],
                       b["A3"][:, 12352:13376], joined(ln, deps_in))
    gw = b["A3"][:, 13888:13952]
    gwt = P.add("gpsimd",
                (lambda e: e.dma_start(
                    gw.rearrange("p (j m) -> p j m", j=NT),
                    dr[pfx + "gwT"].rearrange("(j p) m -> p j m", p=128))),
                deps=deps_in, ctr=P.ctr("gwd"), n=16)
    wring = WRing(P, b["A3"], 8256, 4, 1024, tag="wA")
    pr = PsumRot([b["PSB"][:, 0:512], b["PSC"][:, 0:512], b["PSD"][:, 0:512]])
    ktoks = emit_proj(P, b, wring, dr[pfx + "wkT"], b["A1"], C, 256, pr,
                      mk_evac_to(P, b["A3"], "scalar"), deps_src=rm1,
                      wdeps=deps_in)
    vt1 = P.add("vector", lambda e: e.memset(b["A3"][:, 4096:8256], 1.0),
                deps=joined(rm1, deps_in))
    vtoks = emit_proj(P, b, wring, dr[pfx + "wvT"], b["A1"], C, 256, pr,
                      mk_evac_vtok(P, b, b["A3"][:, 4096:8256], pair_heads,
                                   b["A3"][:, 13376:13888]),
                      deps_src=joined(rm1, [vt1]), wdeps=deps_in)
    qtoks = emit_proj(P, b, wring, dr[pfx + "wqT"], b["A1"], C, C, pr,
                      mk_evac_to(P, b["A2"], "scalar"), deps_src=rm1,
                      wdeps=deps_in)
    a_dead = joined(ktoks, vtoks, qtoks)
    cmskt = b["A1"][:, 4096:6144]
    cmt = P.add("gpsimd", lambda e: e.dma_start(cmskt, dr["c_cmskt"]),
                deps=a_dead, ctr=P.ctr("cmd"), n=16)
    at = emit_attn(P, b, b["A2"], b["A3"], b["A3"][:, 4096:8256], b["A2"],
                   qslot, kslot, vblk, True, joined(a_dead, [cmt]),
                   b["A1"][:, 0:4096], cmskt=cmskt)
    wring2 = WRing(P, b["A3"], 8256, 4, 1024, tag="wA")

    def evac_h(mi, ci, pv, deps):
        c0 = mi * S + ci * CH
        e1 = P.add("vector",
                   (lambda e, c0=c0, pv=pv:
                    e.tensor_tensor(b["A4"][:, c0:c0 + CH],
                                    b["A4"][:, c0:c0 + CH], pv, ALU.add)),
                   deps=deps)
        e2 = P.add("vector",
                   (lambda e, c0=c0:
                    e.scalar_tensor_tensor(b["X"][:, c0:c0 + CH],
                                           b["A4"][:, c0:c0 + CH], 0.5,
                                           b["X"][:, c0:c0 + CH],
                                           ALU.mult, ALU.add)),
                   deps=[e1])
        return e2
    pr2 = PsumRot([b["PSB"][:, 0:512], b["PSC"][:, 0:512], b["PSD"][:, 0:512]])
    htoks = emit_proj(P, b, wring2, dr[pfx + "woT"], b["A2"], C, C, pr2,
                      evac_h, deps_src=at, wdeps=joined(at))
    rm2 = emit_rmsnorm(P, b, b["A4"], b["A2"],
                       lambda j: b["NORMW"][:, 8 + j:9 + j],
                       b["A1"][:, 0:1024], joined(htoks))
    esel = b["A1"][:, 8448:10496]
    cbe4 = b["A1"][:, 10496:12544]
    combt = b["A1"][:, 6400:8448]
    cmb = b["A1"][:, 6144:6160]
    est = P.add("gpsimd", lambda e: e.dma_start(esel[0:16, :], dr["c_esel"]),
                deps=joined(htoks), ctr=P.ctr("esd"), n=16)
    g8 = emit_gate8(P, b, b["A2"], gw, b["F32A"][:, 0:64], cmb, combt,
                    joined(rm2, [gwt]))
    wring3 = WRing(P, b["A1"], 0, 4, 1024, tag="wM")
    m8 = emit_moe8(P, b, b["A2"], dr[pfx + "expwT"], esel, combt, cbe4,
                   wring3, joined(rm2), joined(g8, [est]),
                   fe_buf=b["A3"], wdeps=joined(rm2))
    return joined(m8, htoks)


def emit_sl_l0attn(P, b, dr, scr_q, scr_kb, deps_in):
    def slot(h):
        return (h // 2, h % 2)

    def vblk(h):
        return h

    def pair_heads(mi):
        return (2 * mi, 2 * mi + 1)

    ln = emit_layernorm(P, b, b["A4"], b["A3"][:, 0:2048], deps_in)
    wring = WRing(P, b["C8B"], 0, 2, 1024, tag="wC")
    pr = PsumRot([b["PSB"][:, 0:512], b["PSC"][:, 0:512], b["PSD"][:, 0:512]])
    t2t = emit_proj(P, b, wring, dr["l0a_wvaT"], b["A4"], C, 256, pr,
                    mk_evac_to(P, b["A2"], "scalar"), deps_src=ln,
                    wdeps=deps_in)
    vt1 = P.add("vector", lambda e: e.memset(b["A1"][:, 0:A1COLS], 1.0),
                deps=joined(ln, deps_in))
    vtt = emit_proj(P, b, wring, dr["l0a_wvbT"], b["A2"], 256, C, pr,
                    mk_evac_vtok(P, b, b["A1"], pair_heads,
                                 b["A2"][:, 8192:8704]),
                    deps_src=joined(t2t, [vt1]), wdeps=deps_in)
    t1t = emit_proj(P, b, wring, dr["l0a_wkaT"], b["A4"], C, 256, pr,
                    mk_evac_to(P, b["A2"], "scalar", dst_col0=4096),
                    deps_src=ln, wdeps=deps_in)
    ktt = emit_proj(P, b, wring, scr_kb, b["A2"], 256, C, pr,
                    mk_evac_to(P, b["A3"], "scalar"), deps_src=t1t,
                    src_col0=4096, wdeps=joined(ln, deps_in))
    qtt = emit_proj(P, b, wring, scr_q, b["A4"], C, C, pr,
                    mk_evac_to(P, b["A2"], "scalar"),
                    deps_src=joined(ln, ktt, vtt), wdeps=deps_in)
    at = emit_attn(P, b, b["A2"], b["A3"], b["A1"], b["A2"],
                   slot, slot, vblk, False, joined(qtt, ktt, vtt),
                   b["A4"][:, 0:4096])
    wring2 = WRing(P, b["C8B"], 0, 2, 1024, tag="wC")
    pr2 = PsumRot([b["PSB"][:, 0:512], b["PSC"][:, 0:512], b["PSD"][:, 0:512]])
    ot = emit_proj(P, b, wring2, dr["l0a_woT"], b["A2"], C, C, pr2,
                   mk_evac_resid(P, b), deps_src=at, wdeps=joined(at))
    return joined(ot)


def emit_sl_l1ffn(P, b, dr, gbcb, deps_in):
    esel = b["A1"][:, 6144:8192]
    combt = b["A1"][:, 8192:10240]
    cbe4 = b["A1"][:, 10240:12288]
    sg = b["A1"][:, 12288:13312]
    gw = b["A1"][:, 13312:13440]
    cmb = b["A1"][:, 13440:13456]
    ln = emit_layernorm(P, b, b["A4"], b["A2"][:, 0:2048], deps_in)
    est = P.add("gpsimd", lambda e: e.dma_start(esel[0:16, :], dr["c_esel"]),
                deps=deps_in, ctr=P.ctr("esd"), n=16)
    gwt = P.add("gpsimd",
                (lambda e: e.dma_start(
                    gw.rearrange("p (j m) -> p j m", j=NT),
                    dr["l1f_gwT"].rearrange("(j p) m -> p j m", p=128))),
                deps=deps_in, ctr=P.ctr("gwd"), n=16)
    g16 = emit_gate16(P, b, b["A4"], gw, gbcb[:, 0:16], b["F32A"][:, 0:64],
                      cmb, combt, joined(ln, [gwt, est]))
    wring = WRing(P, b["A1"], 0, 4, 1024, tag="wA")
    w2ring = WRing(P, b["A1"], 4096, 4, 512, tag="wB")
    m16 = emit_moe16(P, b, b["A4"], dr["l1f_w13T"], dr["l1f_w2T"],
                     dr["l1f_w13sT"], dr["l1f_w2sT"], esel, combt, cbe4, sg,
                     b["A2"], wring, w2ring, joined(ln, deps_in), joined(g16))
    return joined(m16)


def emit_sl_l1attn(P, b, dr, deps_in):
    ln = emit_layernorm(P, b, b["A4"], b["A2"][:, 0:2048], deps_in)
    wring = WRing(P, b["A1"], 0, 4, 1024, tag="wA")
    rw = emit_rwkv(P, b, b["A4"], dr["l1a_wkT"], dr["l1a_wvT"],
                   dr["l1a_wrT"], b["NORMW"][:, 16:24], b["A3"], b["A2"],
                   wring, b["F32A"][:, 0:512], joined(ln, deps_in))
    return joined(rw)



W_SHAPES = {
    "l0f_wqT": (C, C), "l0f_wkT": (C, 256), "l0f_wvT": (C, 256),
    "l0f_woT": (C, C), "l0f_gwT": (C, 8), "l0f_expwT": (8, C, C),
    "l0f_nw": (128, 16),
    "l0a_wq": (C, C), "l0a_wkaT": (C, 256), "l0a_wkb": (C, 256),
    "l0a_wvaT": (C, 256), "l0a_wvbT": (256, C), "l0a_woT": (C, C),
    "l1f_gwT": (C, 16), "l1f_gbc": (128, 16), "l1f_w13T": (16, C, C),
    "l1f_w2T": (16, 512, C), "l1f_w13sT": (C, C), "l1f_w2sT": (512, C),
    "l1a_wkT": (C, C), "l1a_wvT": (C, C), "l1a_wrT": (C, C),
    "l1a_td": (128, 8),
}
CONST_NAMES = ("c_idn", "c_onesc", "c_onesf", "c_onescf", "c_cmskt",
               "c_esel", "c_rt")


def _declare(P, cs, kinds):
    nc = P.nc
    dr = {}
    dr["x"] = nc.dram_tensor("x", [BPC, C, S], F32, kind="ExternalInput").ap()
    dr["xo"] = nc.dram_tensor("xo", [BPC, C, S], F32,
                              kind="ExternalOutput").ap()
    names = ["x"]
    for nm, shp in W_SHAPES.items():
        if nm.split("_")[0] in kinds:
            dr[nm] = nc.dram_tensor(nm, list(shp), F32,
                                    kind="ExternalInput").ap()
            names.append(nm)
    need_c = {"l0f": ("c_idn", "c_onesc", "c_onesf", "c_onescf", "c_cmskt",
                      "c_esel"),
              "l0a": ("c_idn", "c_onesc", "c_onesf", "c_onescf", "c_rt"),
              "l1f": ("c_idn", "c_onesc", "c_onesf", "c_onescf", "c_esel"),
              "l1a": ("c_idn", "c_onesc", "c_onesf", "c_onescf")}
    cset = []
    for k in kinds:
        cset += list(need_c[k])
    for nm in CONST_NAMES:
        if nm in cset:
            dr[nm] = nc.dram_tensor(nm, list(cs[nm].shape),
                                    BF16 if cs[nm].dtype == BF16NP else F32,
                                    kind="ExternalInput").ap()
            names.append(nm)
    return dr, names


def _prologues(P, b, dr, kinds):
    st = {}
    toks = []
    if "l0f" in kinds:
        t = P.add("gpsimd",
                  lambda e: e.dma_start(b["NORMW"][:, 0:16], dr["l0f_nw"]),
                  ctr=P.ctr("nwd"), n=16)
        toks.append(t)
    if "l1a" in kinds:
        t = P.add("gpsimd",
                  lambda e: e.dma_start(b["NORMW"][:, 16:24], dr["l1a_td"]),
                  ctr=P.ctr("tdd"), n=16)
        toks.append(t)
    if "l1f" in kinds:
        st["gbcb"] = P.sbuf("GBC", [128, 16], F32)
        t = P.add("gpsimd",
                  lambda e: e.dma_start(st["gbcb"][:, :], dr["l1f_gbc"]),
                  ctr=P.ctr("gbd"), n=16)
        toks.append(t)
    if "l0a" in kinds:
        nc = P.nc
        st["scr_q"] = nc.dram_tensor("scr_q", [C, C], BF16,
                                     kind="Internal").ap()
        st["scr_kb"] = nc.dram_tensor("scr_kb", [256, C], BF16,
                                      kind="Internal").ap()
        rtb = b["A3"][:, 0:8192]
        rtt = P.add("gpsimd",
                    (lambda e: e.dma_start(
                        rtb.rearrange("p (kt m) -> p kt m", kt=NT),
                        dr["c_rt"].rearrange("(kt p) m -> p kt m", p=128))),
                    ctr=P.ctr("rtd"), n=16)
        wringf = WRing(P, b["A1"], 0, 2, 1024, tag="wF")
        fst = b["A1"][:, 2048:3072]
        f1 = emit_fold(P, b, dr["l0a_wq"], rtb, fst, st["scr_q"], NT, wringf,
                       [rtt])
        f2 = emit_fold(P, b, dr["l0a_wkb"], rtb, fst, st["scr_kb"], 2, wringf,
                       [rtt] + f1[-1:])
        toks += f1 + f2
    st["toks"] = toks
    return st


def build_prog(cs, seq):
    P = Prog()
    kinds = sorted(set(seq))
    dr, names = _declare(P, cs, kinds)
    b = make_bufs(P)
    ct = emit_const_loads(P, b, dr)
    st = _prologues(P, b, dr, kinds)
    prev_store = []
    for bi in range(BPC):
        lx = emit_load_x(P, b, dr["x"], bi,
                         deps=joined(ct, prev_store, st["toks"]))
        t = joined(lx, ct, st["toks"], prev_store)
        for kind in seq:
            if kind == "l0f":
                t = emit_sl_l0ffn(P, b, dr, "l0f_", t)
            elif kind == "l0a":
                t = emit_sl_l0attn(P, b, dr, st["scr_q"], st["scr_kb"], t)
            elif kind == "l1f":
                t = emit_sl_l1ffn(P, b, dr, st["gbcb"], t)
            elif kind == "l1a":
                t = emit_sl_l1attn(P, b, dr, t)
        prev_store = emit_store_x(P, b, dr["xo"], bi, deps=t)
    P.realize()
    return P.nc, names



def _t(a):
    return np.ascontiguousarray(np.asarray(a, np.float32).T)


def _qperm_cols():
    perm = np.zeros(C, dtype=np.int64)
    for h in range(NH):
        tile = (h % 4) + 4 * (h // 8)
        par = (h // 4) % 2
        slot = 128 * tile + 64 * par
        perm[slot:slot + 64] = np.arange(64 * h, 64 * h + 64)
    return perm


def host_prep(inputs):
    _ = inputs["L1_ffn_input_ids"]
    _ = inputs["L1_attn_time_first"]
    cs = host_consts()
    qp = _qperm_cols()
    w = {
        "l0f_wqT": np.ascontiguousarray(_t(inputs["L0_ffn_wq"])[:, qp]),
        "l0f_wkT": _t(inputs["L0_ffn_wk"]),
        "l0f_wvT": _t(inputs["L0_ffn_wv"]),
        "l0f_woT": np.ascontiguousarray(_t(inputs["L0_ffn_wo"])[qp, :]),
        "l0f_gwT": _t(inputs["L0_ffn_gate_w"]),
        "l0f_expwT": np.ascontiguousarray(
            np.asarray(inputs["L0_ffn_expert_w"],
                       np.float32).transpose(0, 2, 1)),
        "l0f_nw": np.concatenate([featvec(inputs["L0_ffn_attn_norm_w"]),
                                  featvec(inputs["L0_ffn_ffn_norm_w"])],
                                 axis=1),
        "l0a_wq": np.ascontiguousarray(
            np.asarray(inputs["L0_attn_wq"], np.float32)),
        "l0a_wkaT": _t(inputs["L0_attn_wk_a"]),
        "l0a_wkb": np.ascontiguousarray(
            np.asarray(inputs["L0_attn_wk_b"], np.float32)),
        "l0a_wvaT": _t(inputs["L0_attn_wv_a"]),
        "l0a_wvbT": _t(inputs["L0_attn_wv_b"]),
        "l0a_woT": _t(inputs["L0_attn_wo"]),
        "l1f_gwT": _t(inputs["L1_ffn_gate_w"]),
        "l1f_gbc": np.ascontiguousarray(
            np.broadcast_to(np.asarray(inputs["L1_ffn_gate_b"], np.float32),
                            (128, 16))),
        "l1f_w13T": np.ascontiguousarray(
            np.asarray(inputs["L1_ffn_w13"], np.float32).transpose(0, 2, 1)),
        "l1f_w2T": np.ascontiguousarray(
            np.asarray(inputs["L1_ffn_w2"], np.float32).transpose(0, 2, 1)),
        "l1f_w13sT": _t(inputs["L1_ffn_w13_s"]),
        "l1f_w2sT": _t(inputs["L1_ffn_w2_s"]),
        "l1a_wkT": _t(inputs["L1_attn_Wk"]),
        "l1a_wvT": _t(inputs["L1_attn_Wv"]),
        "l1a_wrT": _t(inputs["L1_attn_Wr"]),
        "l1a_td": featvec(inputs["L1_attn_time_decay"]),
    }
    return cs, w


FUSED = True


def kernel(**inputs):
    cs, w = host_prep(inputs)
    x = np.ascontiguousarray(
        np.asarray(inputs["L0_attn_x"], np.float32).transpose(0, 2, 1))
    shards = [np.ascontiguousarray(x[c * BPC:(c + 1) * BPC])
              for c in range(NCORES)]

    def run(nc, names, xs):
        in_maps = []
        for c in range(NCORES):
            m = {"x": xs[c]}
            for nm in names:
                if nm == "x":
                    continue
                m[nm] = w[nm] if nm in w else cs[nm]
            in_maps.append(m)
        res = run_bass_kernel_spmd(nc, in_maps, core_ids=list(range(NCORES)))
        return [r["xo"] for r in res.results]

    if FUSED:
        nc, names = build_prog(cs, ["l0f", "l0a", "l0f", "l1f", "l1a", "l1f"])
        shards = run(nc, names, shards)
    else:
        progs = {k: build_prog(cs, [k]) for k in ("l0f", "l0a", "l1f", "l1a")}
        for kind in ["l0f", "l0a", "l0f", "l1f", "l1a", "l1f"]:
            nc, names = progs[kind]
            shards = run(nc, names, shards)

    out = np.concatenate(shards, axis=0)
    return np.ascontiguousarray(out.transpose(0, 2, 1)).astype(np.float32)


# version 11
# speedup vs baseline: 1.0084x; 1.0084x over previous
import numpy as np
import ml_dtypes
from contextlib import ExitStack

import concourse.bass as bass
import concourse.mybir as mybir
from concourse.bass_utils import run_bass_kernel_spmd

F32 = mybir.dt.float32
BF16 = mybir.dt.bfloat16
AF = mybir.ActivationFunctionType
ALU = mybir.AluOpType
AX = mybir.AxisListType
BF16NP = ml_dtypes.bfloat16

NCORES = 8
BPC = 4
S = 2048
C = 1024
NT = C // 128
CH = 512
NCH = S // CH
NH = 16
HD = 64
A1COLS = NH * 16 * 65



class Prog:
    ENGS = ("tensor", "vector", "scalar", "gpsimd", "sync")

    def __init__(self):
        self.nc = bass.Bass("TRN2", target_bir_lowering=False)
        self.ops = {e: [] for e in self.ENGS}
        self.ctx = ExitStack()
        self.ctrs = {}
        for e in self.ENGS:
            self._mkctr(e)
        for d in ("gdma", "sdma", "odma", "xdma", "c8d"):
            self._mkctr(d)
        self.ring_busy = {}

    def _mkctr(self, name):
        h = self.ctx.enter_context(self.nc.semaphore(name))
        self.ctrs[name] = [h, 0]

    def ctr(self, name):
        if name not in self.ctrs:
            self._mkctr(name)
        return name

    def sbuf(self, name, shape, dt):
        return self.ctx.enter_context(self.nc.sbuf_tensor(name, shape, dt))

    def psum(self, name, shape, dt=F32):
        return self.ctx.enter_context(self.nc.psum_tensor(name, shape, dt))

    def add(self, eng, build, deps=(), ctr=None, n=1, fifo=False):
        cname = ctr or eng
        h, cnt = self.ctrs[cname]
        waits = {}
        for d in deps:
            if d is None:
                continue
            dc, dv = d
            if dv <= 0 or (fifo and dc == eng):
                continue
            waits[dc] = max(waits.get(dc, 0), dv)
        wl = [(self.ctrs[dc][0], dv) for dc, dv in waits.items()]
        self.ctrs[cname][1] = cnt + n
        self.ops[eng].append((wl, build, (h, n)))
        return (cname, cnt + n)

    def realize(self):
        finals = [(self.ctrs[n][0], self.ctrs[n][1]) for n in self.ctrs
                  if n not in self.ENGS and self.ctrs[n][1] > 0]
        blk = self.ctx.enter_context(self.nc.Block())
        for eng in self.ENGS:
            ops = self.ops[eng]

            def body(e, ops=ops, eng=eng):
                for waits, build, inc in ops:
                    for (h, v) in waits:
                        e.wait_ge(h, v)
                    r = build(e)
                    if r is not None:
                        r.then_inc(*inc)
                if eng == "sync":
                    for h, v in finals:
                        e.wait_ge(h, v)
            getattr(blk, eng)(body)
        return self.nc


def joined(*tok_lists):
    best = {}
    for tl in tok_lists:
        for t in tl:
            if t is None:
                continue
            c, v = t
            best[c] = max(best.get(c, 0), v)
    return [(c, v) for c, v in best.items()]


class PsumRot:
    def __init__(self, views):
        self.views = views
        self.free = [None] * len(views)
        self.i = 0

    def next(self):
        i = self.i
        self.i = (self.i + 1) % len(self.views)
        return i, self.views[i], ([self.free[i]] if self.free[i] else [])

    def done(self, i, tok):
        self.free[i] = tok



def host_consts():
    cs = {}
    cs["c_idn"] = np.eye(128, dtype=BF16NP)
    cs["c_ones2"] = np.ones((128, 128), dtype=BF16NP)
    cs["c_onesc"] = np.ones((128, 1), dtype=BF16NP)
    cs["c_onesf"] = np.ones((1, 128), dtype=np.float32)
    cs["c_onescf"] = np.ones((128, 1), dtype=np.float32)
    kp = np.arange(128)[:, None]
    q = np.arange(512)[None, :]
    m = np.stack([np.where(q >= 128 * r + kp, 0.0, -1e30) for r in range(4)], 1)
    cs["c_cmskt"] = np.ascontiguousarray(m.reshape(128, 2048)).astype(BF16NP)
    es = np.zeros((16, 16, 128), dtype=np.float32)
    for e in range(16):
        es[e, e, :] = 1.0
    cs["c_esel"] = np.ascontiguousarray(es.transpose(0, 1, 2).reshape(16, 2048)
                                        ).astype(BF16NP)
    R = np.eye(C, dtype=np.float64)
    inv = 10000.0 ** (-np.arange(0, 16, 2) / 64.0)
    for h in range(NH):
        for j in range(8):
            ct, st = np.cos(h * inv[j]), np.sin(h * inv[j])
            a, b = 64 * h + 2 * j, 64 * h + 2 * j + 1
            R[a, a] = ct
            R[a, b] = -st
            R[b, a] = st
            R[b, b] = ct
    cs["c_rt"] = np.ascontiguousarray(R.T).astype(np.float32)
    return cs


def featvec(v):
    return np.ascontiguousarray(np.asarray(v, np.float32).reshape(NT, 128).T)



def make_bufs(P):
    b = {}
    b["X"] = P.sbuf("X", [128, NT * S], F32)
    b["A1"] = P.sbuf("A1", [128, A1COLS], BF16)
    b["A2"] = P.sbuf("A2", [128, NT * S], BF16)
    b["A3"] = P.sbuf("A3", [128, NT * S], BF16)
    b["A4"] = P.sbuf("A4", [128, NT * S], BF16)
    b["F32A"] = P.sbuf("F32A", [128, 1024], F32)
    b["ROWS"] = P.sbuf("ROWS", [1, 1536], F32)
    b["C8B"] = P.sbuf("C8B", [128, 2048], BF16)
    b["COL"] = P.sbuf("COL", [128, 40], F32)
    b["CBC"] = P.sbuf("CBC", [128, 16], F32)
    b["IDN"] = P.sbuf("IDN", [128, 128], BF16)
    b["ONESC"] = P.sbuf("ONESC", [128, 1], BF16)
    b["ONESF"] = P.sbuf("ONESF", [1, 128], F32)
    b["ONESCF"] = P.sbuf("ONESCF", [128, 1], F32)
    b["NORMW"] = P.sbuf("NORMW", [128, 24], F32)
    b["EPSC"] = P.sbuf("EPSC", [1, 2], F32)
    b["PSA"] = P.psum("PSA", [128, 2048], F32)
    b["PSB"] = P.psum("PSB", [128, 512], F32)
    b["PSC"] = P.psum("PSC", [128, 512], F32)
    b["PSD"] = P.psum("PSD", [128, 512], F32)
    b["PTP"] = P.psum("PTP", [128, 512], BF16)
    return b


def xt(b, j):
    return b["X"][:, j * S:(j + 1) * S]


def emit_const_loads(P, b, d, extra=()):
    def g(e):
        sem = P.ctrs["sdma"][0]
        e.dma_start(b["IDN"][:, :], d["c_idn"]).then_inc(sem, 16)
        e.dma_start(b["ONESC"][:, :], d["c_onesc"]).then_inc(sem, 16)
        e.dma_start(b["ONESF"][:, :], d["c_onesf"]).then_inc(sem, 16)
        e.dma_start(b["ONESCF"][:, :], d["c_onescf"]).then_inc(sem, 16)
        for dst, src in extra:
            e.dma_start(dst, src).then_inc(sem, 16)
        return None
    t1 = P.add("sync", g, ctr="sdma", n=16 * (4 + len(extra)))

    def ge(e):
        e.memset(b["EPSC"][0:1, 0:1], 1e-5)
        return e.memset(b["EPSC"][0:1, 1:2], 1e-6)
    t2 = P.add("vector", ge)
    return [t1, t2]


def emit_load_x(P, b, x_ap, bi, deps=()):
    src = x_ap[bi].rearrange("(j p) t -> p j t", p=128)
    dst = b["X"][:, :].rearrange("p (j t) -> p j t", t=S)
    return [P.add("sync", lambda e: e.dma_start(dst, src), deps=deps,
                  ctr="xdma", n=16)]


def emit_store_x(P, b, xo_ap, bi, deps=()):
    dst = xo_ap[bi].rearrange("(j p) t -> p j t", p=128)
    src = b["X"][:, :].rearrange("p (j t) -> p j t", t=S)
    return [P.add("sync", lambda e: e.dma_start(dst, src), deps=deps,
                  ctr="odma", n=16)]



class WRing:

    def __init__(self, P, arena, base_col, nslots, cols, tag="w"):
        self.P = P
        self.tag = tag
        self.slots = [arena[:, base_col + i * cols: base_col + (i + 1) * cols]
                      for i in range(nslots)]
        self.n = nslots
        self.i = 0

    def load(self, dram_slice, ktiles, deps=()):
        P = self.P
        i = self.i
        self.i = (self.i + 1) % self.n
        sl = self.slots[i]
        cast = dram_slice.tensor.dtype != BF16
        cname = P.ctr(f"{self.tag}{'g' if cast else 's'}{i}")
        key = (self.tag, i)
        d = list(deps) + [P.ring_busy.get(key)]
        mcols = dram_slice.shape[-1]
        src = dram_slice.rearrange("(kt p) m -> p kt m", p=128)
        dst = sl[:, 0:ktiles * mcols].rearrange("p (kt m) -> p kt m", kt=ktiles)
        eng = "gpsimd" if cast else "sync"
        tok = P.add(eng, (lambda e: e.dma_start(dst, src)), deps=d,
                    ctr=cname, n=16)
        return sl, key, tok

    def used(self, key, tok):
        self.P.ring_busy[key] = tok



def emit_proj(P, b, wring, wt_dram, src_buf, kdim, mdim, prot, evac,
              deps_src=(), src_stride=S, src_col0=0, chunks=None, wdeps=()):
    ktiles = kdim // 128
    mtiles = mdim // 128
    chunks = list(range(NCH)) if chunks is None else chunks
    etoks = []
    mmtok = None
    for mi in range(mtiles):
        wsl, wi, wtok = wring.load(wt_dram[:, mi * 128:(mi + 1) * 128], ktiles,
                                   deps=wdeps)
        for ci in chunks:
            si, pv, ftoks = prot.next()

            def g(e, wsl=wsl, pv=pv, ci=ci, ktiles=ktiles):
                last = None
                for kt in range(ktiles):
                    c0 = src_col0 + kt * src_stride + ci * CH
                    last = e.matmul(pv, wsl[:, kt * 128:(kt + 1) * 128],
                                    src_buf[:, c0:c0 + CH],
                                    start=(kt == 0), stop=(kt == ktiles - 1))
                return last
            mmtok = P.add("tensor", g, deps=[wtok] + list(deps_src) + ftoks,
                          fifo=True)
            et = evac(mi, ci, pv, [mmtok])
            prot.done(si, et)
            etoks.append(et)
        wring.used(wi, mmtok)
    return etoks


def mk_evac_to(P, dst_buf, eng="scalar", dst_stride=S, dst_col0=0):
    def evac(mi, ci, pv, deps):
        c0 = dst_col0 + mi * dst_stride + ci * CH
        dst = dst_buf[:, c0:c0 + CH]
        if eng == "scalar":
            return P.add("scalar", lambda e: e.activation(dst, pv, AF.Copy),
                         deps=deps)
        return P.add("vector", lambda e: e.tensor_copy(dst, pv), deps=deps)
    return evac


def mk_evac_resid(P, b, scale=None, extra=()):
    def evac(mi, ci, pv, deps):
        dst = b["X"][:, mi * S + ci * CH: mi * S + ci * CH + CH]
        if scale is None:
            return P.add("vector",
                         lambda e: e.tensor_tensor(dst, dst, pv, ALU.add),
                         deps=list(deps) + list(extra))
        return P.add("vector",
                     lambda e: e.scalar_tensor_tensor(dst, pv, float(scale),
                                                      dst, ALU.mult, ALU.add),
                     deps=list(deps) + list(extra))
    return evac



def emit_layernorm(P, b, dst_buf, arena, deps, eps=1e-5):
    xq = [arena[:, 1024:1536], arena[:, 1536:2048]]
    out = []
    sum_free = []
    row_free = []
    rowvar_free = {}
    xq_free = {0: None, 1: None}
    for ci in range(NCH):
        negmu = b["ROWS"][0:1, 0:512]
        invr = b["ROWS"][0:1, 512:1024]
        msh = b["ROWS"][0:1, 1024:1536]
        sumx = b["PSA"][0:1, 0:512]
        sumq = b["PSA"][0:1, 512:1024]
        mmx = mmq = None
        for j in range(NT):
            sl = j % 2
            mmx = P.add("tensor",
                        (lambda e, j=j, ci=ci:
                         e.matmul(sumx, b["ONESCF"][:, 0:1],
                                  xt(b, j)[:, ci * CH:ci * CH + CH],
                                  start=(j == 0), stop=(j == NT - 1))),
                        deps=list(deps) + sum_free, fifo=True)
            sq = P.add("scalar",
                       (lambda e, sl=sl, j=j, ci=ci:
                        e.activation(xq[sl],
                                     xt(b, j)[:, ci * CH:ci * CH + CH],
                                     AF.Square)),
                       deps=list(deps) + [xq_free[sl]] + sum_free)
            mmq = P.add("tensor",
                        (lambda e, sl=sl, j=j:
                         e.matmul(sumq, b["ONESC"][:, 0:1], xq[sl],
                                  start=(j == 0), stop=(j == NT - 1))),
                        deps=[sq] + sum_free, fifo=True)
            xq_free[sl] = mmq
        t1 = P.add("vector",
                   (lambda e, negmu=negmu:
                    e.tensor_scalar(negmu, sumx, -1.0 / C, None, ALU.mult)),
                   deps=[mmx] + [rowvar_free.get(ci % 2)])
        t2 = P.add("vector",
                   (lambda e, negmu=negmu, msh=msh:
                    e.tensor_tensor(msh, negmu, negmu, ALU.mult)),
                   deps=[t1])
        t3 = P.add("vector",
                   (lambda e, invr=invr, msh=msh:
                    e.scalar_tensor_tensor(invr, sumq, 1.0 / C, msh,
                                           ALU.mult, ALU.subtract)),
                   deps=[t2, mmq])
        sum_free = [t1, t3]
        t4 = P.add("scalar",
                   (lambda e, invr=invr, msh=msh:
                    e.activation(msh, invr, AF.Ln, bias=b["EPSC"][0:1, 0:1])),
                   deps=[t3])
        t5 = P.add("scalar",
                   (lambda e, invr=invr, msh=msh:
                    e.activation(invr, msh, AF.Exp, scale=-0.5)),
                   deps=[t4])
        t6 = P.add("vector",
                   (lambda e, negmu=negmu, invr=invr, msh=msh:
                    e.tensor_tensor(msh, negmu, invr, ALU.mult)),
                   deps=[t5])
        bcA = P.add("tensor",
                    (lambda e, invr=invr:
                     e.matmul(b["PSB"][:, 0:512], b["ONESF"][0:1, :],
                              invr, start=True, stop=True)),
                    deps=[t5] + row_free)
        bcB = P.add("tensor",
                    (lambda e, msh=msh:
                     e.matmul(b["PSC"][:, 0:512], b["ONESF"][0:1, :],
                              msh, start=True, stop=True)),
                    deps=[t6] + row_free)
        rowvar_free[ci % 2] = bcB
        rf = []
        for j in range(NT):
            tt1 = P.add("vector",
                        (lambda e, j=j, ci=ci:
                         e.tensor_tensor(b["F32A"][:, 512:1024],
                                         xt(b, j)[:, ci * CH:ci * CH + CH],
                                         b["PSB"][:, 0:512], ALU.mult)),
                        deps=[bcA] + ([rf[-1]] if rf else []))
            tt2 = P.add("vector",
                        (lambda e, j=j, ci=ci:
                         e.tensor_tensor(
                             dst_buf[:, j * S + ci * CH:j * S + ci * CH + CH],
                             b["F32A"][:, 512:1024], b["PSC"][:, 0:512],
                             ALU.add)),
                        deps=[bcB, tt1])
            rf.append(tt2)
            out.append(tt2)
        row_free = [rf[-1]]
    return out


def emit_rmsnorm(P, b, src_buf, dst_buf, nw_col, arena, deps, eps=1e-6):
    xq = [arena[:, 0:512], arena[:, 512:1024]]
    out = []
    sum_free = []
    row_free = []
    rowvar_free = {}
    xq_free = {0: None, 1: None}
    for ci in range(NCH):
        invr = b["ROWS"][0:1, 512:1024]
        msh = b["ROWS"][0:1, 1024:1536]
        sumq = b["PSA"][0:1, 512:1024]
        mmq = None
        for j in range(NT):
            sl = j % 2
            sq = P.add("scalar",
                       (lambda e, j=j, ci=ci, sl=sl:
                        e.activation(
                            xq[sl],
                            src_buf[:, j * S + ci * CH:j * S + ci * CH + CH],
                            AF.Square)),
                       deps=list(deps) + [xq_free[sl]] + sum_free)
            mmq = P.add("tensor",
                        (lambda e, sl=sl, j=j:
                         e.matmul(sumq, b["ONESC"][:, 0:1], xq[sl],
                                  start=(j == 0), stop=(j == NT - 1))),
                        deps=[sq] + sum_free, fifo=True)
            xq_free[sl] = mmq
        t1 = P.add("vector",
                   (lambda e, invr=invr:
                    e.tensor_scalar(invr, sumq, 1.0 / C, None, ALU.mult)),
                   deps=[mmq] + [rowvar_free.get(ci % 2)])
        sum_free = [t1]
        t2 = P.add("scalar",
                   (lambda e, invr=invr, msh=msh:
                    e.activation(msh, invr, AF.Ln, bias=b["EPSC"][0:1, 1:2])),
                   deps=[t1])
        t3 = P.add("scalar",
                   (lambda e, invr=invr, msh=msh:
                    e.activation(invr, msh, AF.Exp, scale=-0.5)),
                   deps=[t2])
        bcA = P.add("tensor",
                    (lambda e, invr=invr:
                     e.matmul(b["PSC"][:, 0:512], b["ONESF"][0:1, :],
                              invr, start=True, stop=True)),
                    deps=[t3] + row_free)
        rowvar_free[ci % 2] = bcA
        rf = []
        for j in range(NT):
            tt = P.add("vector",
                       (lambda e, j=j, ci=ci:
                        e.scalar_tensor_tensor(
                            dst_buf[:, j * S + ci * CH:j * S + ci * CH + CH],
                            src_buf[:, j * S + ci * CH:j * S + ci * CH + CH],
                            nw_col(j), b["PSC"][:, 0:512],
                            ALU.mult, ALU.mult)),
                       deps=[bcA] + ([rf[-1]] if rf else []))
            rf.append(tt)
            out.append(tt)
        row_free = [rf[-1]]
    return out



def mk_evac_vtok(P, b, vt_buf, pair_heads, vs_view):
    state = {"ptp_free": None, "vs_free": None}

    def evac(mi, ci, pv, deps):
        hA, hB = pair_heads(mi)
        cp = P.add("scalar", lambda e: e.activation(vs_view, pv, AF.Copy),
                   deps=list(deps) + [state["vs_free"]])
        trs = []
        for blk in range(4):
            tr = P.add("tensor",
                       (lambda e, blk=blk:
                        e.transpose(b["PTP"][:, blk * 128:blk * 128 + 128],
                                    vs_view[:, blk * 128:blk * 128 + 128],
                                    b["IDN"][:, :])),
                       deps=[cp, state["ptp_free"]], fifo=True)
            trs.append(tr)
        state["vs_free"] = trs[-1]
        kt0 = ci * 4
        last = None
        for blk in range(4):
            kt = kt0 + blk
            cA = 65 * (hA * 16 + kt)
            cB = 65 * (hB * 16 + kt)

            def gv(e, cA=cA, cB=cB, blk=blk):
                e.tensor_copy(vt_buf[:, cA:cA + 64],
                              b["PTP"][:, blk * 128:blk * 128 + 64])
                return e.tensor_copy(vt_buf[:, cB:cB + 64],
                                     b["PTP"][:, blk * 128 + 64:blk * 128 + 128])
            last = P.add("vector", gv, deps=[trs[3]])
        state["ptp_free"] = last
        return last
    return evac


def emit_attn(P, b, q_buf, k_buf, vt_buf, o_buf, qslot, kslot, vblk, causal,
              deps, arena, cmskt=None):
    sqst = arena[:, 0:2048]
    ptst = arena[:, 2048:4096]
    rcrow = b["ROWS"][0:1, 512:1024]
    srot = PsumRot([b["PSA"][:, i * 512:(i + 1) * 512] for i in range(4)])
    orot = PsumRot([b["PSB"][:, 0:512], b["PSD"][:, 0:512]])
    out = []
    rows = [b["PSC"][0:1, 0:512], b["PSD"][0:1, 0:512]]
    rowtok = [[], []]
    sq_free = []
    kvs = {}
    for h in range(NH):
        kvs.setdefault(kslot(h), []).append(h)
    kcol = {}
    acc = []
    prev_kx = []
    for kv, heads in kvs.items():
        kt_, kp_ = kv
        ksl = k_buf[64 * kp_:64 * kp_ + 64, kt_ * S:(kt_ + 1) * S]
        sqk = sqst[64 * kp_:64 * kp_ + 64, 0:2048]
        sq1 = P.add("scalar",
                    (lambda e, sqk=sqk, ksl=ksl:
                     e.activation(sqk, ksl, AF.Square)),
                    deps=list(deps) + sq_free)
        kms = []
        cell = b["COL"][0:1, 1 + len(kcol):2 + len(kcol)]
        for ci in range(NCH):
            rv = rows[ci % 2]
            km = P.add("tensor",
                       (lambda e, ci=ci, sqk=sqk, kp_=kp_, rv=rv:
                        e.matmul(rv, b["ONESC"][64 * kp_:64 * kp_ + 64, 0:1],
                                 sqk[:, ci * 512:ci * 512 + 512],
                                 start=True, stop=True)),
                       deps=[sq1] + rowtok[ci % 2])
            kc = P.add("vector",
                       (lambda e, ci=ci, rv=rv:
                        e.tensor_reduce(b["COL"][0:1, 20 + ci:21 + ci], rv,
                                        AX.X, ALU.max)),
                       deps=[km] + prev_kx)
            rowtok[ci % 2] = [kc]
            kms.append(kc)
        kx = P.add("vector",
                   (lambda e, cell=cell:
                    e.tensor_reduce(cell, b["COL"][0:1, 20:24], AX.X, ALU.max)),
                   deps=kms)
        kcol[kv] = (cell, kx)
        sq_free = kms
        prev_kx = [kx]
        acc.append(kx)
    qcell = b["COL"][0:1, 0:1]
    cbc_toks = {}
    for h in range(NH):
        qt, qp = qslot(h)
        qsl = q_buf[64 * qp:64 * qp + 64, qt * S:(qt + 1) * S]
        sqq = sqst[64 * qp:64 * qp + 64, 0:2048]
        sq2 = P.add("scalar",
                    (lambda e, sqq=sqq, qsl=qsl:
                     e.activation(sqq, qsl, AF.Square)),
                    deps=list(deps) + sq_free)
        qms = []
        for ci in range(NCH):
            rv = rows[ci % 2]
            qm = P.add("tensor",
                       (lambda e, ci=ci, sqq=sqq, qp=qp, rv=rv:
                        e.matmul(rv, b["ONESC"][64 * qp:64 * qp + 64, 0:1],
                                 sqq[:, ci * 512:ci * 512 + 512],
                                 start=True, stop=True)),
                       deps=[sq2] + rowtok[ci % 2])
            qc = P.add("vector",
                       (lambda e, ci=ci, rv=rv:
                        e.tensor_reduce(b["COL"][0:1, 24 + ci:25 + ci], rv,
                                        AX.X, ALU.max)),
                       deps=[qm, cbc_toks.get(h - 1)])
            rowtok[ci % 2] = [qc]
            qms.append(qc)
        sq_free = qms
        kcell, kxt = kcol[kslot(h)]
        t1 = P.add("vector",
                   (lambda e: e.tensor_reduce(qcell, b["COL"][0:1, 24:28],
                                              AX.X, ALU.max)),
                   deps=qms + [cbc_toks.get(h - 1)])
        t2 = P.add("vector",
                   (lambda e, kcell=kcell:
                    e.tensor_tensor(qcell, qcell, kcell, ALU.mult)),
                   deps=[t1, kxt])
        t3 = P.add("scalar",
                   lambda e: e.activation(qcell, qcell, AF.Sqrt),
                   deps=[t2])
        t4 = P.add("vector",
                   (lambda e: e.tensor_scalar(qcell, qcell, -0.125, None,
                                              ALU.mult)),
                   deps=[t3])
        t5 = P.add("tensor",
                   (lambda e: e.matmul(b["PSC"][:, 0:1], b["ONESF"][0:1, :],
                                       qcell, start=True, stop=True)),
                   deps=[t4] + rowtok[0] + rowtok[1])
        cb = P.add("vector",
                   (lambda e, h=h: e.tensor_copy(b["CBC"][:, h:h + 1],
                                                 b["PSC"][:, 0:1])),
                   deps=[t5])
        rowtok[0] = rowtok[1] = [cb]
        cbc_toks[h] = cb
        acc.append(cb)
    prep = joined(acc, [cbc_toks[NH - 1]])
    pend = None
    pt_free = {0: [], 1: []}

    def emit_pv(prevp):
        ex, kts, pp, h, ov, nkt, ofree = prevp
        last = None
        for i, kt in enumerate(kts):
            vcol = 65 * (vblk(h) * 16 + kt)
            pt = ptst[:, pp * 1024 + i * 512: pp * 1024 + i * 512 + 512]
            last = P.add("tensor",
                         (lambda e, kt=kt, nkt=nkt, vcol=vcol, ov=ov, pt=pt:
                          e.matmul(ov[0:65, :], vt_buf[:, vcol:vcol + 65],
                                   pt, start=(kt == 0),
                                   stop=(kt == nkt - 1))),
                         deps=[ex] + (ofree if kt == 0 else []), fifo=True)
        pt_free[pp] = [last]
        return last

    norm_state = {"prev": []}

    def emit_norm(pend):
        ov, oi, qt, qp, ci, _pv, _nf = pend
        rc = P.add("vector",
                   (lambda e, ov=ov: e.reciprocal(rcrow, ov[64:65, :])),
                   deps=[pend[5]] + pend[6] + norm_state["prev"])
        bc = P.add("tensor",
                   lambda e: e.matmul(b["PSC"][0:64, 0:512],
                                      b["ONESF"][0:1, 0:64], rcrow,
                                      start=True, stop=True),
                   deps=[rc])
        cbe = P.add("vector",
                    lambda e: e.tensor_copy(b["F32A"][0:64, 0:512],
                                            b["PSC"][0:64, 0:512]),
                    deps=[bc] + norm_state["prev"])
        oc0 = qt * S + ci * CH
        fin = P.add("vector",
                    (lambda e, ov=ov, oc0=oc0, qp=qp:
                     e.tensor_tensor(
                         o_buf[64 * qp:64 * qp + 64, oc0:oc0 + CH],
                         ov[0:64, :], b["F32A"][0:64, 0:512], ALU.mult)),
                    deps=[cbe])
        orot.done(oi, fin)
        out.append(fin)
        norm_state["prev"] = [rc, fin]
        return [cbe, fin]

    nfree = []
    for h in range(NH):
        qt, qp = qslot(h)
        kt_, kp_ = kslot(h)
        qsl = q_buf[64 * qp:64 * qp + 64, qt * S:(qt + 1) * S]
        ksl = k_buf[64 * kp_:64 * kp_ + 64, kt_ * S:(kt_ + 1) * S]
        for ci in range(NCH):
            nkt = (4 * ci + 4) if causal else 16
            oi, ov, ofree = orot.next()
            pv_last = None
            prevp = None
            for p in range(nkt // 2):
                kts = (2 * p, 2 * p + 1)
                mdep = []
                svs = []
                s0 = None
                for kt in kts:
                    si, sv, sfree = srot.next()
                    if s0 is None:
                        s0 = (si, sv)
                    svs.append(si)

                    def gmm(e, sv=sv, kt=kt, ci=ci, qsl=qsl, ksl=ksl):
                        return e.matmul(sv, ksl[:, kt * 128:kt * 128 + 128],
                                        qsl[:, ci * CH:ci * CH + CH],
                                        start=True, stop=True)
                    mm = P.add("tensor", gmm, deps=sfree + list(deps) + prep,
                               fifo=True)
                    if causal and kt >= 4 * ci:
                        r = kt - 4 * ci
                        mk = P.add("vector",
                                   (lambda e, sv=sv, r=r:
                                    e.tensor_tensor(
                                        sv, sv,
                                        cmskt[:, r * 512:r * 512 + 512],
                                        ALU.add)),
                                   deps=[mm])
                        mdep.append(mk)
                    else:
                        mdep.append(mm)
                pp = p % 2
                pt2 = ptst[:, pp * 1024:pp * 1024 + 1024]
                sv2 = b["PSA"][:, s0[0] * 512:s0[0] * 512 + 1024]
                ex = P.add("scalar",
                           (lambda e, sv2=sv2, pt2=pt2, h=h:
                            e.activation(pt2, sv2, AF.Exp, scale=0.125,
                                         bias=b["CBC"][:, h:h + 1])),
                           deps=mdep + pt_free[pp] + prep)
                for si in svs:
                    srot.done(si, ex)
                if prevp is not None:
                    pv_last = emit_pv(prevp)
                prevp = (ex, kts, pp, h, ov, nkt, ofree)
            pv_last = emit_pv(prevp)
            this = (ov, oi, qt, qp, ci, pv_last, nfree)
            if pend is not None:
                nfree = emit_norm(pend)
            pend = this
    if pend is not None:
        emit_norm(pend)
    return out



def emit_gate8(P, b, f_buf, gw_view, gsc, cmb_view, combt_view, deps,
               scale=0.5):
    out = []
    L = gsc[:, 0:8]
    T = gsc[:, 8:16]
    E = gsc[:, 16:24]
    m1 = b["COL"][:, 1:2]
    nm1 = b["COL"][:, 2:3]
    m2 = b["COL"][:, 3:4]
    den = b["COL"][:, 4:5]
    rc = b["COL"][:, 5:6]
    prev = []
    for tc in range(16):
        def gmm(e, tc=tc):
            last = None
            for j in range(NT):
                last = e.matmul(b["PSA"][:, 1024:1040][:, 0:8],
                                f_buf[:, j * S + tc * 128:j * S + tc * 128 + 128],
                                gw_view[:, j * 8:(j + 1) * 8],
                                start=(j == 0), stop=(j == NT - 1))
            return last
        mm = P.add("tensor", gmm, deps=list(deps) + prev, fifo=True)
        t0 = P.add("vector",
                   lambda e: e.tensor_copy(L, b["PSA"][:, 1024:1032]),
                   deps=[mm])
        t1 = P.add("vector", lambda e: e.tensor_reduce(m1, L, AX.X, ALU.max),
                   deps=[t0])
        t1b = P.add("vector",
                    lambda e: e.tensor_scalar(nm1, m1, -1.0, None, ALU.mult),
                    deps=[t1])
        t2 = P.add("vector",
                   lambda e: e.tensor_scalar(T, L, m1[0:128, 0:1], None,
                                             ALU.is_equal),
                   deps=[t1])
        t3 = P.add("vector",
                   lambda e: e.scalar_tensor_tensor(T, T, -1e30, L, ALU.mult,
                                                    ALU.add),
                   deps=[t2])
        t4 = P.add("vector", lambda e: e.tensor_reduce(m2, T, AX.X, ALU.max),
                   deps=[t3])
        t5 = P.add("vector",
                   lambda e: e.tensor_scalar(T, L, m2[0:128, 0:1], None,
                                             ALU.is_ge),
                   deps=[t4])
        t6 = P.add("scalar",
                   lambda e: e.activation(E, L, AF.Exp, bias=nm1[0:128, 0:1]),
                   deps=[t1b, t0])
        t7 = P.add("vector", lambda e: e.tensor_tensor(E, E, T, ALU.mult),
                   deps=[t5, t6])
        t8 = P.add("vector", lambda e: e.tensor_reduce(den, E, AX.X, ALU.add),
                   deps=[t7])
        t9 = P.add("vector", lambda e: e.reciprocal(rc, den), deps=[t8])
        t10 = P.add("vector",
                    (lambda e: e.tensor_scalar(cmb_view[:, 0:8], E,
                                               rc[0:128, 0:1], float(scale),
                                               ALU.mult, ALU.mult)),
                    deps=[t9])
        tr = P.add("tensor",
                   (lambda e: e.transpose(b["PTP"][0:8, 0:128],
                                          cmb_view[:, 0:8], b["IDN"][:, :])),
                   deps=[t10] + prev)
        ev = P.add("vector",
                   (lambda e, tc=tc:
                    e.tensor_copy(combt_view[0:8, tc * 128:tc * 128 + 128],
                                  b["PTP"][0:8, 0:128])),
                   deps=[tr])
        prev = [ev]
        out.append(ev)
    return out


def emit_gate16(P, b, f_buf, gw_view, gbc_view, gsc, cmb_view, combt_view,
                deps, route_scale=2.5, sub_scale=0.5):
    out = []
    L = gsc[:, 0:16]
    Bv = gsc[:, 16:32]
    T = gsc[:, 32:48]
    E = gsc[:, 48:64]
    mx = b["COL"][:, 1:2]
    den = b["COL"][:, 4:5]
    rc = b["COL"][:, 5:6]
    prev = []
    for tc in range(16):
        def gmm(e, tc=tc):
            last = None
            for j in range(NT):
                last = e.matmul(b["PSA"][:, 1024:1040],
                                f_buf[:, j * S + tc * 128:j * S + tc * 128 + 128],
                                gw_view[:, j * 16:(j + 1) * 16],
                                start=(j == 0), stop=(j == NT - 1))
            return last
        mm = P.add("tensor", gmm, deps=list(deps) + prev, fifo=True)
        t0 = P.add("scalar",
                   lambda e: e.activation(L, b["PSA"][:, 1024:1040], AF.Sigmoid),
                   deps=[mm])
        t1 = P.add("vector", lambda e: e.tensor_tensor(Bv, L, gbc_view, ALU.add),
                   deps=[t0])
        tsel = t1
        for r in range(4):
            ta = P.add("vector",
                       lambda e: e.tensor_reduce(mx, Bv, AX.X, ALU.max),
                       deps=[tsel])
            if r < 3:
                tb = P.add("vector",
                           (lambda e:
                            e.tensor_scalar(T, Bv, mx[0:128, 0:1], None,
                                            ALU.is_equal)),
                           deps=[ta])
                tsel = P.add("vector",
                             (lambda e:
                              e.scalar_tensor_tensor(Bv, T, -1e30, Bv,
                                                     ALU.mult, ALU.add)),
                             deps=[tb])
            else:
                tsel = ta
        t2 = P.add("vector", lambda e: e.tensor_tensor(Bv, L, gbc_view, ALU.add),
                   deps=[tsel])
        t3 = P.add("vector",
                   lambda e: e.tensor_scalar(T, Bv, mx[0:128, 0:1], None,
                                             ALU.is_ge),
                   deps=[t2])
        t4 = P.add("vector", lambda e: e.tensor_tensor(E, L, T, ALU.mult),
                   deps=[t3])
        t5 = P.add("vector", lambda e: e.tensor_reduce(den, E, AX.X, ALU.add),
                   deps=[t4])
        t6 = P.add("vector", lambda e: e.reciprocal(rc, den), deps=[t5])
        t7 = P.add("vector",
                   (lambda e:
                    e.tensor_scalar(cmb_view[:, 0:16], E, rc[0:128, 0:1],
                                    float(route_scale * sub_scale),
                                    ALU.mult, ALU.mult)),
                   deps=[t6])
        tr = P.add("tensor",
                   (lambda e: e.transpose(b["PTP"][0:16, 0:128],
                                          cmb_view[:, 0:16], b["IDN"][:, :])),
                   deps=[t7] + prev)
        ev = P.add("vector",
                   (lambda e, tc=tc:
                    e.tensor_copy(combt_view[0:16, tc * 128:tc * 128 + 128],
                                  b["PTP"][0:16, 0:128])),
                   deps=[tr])
        prev = [ev]
        out.append(ev)
    return out


def emit_comb_bc(P, b, esel_view, combt_view, nexp, e_idx, cbe4_view, deps,
                 ps_view=None):
    pv = b["PSD"][:, 0:512] if ps_view is None else ps_view
    toks = []
    for ci in range(NCH):
        mm = P.add("tensor",
                   (lambda e, ci=ci:
                    e.matmul(pv,
                             esel_view[0:nexp, e_idx * 128:e_idx * 128 + 128],
                             combt_view[0:nexp, ci * CH:ci * CH + CH],
                             start=True, stop=True)),
                   deps=list(deps) + toks[-1:])
        ev = P.add("scalar",
                   (lambda e, ci=ci:
                    e.activation(cbe4_view[:, ci * CH:ci * CH + CH],
                                 pv, AF.Copy)),
                   deps=[mm])
        toks.append(ev)
    return toks


def emit_moe8(P, b, f_buf, expw_ap, esel_view, combt_view, cbe4_view, wring,
              deps, gate_deps, fe_buf=None, wdeps=()):
    out = []
    orot = PsumRot([b["PSB"][:, 0:512], b["PSC"][:, 0:512]])
    t1 = b["F32A"][:, 512:1024]
    cb_tok = []
    prev_t1 = []
    cbe4_readers = []
    for e_ in range(8):
        cb_tok = emit_comb_bc(P, b, esel_view, combt_view, 8, e_, cbe4_view,
                              joined(gate_deps, cb_tok, cbe4_readers))
        cbe4_readers = []
        for mo in range(NT):
            wsl, wi, wtok = wring.load(expw_ap[e_][:, mo * 128:(mo + 1) * 128],
                                       NT, deps=wdeps)
            last = None
            for ci in range(NCH):
                si, pv, sfree = orot.next()

                def gmm(e, wsl=wsl, pv=pv, ci=ci):
                    r = None
                    for kt in range(NT):
                        r = e.matmul(pv, wsl[:, kt * 128:(kt + 1) * 128],
                                     f_buf[:, kt * S + ci * CH:
                                           kt * S + ci * CH + CH],
                                     start=(kt == 0), stop=(kt == NT - 1))
                    return r
                mm = P.add("tensor", gmm,
                           deps=[wtok] + list(deps) + sfree, fifo=True)
                tta = P.add("vector",
                            (lambda e, pv=pv, ci=ci:
                             e.tensor_tensor(t1, pv,
                                             cbe4_view[:, ci * CH:ci * CH + CH],
                                             ALU.mult)),
                            deps=[mm] + prev_t1 + cb_tok)
                ttb = P.add("vector",
                            (lambda e, mo=mo, ci=ci:
                             e.tensor_tensor(
                                 b["X"][:, mo * S + ci * CH:mo * S + ci * CH + CH],
                                 b["X"][:, mo * S + ci * CH:mo * S + ci * CH + CH],
                                 t1, ALU.add)),
                            deps=[tta])
                prev_t1 = [ttb]
                cbe4_readers = [tta]
                orot.done(si, tta)
                last = ttb
                out.append(ttb)
            wring.used(wi, last)
    return out


def emit_moe16(P, b, y_buf, w13_ap, w2_ap, w13s_ap, w2s_ap, esel_view,
               combt_view, cbe4_view, sg_view, h_buf, wring, w2ring, deps,
               gate_deps):
    out = []
    t1 = b["F32A"][:, 512:1024]
    grot = PsumRot([b["PSB"][:, 0:512], b["PSC"][:, 0:512],
                    b["PSD"][:, 0:512], b["PSA"][:, 1536:2048]])
    orot = PsumRot([b["PSA"][:, 0:512], b["PSA"][:, 512:1024]])
    prev_t1 = []
    prev_sg = {0: [], 1: []}
    cbe4_readers = []
    groups = [(2 * g, 2 * g + 1) for g in range(8)] + [(16,)]
    for grp in groups:
        for slot, e_ in enumerate(grp):
            if e_ < 16:
                cb = emit_comb_bc(P, b, esel_view, combt_view, 16, e_,
                                  cbe4_view,
                                  joined(gate_deps, cbe4_readers),
                                  ps_view=b["PSA"][:, 1024:1536])
            else:
                cb = [P.add("vector",
                            lambda e: e.memset(cbe4_view[:, 0:2048], 0.5),
                            deps=joined(gate_deps, cbe4_readers))]
            cbe4_readers = []
            w13 = w13_ap[e_] if e_ < 16 else w13s_ap
            for mg in range(4):
                wg, wgi, wgt = wring.load(w13[:, mg * 128:(mg + 1) * 128], NT)
                wu, wui, wut = wring.load(
                    w13[:, (mg + 4) * 128:(mg + 5) * 128], NT)
                lastg = lastu = None
                for ci in range(NCH):
                    gi, gv, gfree = grot.next()
                    ui, uv, ufree = grot.next()

                    def gmm(e, wsl=wg, pv=gv, ci=ci):
                        r = None
                        for kt in range(NT):
                            r = e.matmul(pv, wsl[:, kt * 128:(kt + 1) * 128],
                                         y_buf[:, kt * S + ci * CH:
                                               kt * S + ci * CH + CH],
                                         start=(kt == 0), stop=(kt == NT - 1))
                        return r
                    mmg = P.add("tensor", gmm, deps=[wgt] + list(deps) + gfree,
                                fifo=True)

                    def umm(e, wsl=wu, pv=uv, ci=ci):
                        r = None
                        for kt in range(NT):
                            r = e.matmul(pv, wsl[:, kt * 128:(kt + 1) * 128],
                                         y_buf[:, kt * S + ci * CH:
                                               kt * S + ci * CH + CH],
                                         start=(kt == 0), stop=(kt == NT - 1))
                        return r
                    mmu = P.add("tensor", umm, deps=[wut] + list(deps) + ufree,
                                fifo=True)
                    sgv = sg_view[:, (ci % 2) * 512:(ci % 2) * 512 + 512]
                    sg = P.add("scalar",
                               (lambda e, gv=gv, sgv=sgv:
                                e.activation(sgv, gv, AF.Silu)),
                               deps=[mmg] + prev_sg[ci % 2])
                    up = P.add("vector",
                               (lambda e, uv=uv, ci=ci:
                                e.tensor_tensor(
                                    t1, uv,
                                    cbe4_view[:, ci * CH:ci * CH + CH],
                                    ALU.mult)),
                               deps=[mmu] + prev_t1 + cb)
                    hc0 = slot * (4 * S) + mg * S + ci * CH
                    hh = P.add("vector",
                               (lambda e, hc0=hc0, sgv=sgv:
                                e.tensor_tensor(h_buf[:, hc0:hc0 + CH], t1,
                                                sgv, ALU.mult)),
                               deps=[up, sg])
                    prev_t1 = [hh]
                    prev_sg[ci % 2] = [hh]
                    cbe4_readers.append(up)
                    grot.done(gi, sg)
                    grot.done(ui, up)
                    lastg, lastu = hh, hh
                wring.used(wgi, lastg)
                wring.used(wui, lastu)
        w2s = []
        for slot, e_ in enumerate(grp):
            w2s.append(w2_ap[e_] if e_ < 16 else w2s_ap)
        for mo in range(NT):
            wts = []
            for slot, e_ in enumerate(grp):
                wsl, wi, wtok = w2ring.load(w2s[slot][:, mo * 128:(mo + 1) * 128],
                                            4)
                wts.append((wsl, wi, wtok))
            last = None
            for ci in range(NCH):
                oi, pv, ofree = orot.next()

                def gmm(e, pv=pv, ci=ci, wts=wts, grp=grp):
                    r = None
                    n = len(grp) * 4
                    i = 0
                    for slot in range(len(grp)):
                        wsl = wts[slot][0]
                        for kt in range(4):
                            hc0 = slot * (4 * S) + kt * S + ci * CH
                            r = e.matmul(pv, wsl[:, kt * 128:(kt + 1) * 128],
                                         h_buf[:, hc0:hc0 + CH],
                                         start=(i == 0), stop=(i == n - 1))
                            i += 1
                    return r
                hdeps = [prev_t1[0]] if prev_t1 else []
                mm = P.add("tensor", gmm,
                           deps=[w[2] for w in wts] + hdeps + ofree, fifo=True)
                ev = P.add("vector",
                           (lambda e, mo=mo, ci=ci, pv=pv:
                            e.tensor_tensor(
                                b["X"][:, mo * S + ci * CH:mo * S + ci * CH + CH],
                                b["X"][:, mo * S + ci * CH:mo * S + ci * CH + CH],
                                pv, ALU.add)),
                           deps=[mm])
                orot.done(oi, ev)
                last = ev
                out.append(ev)
            for wsl, wi, wtok in wts:
                w2ring.used(wi, last)
        prev_t1 = [last]
        prev_sg = {0: [last], 1: [last]}
    return out



def emit_rwkv(P, b, y_buf, wk_ap, wv_ap, wr_ap, td_view, r_buf, rs_buf,
              wring, wbc_view, deps):
    out = []
    wcol = b["COL"][:, 8:16]
    t0 = P.add("scalar", lambda e: e.activation(wcol, td_view, AF.Exp),
               deps=list(deps))
    t1 = P.add("vector",
               lambda e: e.tensor_scalar(wcol, wcol, -1.0, None, ALU.mult),
               deps=[t0])
    lastc = b["COL"][:, 6:7]
    rs_toks = []
    for j in range(NT):
        wsl, wi, wtok = wring.load(wk_ap[:, j * 128:(j + 1) * 128], NT)

        def gk(e, wsl=wsl):
            r = None
            for ci in range(NCH):
                for kt in range(NT):
                    r = e.matmul(b["PSA"][:, ci * CH:ci * CH + CH],
                                 wsl[:, kt * 128:(kt + 1) * 128],
                                 y_buf[:, kt * S + ci * CH:kt * S + ci * CH + CH],
                                 start=(kt == 0), stop=(kt == NT - 1))
            return r
        kdep = [rs_toks[-1]] if rs_toks else []
        kmm = P.add("tensor", gk, deps=[wtok] + list(deps) + kdep, fifo=True)
        wring.used(wi, kmm)
        wb0 = P.add("vector", lambda e: e.memset(wbc_view, 1.0),
                    deps=kdep)
        wb1 = P.add("vector",
                    (lambda e, j=j:
                     e.tensor_scalar(wbc_view, wbc_view, wcol[:, j:j + 1],
                                     None, ALU.mult)),
                    deps=[wb0, t1])
        wsl2, wi2, wtok2 = wring.load(wr_ap[:, j * 128:(j + 1) * 128], NT)
        rtoks = []
        for ci in range(NCH):
            def gr(e, wsl2=wsl2, ci=ci):
                r = None
                for kt in range(NT):
                    r = e.matmul(b["PSD"][:, 0:512],
                                 wsl2[:, kt * 128:(kt + 1) * 128],
                                 y_buf[:, kt * S + ci * CH:kt * S + ci * CH + CH],
                                 start=(kt == 0), stop=(kt == NT - 1))
                return r
            rdep = rtoks[-1:] if rtoks else (rs_toks[-1:] if rs_toks else [])
            rmm = P.add("tensor", gr, deps=[wtok2] + list(deps) + rdep,
                        fifo=True)
            rt = P.add("scalar",
                       (lambda e, j=j, ci=ci:
                        e.activation(r_buf[:, j * S + ci * CH:j * S + ci * CH + CH],
                                     b["PSD"][:, 0:512], AF.Sigmoid)),
                       deps=[rmm])
            rtoks.append(rt)
        wring.used(wi2, rtoks[-1])
        sc_prev = None
        for ci in range(NCH):
            sv = b["PSB"] if ci % 2 == 0 else b["PSC"]
            init = 0.0 if ci == 0 else lastc[0:128, 0:1]
            sc = P.add("vector",
                       (lambda e, sv=sv, ci=ci, init=init:
                        e.tensor_tensor_scan(sv[:, 0:512], wbc_view,
                                             b["PSA"][:, ci * CH:ci * CH + CH],
                                             init, ALU.mult, ALU.add)),
                       deps=[kmm, wb1] + ([sc_prev] if sc_prev else [])
                       + (rs_toks[-2:] if rs_toks else []))
            cp = P.add("vector",
                       (lambda e, sv=sv:
                        e.tensor_copy(lastc, sv[:, 511:512])),
                       deps=[sc])
            rs = P.add("vector",
                       (lambda e, sv=sv, j=j, ci=ci:
                        e.tensor_tensor(
                            rs_buf[:, j * S + ci * CH:j * S + ci * CH + CH],
                            sv[:, 0:512],
                            r_buf[:, j * S + ci * CH:j * S + ci * CH + CH],
                            ALU.mult)),
                       deps=[sc, rtoks[ci], cp])
            sc_prev = cp
            rs_toks.append(rs)
    orot = PsumRot([b["PSA"][:, i * 512:(i + 1) * 512] for i in range(4)])
    for mo in range(NT):
        wsl, wi, wtok = wring.load(wv_ap[:, mo * 128:(mo + 1) * 128], NT)
        last = None
        for ci in range(NCH):
            oi, pv, ofree = orot.next()

            def gmm(e, wsl=wsl, pv=pv, ci=ci):
                r = None
                for kt in range(NT):
                    r = e.matmul(pv, wsl[:, kt * 128:(kt + 1) * 128],
                                 rs_buf[:, kt * S + ci * CH:kt * S + ci * CH + CH],
                                 start=(kt == 0), stop=(kt == NT - 1))
                return r
            mm = P.add("tensor", gmm,
                       deps=[wtok] + rs_toks[-1:] + ofree, fifo=True)
            ev = P.add("vector",
                       (lambda e, mo=mo, ci=ci, pv=pv:
                        e.tensor_tensor(
                            b["X"][:, mo * S + ci * CH:mo * S + ci * CH + CH],
                            b["X"][:, mo * S + ci * CH:mo * S + ci * CH + CH],
                            pv, ALU.add)),
                       deps=[mm])
            orot.done(oi, ev)
            last = ev
            out.append(ev)
        wring.used(wi, last)
    return out



def emit_fold(P, b, w_dram, rtb_view, fst_view, scratch_ap, itiles, wring,
              deps):
    toks = []
    for i in range(itiles):
        wsl, wi, wtok = wring.load(w_dram[:, i * 128:(i + 1) * 128], NT)
        evs = []
        for n in range(2):
            ps = b["PSB"] if n == 0 else b["PSC"]

            def gmm(e, wsl=wsl, n=n, ps=ps):
                r = None
                for kt in range(NT):
                    r = e.matmul(ps[:, 0:512],
                                 wsl[:, kt * 128:(kt + 1) * 128],
                                 rtb_view[:, kt * 1024 + n * 512:
                                          kt * 1024 + n * 512 + 512],
                                 start=(kt == 0), stop=(kt == NT - 1))
                return r
            mm = P.add("tensor", gmm, deps=[wtok] + list(deps) + toks[-1:]
                       + evs[-2:-1], fifo=True)
            ev = P.add("scalar",
                       (lambda e, n=n, ps=ps:
                        e.activation(fst_view[:, n * 512:n * 512 + 512],
                                     ps[:, 0:512], AF.Copy)),
                       deps=[mm])
            evs.append(ev)
        st = P.add("sync",
                   (lambda e, i=i:
                    e.dma_start(scratch_ap[i * 128:(i + 1) * 128, :],
                                fst_view[:, 0:1024])),
                   deps=evs, ctr=P.ctr("fsd"), n=16)
        toks.append(st)
        wring.used(wi, st)
    return toks



def emit_sl_l0ffn(P, b, dr, pfx, deps_in):
    def qslot(h):
        return ((h % 4) + 4 * (h // 8), (h // 4) % 2)

    def kslot(h):
        return ((h // 4) // 2, (h // 4) % 2)

    def vblk(h):
        return h // 4

    def pair_heads(mi):
        return (2 * mi, 2 * mi + 1)

    ln = emit_layernorm(P, b, b["A4"], b["A1"][:, 0:2048], deps_in)
    rm1 = emit_rmsnorm(P, b, b["A4"], b["A1"],
                       lambda j: b["NORMW"][:, j:j + 1],
                       b["A3"][:, 12352:13376], joined(ln, deps_in))
    gw = b["A3"][:, 13888:13952]
    gwt = P.add("gpsimd",
                (lambda e: e.dma_start(
                    gw.rearrange("p (j m) -> p j m", j=NT),
                    dr[pfx + "gwT"].rearrange("(j p) m -> p j m", p=128))),
                deps=deps_in, ctr=P.ctr("gwd"), n=16)
    wring = WRing(P, b["A3"], 8256, 4, 1024, tag="wA")
    pr = PsumRot([b["PSB"][:, 0:512], b["PSC"][:, 0:512], b["PSD"][:, 0:512]])
    ktoks = emit_proj(P, b, wring, dr[pfx + "wkT"], b["A1"], C, 256, pr,
                      mk_evac_to(P, b["A3"], "scalar"), deps_src=rm1,
                      wdeps=deps_in)
    vt1 = P.add("vector", lambda e: e.memset(b["A3"][:, 4096:8256], 1.0),
                deps=joined(rm1, deps_in))
    vtoks = emit_proj(P, b, wring, dr[pfx + "wvT"], b["A1"], C, 256, pr,
                      mk_evac_vtok(P, b, b["A3"][:, 4096:8256], pair_heads,
                                   b["A3"][:, 13376:13888]),
                      deps_src=joined(rm1, [vt1]), wdeps=deps_in)
    qtoks = emit_proj(P, b, wring, dr[pfx + "wqT"], b["A1"], C, C, pr,
                      mk_evac_to(P, b["A2"], "scalar"), deps_src=rm1,
                      wdeps=deps_in)
    a_dead = joined(ktoks, vtoks, qtoks)
    cmskt = b["A1"][:, 4096:6144]
    cmt = P.add("gpsimd", lambda e: e.dma_start(cmskt, dr["c_cmskt"]),
                deps=a_dead, ctr=P.ctr("cmd"), n=16)
    at = emit_attn(P, b, b["A2"], b["A3"], b["A3"][:, 4096:8256], b["A2"],
                   qslot, kslot, vblk, True, joined(a_dead, [cmt]),
                   b["A1"][:, 0:4096], cmskt=cmskt)
    wring2 = WRing(P, b["A3"], 8256, 4, 1024, tag="wA")

    def evac_h(mi, ci, pv, deps):
        c0 = mi * S + ci * CH
        e1 = P.add("vector",
                   (lambda e, c0=c0, pv=pv:
                    e.tensor_tensor(b["A4"][:, c0:c0 + CH],
                                    b["A4"][:, c0:c0 + CH], pv, ALU.add)),
                   deps=deps)
        e2 = P.add("vector",
                   (lambda e, c0=c0:
                    e.scalar_tensor_tensor(b["X"][:, c0:c0 + CH],
                                           b["A4"][:, c0:c0 + CH], 0.5,
                                           b["X"][:, c0:c0 + CH],
                                           ALU.mult, ALU.add)),
                   deps=[e1])
        return e2
    pr2 = PsumRot([b["PSB"][:, 0:512], b["PSC"][:, 0:512], b["PSD"][:, 0:512]])
    htoks = emit_proj(P, b, wring2, dr[pfx + "woT"], b["A2"], C, C, pr2,
                      evac_h, deps_src=at, wdeps=joined(at))
    rm2 = emit_rmsnorm(P, b, b["A4"], b["A2"],
                       lambda j: b["NORMW"][:, 8 + j:9 + j],
                       b["A1"][:, 0:1024], joined(htoks))
    esel = b["A1"][:, 8448:10496]
    cbe4 = b["A1"][:, 10496:12544]
    combt = b["A1"][:, 6400:8448]
    cmb = b["A1"][:, 6144:6160]
    est = P.add("gpsimd", lambda e: e.dma_start(esel[0:16, :], dr["c_esel"]),
                deps=joined(htoks), ctr=P.ctr("esd"), n=16)
    g8 = emit_gate8(P, b, b["A2"], gw, b["F32A"][:, 0:64], cmb, combt,
                    joined(rm2, [gwt]))
    wring3 = WRing(P, b["A1"], 0, 4, 1024, tag="wM")
    m8 = emit_moe8(P, b, b["A2"], dr[pfx + "expwT"], esel, combt, cbe4,
                   wring3, joined(rm2), joined(g8, [est]),
                   fe_buf=b["A3"], wdeps=joined(rm2))
    return joined(m8, htoks)


def emit_sl_l0attn(P, b, dr, scr_q, scr_kb, deps_in):
    def slot(h):
        return (h // 2, h % 2)

    def vblk(h):
        return h

    def pair_heads(mi):
        return (2 * mi, 2 * mi + 1)

    ln = emit_layernorm(P, b, b["A4"], b["A3"][:, 0:2048], deps_in)
    wring = WRing(P, b["C8B"], 0, 2, 1024, tag="wC")
    pr = PsumRot([b["PSB"][:, 0:512], b["PSC"][:, 0:512], b["PSD"][:, 0:512]])
    t2t = emit_proj(P, b, wring, dr["l0a_wvaT"], b["A4"], C, 256, pr,
                    mk_evac_to(P, b["A2"], "scalar"), deps_src=ln,
                    wdeps=deps_in)
    vt1 = P.add("vector", lambda e: e.memset(b["A1"][:, 0:A1COLS], 1.0),
                deps=joined(ln, deps_in))
    vtt = emit_proj(P, b, wring, dr["l0a_wvbT"], b["A2"], 256, C, pr,
                    mk_evac_vtok(P, b, b["A1"], pair_heads,
                                 b["A2"][:, 8192:8704]),
                    deps_src=joined(t2t, [vt1]), wdeps=deps_in)
    t1t = emit_proj(P, b, wring, dr["l0a_wkaT"], b["A4"], C, 256, pr,
                    mk_evac_to(P, b["A2"], "scalar", dst_col0=4096),
                    deps_src=ln, wdeps=deps_in)
    ktt = emit_proj(P, b, wring, scr_kb, b["A2"], 256, C, pr,
                    mk_evac_to(P, b["A3"], "scalar"), deps_src=t1t,
                    src_col0=4096, wdeps=joined(ln, deps_in))
    qtt = emit_proj(P, b, wring, scr_q, b["A4"], C, C, pr,
                    mk_evac_to(P, b["A2"], "scalar"),
                    deps_src=joined(ln, ktt, vtt), wdeps=deps_in)
    at = emit_attn(P, b, b["A2"], b["A3"], b["A1"], b["A2"],
                   slot, slot, vblk, False, joined(qtt, ktt, vtt),
                   b["A4"][:, 0:4096])
    wring2 = WRing(P, b["C8B"], 0, 2, 1024, tag="wC")
    pr2 = PsumRot([b["PSB"][:, 0:512], b["PSC"][:, 0:512], b["PSD"][:, 0:512]])
    ot = emit_proj(P, b, wring2, dr["l0a_woT"], b["A2"], C, C, pr2,
                   mk_evac_resid(P, b), deps_src=at, wdeps=joined(at))
    return joined(ot)


def emit_sl_l1ffn(P, b, dr, gbcb, deps_in):
    esel = b["A1"][:, 6144:8192]
    combt = b["A1"][:, 8192:10240]
    cbe4 = b["A1"][:, 10240:12288]
    sg = b["A1"][:, 12288:13312]
    gw = b["A1"][:, 13312:13440]
    cmb = b["A1"][:, 13440:13456]
    ln = emit_layernorm(P, b, b["A4"], b["A2"][:, 0:2048], deps_in)
    est = P.add("gpsimd", lambda e: e.dma_start(esel[0:16, :], dr["c_esel"]),
                deps=deps_in, ctr=P.ctr("esd"), n=16)
    gwt = P.add("gpsimd",
                (lambda e: e.dma_start(
                    gw.rearrange("p (j m) -> p j m", j=NT),
                    dr["l1f_gwT"].rearrange("(j p) m -> p j m", p=128))),
                deps=deps_in, ctr=P.ctr("gwd"), n=16)
    g16 = emit_gate16(P, b, b["A4"], gw, gbcb[:, 0:16], b["F32A"][:, 0:64],
                      cmb, combt, joined(ln, [gwt, est]))
    wring = WRing(P, b["A1"], 0, 4, 1024, tag="wA")
    w2ring = WRing(P, b["A1"], 4096, 4, 512, tag="wB")
    m16 = emit_moe16(P, b, b["A4"], dr["l1f_w13T"], dr["l1f_w2T"],
                     dr["l1f_w13sT"], dr["l1f_w2sT"], esel, combt, cbe4, sg,
                     b["A2"], wring, w2ring, joined(ln, deps_in), joined(g16))
    return joined(m16)


def emit_sl_l1attn(P, b, dr, deps_in):
    ln = emit_layernorm(P, b, b["A4"], b["A2"][:, 0:2048], deps_in)
    wring = WRing(P, b["A1"], 0, 4, 1024, tag="wA")
    rw = emit_rwkv(P, b, b["A4"], dr["l1a_wkT"], dr["l1a_wvT"],
                   dr["l1a_wrT"], b["NORMW"][:, 16:24], b["A3"], b["A2"],
                   wring, b["F32A"][:, 0:512], joined(ln, deps_in))
    return joined(rw)



W_SHAPES = {
    "l0f_wqT": (C, C), "l0f_wkT": (C, 256), "l0f_wvT": (C, 256),
    "l0f_woT": (C, C), "l0f_gwT": (C, 8), "l0f_expwT": (8, C, C),
    "l0f_nw": (128, 16),
    "l0a_wq": (C, C), "l0a_wkaT": (C, 256), "l0a_wkb": (C, 256),
    "l0a_wvaT": (C, 256), "l0a_wvbT": (256, C), "l0a_woT": (C, C),
    "l1f_gwT": (C, 16), "l1f_gbc": (128, 16), "l1f_w13T": (16, C, C),
    "l1f_w2T": (16, 512, C), "l1f_w13sT": (C, C), "l1f_w2sT": (512, C),
    "l1a_wkT": (C, C), "l1a_wvT": (C, C), "l1a_wrT": (C, C),
    "l1a_td": (128, 8),
}
CONST_NAMES = ("c_idn", "c_onesc", "c_onesf", "c_onescf", "c_cmskt",
               "c_esel", "c_rt")


def _declare(P, cs, kinds):
    nc = P.nc
    dr = {}
    dr["x"] = nc.dram_tensor("x", [BPC, C, S], F32, kind="ExternalInput").ap()
    dr["xo"] = nc.dram_tensor("xo", [BPC, C, S], F32,
                              kind="ExternalOutput").ap()
    names = ["x"]
    for nm, shp in W_SHAPES.items():
        if nm.split("_")[0] in kinds:
            dr[nm] = nc.dram_tensor(nm, list(shp), F32,
                                    kind="ExternalInput").ap()
            names.append(nm)
    need_c = {"l0f": ("c_idn", "c_onesc", "c_onesf", "c_onescf", "c_cmskt",
                      "c_esel"),
              "l0a": ("c_idn", "c_onesc", "c_onesf", "c_onescf", "c_rt"),
              "l1f": ("c_idn", "c_onesc", "c_onesf", "c_onescf", "c_esel"),
              "l1a": ("c_idn", "c_onesc", "c_onesf", "c_onescf")}
    cset = []
    for k in kinds:
        cset += list(need_c[k])
    for nm in CONST_NAMES:
        if nm in cset:
            dr[nm] = nc.dram_tensor(nm, list(cs[nm].shape),
                                    BF16 if cs[nm].dtype == BF16NP else F32,
                                    kind="ExternalInput").ap()
            names.append(nm)
    return dr, names


def _prologues(P, b, dr, kinds):
    st = {}
    toks = []
    if "l0f" in kinds:
        t = P.add("gpsimd",
                  lambda e: e.dma_start(b["NORMW"][:, 0:16], dr["l0f_nw"]),
                  ctr=P.ctr("nwd"), n=16)
        toks.append(t)
    if "l1a" in kinds:
        t = P.add("gpsimd",
                  lambda e: e.dma_start(b["NORMW"][:, 16:24], dr["l1a_td"]),
                  ctr=P.ctr("tdd"), n=16)
        toks.append(t)
    if "l1f" in kinds:
        st["gbcb"] = P.sbuf("GBC", [128, 16], F32)
        t = P.add("gpsimd",
                  lambda e: e.dma_start(st["gbcb"][:, :], dr["l1f_gbc"]),
                  ctr=P.ctr("gbd"), n=16)
        toks.append(t)
    if "l0a" in kinds:
        nc = P.nc
        st["scr_q"] = nc.dram_tensor("scr_q", [C, C], BF16,
                                     kind="Internal").ap()
        st["scr_kb"] = nc.dram_tensor("scr_kb", [256, C], BF16,
                                      kind="Internal").ap()
        rtb = b["A3"][:, 0:8192]
        rtt = P.add("gpsimd",
                    (lambda e: e.dma_start(
                        rtb.rearrange("p (kt m) -> p kt m", kt=NT),
                        dr["c_rt"].rearrange("(kt p) m -> p kt m", p=128))),
                    ctr=P.ctr("rtd"), n=16)
        wringf = WRing(P, b["A1"], 0, 2, 1024, tag="wF")
        fst = b["A1"][:, 2048:3072]
        f1 = emit_fold(P, b, dr["l0a_wq"], rtb, fst, st["scr_q"], NT, wringf,
                       [rtt])
        f2 = emit_fold(P, b, dr["l0a_wkb"], rtb, fst, st["scr_kb"], 2, wringf,
                       [rtt] + f1[-1:])
        toks += f1 + f2
    st["toks"] = toks
    return st


def build_prog(cs, seq):
    P = Prog()
    kinds = sorted(set(seq))
    dr, names = _declare(P, cs, kinds)
    b = make_bufs(P)
    ct = emit_const_loads(P, b, dr)
    st = _prologues(P, b, dr, kinds)
    prev_store = []
    for bi in range(BPC):
        lx = emit_load_x(P, b, dr["x"], bi,
                         deps=joined(ct, prev_store, st["toks"]))
        t = joined(lx, ct, st["toks"], prev_store)
        for kind in seq:
            if kind == "l0f":
                t = emit_sl_l0ffn(P, b, dr, "l0f_", t)
            elif kind == "l0a":
                t = emit_sl_l0attn(P, b, dr, st["scr_q"], st["scr_kb"], t)
            elif kind == "l1f":
                t = emit_sl_l1ffn(P, b, dr, st["gbcb"], t)
            elif kind == "l1a":
                t = emit_sl_l1attn(P, b, dr, t)
        prev_store = emit_store_x(P, b, dr["xo"], bi, deps=t)
    P.realize()
    return P.nc, names



def _t(a):
    return np.ascontiguousarray(np.asarray(a, np.float32).T)


def _qperm_cols():
    perm = np.zeros(C, dtype=np.int64)
    for h in range(NH):
        tile = (h % 4) + 4 * (h // 8)
        par = (h // 4) % 2
        slot = 128 * tile + 64 * par
        perm[slot:slot + 64] = np.arange(64 * h, 64 * h + 64)
    return perm


def host_prep(inputs):
    _ = inputs["L1_ffn_input_ids"]
    _ = inputs["L1_attn_time_first"]
    cs = host_consts()
    qp = _qperm_cols()
    w = {
        "l0f_wqT": np.ascontiguousarray(_t(inputs["L0_ffn_wq"])[:, qp]),
        "l0f_wkT": _t(inputs["L0_ffn_wk"]),
        "l0f_wvT": _t(inputs["L0_ffn_wv"]),
        "l0f_woT": np.ascontiguousarray(_t(inputs["L0_ffn_wo"])[qp, :]),
        "l0f_gwT": _t(inputs["L0_ffn_gate_w"]),
        "l0f_expwT": np.ascontiguousarray(
            np.asarray(inputs["L0_ffn_expert_w"],
                       np.float32).transpose(0, 2, 1)),
        "l0f_nw": np.concatenate([featvec(inputs["L0_ffn_attn_norm_w"]),
                                  featvec(inputs["L0_ffn_ffn_norm_w"])],
                                 axis=1),
        "l0a_wq": np.ascontiguousarray(
            np.asarray(inputs["L0_attn_wq"], np.float32)),
        "l0a_wkaT": _t(inputs["L0_attn_wk_a"]),
        "l0a_wkb": np.ascontiguousarray(
            np.asarray(inputs["L0_attn_wk_b"], np.float32)),
        "l0a_wvaT": _t(inputs["L0_attn_wv_a"]),
        "l0a_wvbT": _t(inputs["L0_attn_wv_b"]),
        "l0a_woT": _t(inputs["L0_attn_wo"]),
        "l1f_gwT": _t(inputs["L1_ffn_gate_w"]),
        "l1f_gbc": np.ascontiguousarray(
            np.broadcast_to(np.asarray(inputs["L1_ffn_gate_b"], np.float32),
                            (128, 16))),
        "l1f_w13T": np.ascontiguousarray(
            np.asarray(inputs["L1_ffn_w13"], np.float32).transpose(0, 2, 1)),
        "l1f_w2T": np.ascontiguousarray(
            np.asarray(inputs["L1_ffn_w2"], np.float32).transpose(0, 2, 1)),
        "l1f_w13sT": _t(inputs["L1_ffn_w13_s"]),
        "l1f_w2sT": _t(inputs["L1_ffn_w2_s"]),
        "l1a_wkT": _t(inputs["L1_attn_Wk"]),
        "l1a_wvT": _t(inputs["L1_attn_Wv"]),
        "l1a_wrT": _t(inputs["L1_attn_Wr"]),
        "l1a_td": featvec(inputs["L1_attn_time_decay"]),
    }
    return cs, w


FUSED = True


def kernel(**inputs):
    cs, w = host_prep(inputs)
    x = np.ascontiguousarray(
        np.asarray(inputs["L0_attn_x"], np.float32).transpose(0, 2, 1))
    shards = [np.ascontiguousarray(x[c * BPC:(c + 1) * BPC])
              for c in range(NCORES)]

    def run(nc, names, xs):
        in_maps = []
        for c in range(NCORES):
            m = {"x": xs[c]}
            for nm in names:
                if nm == "x":
                    continue
                m[nm] = w[nm] if nm in w else cs[nm]
            in_maps.append(m)
        res = run_bass_kernel_spmd(nc, in_maps, core_ids=list(range(NCORES)))
        return [r["xo"] for r in res.results]

    if FUSED:
        nc, names = build_prog(cs, ["l0f", "l0a", "l0f", "l1f", "l1a", "l1f"])
        shards = run(nc, names, shards)
    else:
        progs = {k: build_prog(cs, [k]) for k in ("l0f", "l0a", "l1f", "l1a")}
        for kind in ["l0f", "l0a", "l0f", "l1f", "l1a", "l1f"]:
            nc, names = progs[kind]
            shards = run(nc, names, shards)

    out = np.concatenate(shards, axis=0)
    return np.ascontiguousarray(out.transpose(0, 2, 1)).astype(np.float32)


# version 12
# speedup vs baseline: 1.0449x; 1.0362x over previous
import numpy as np
import ml_dtypes
from contextlib import ExitStack

import concourse.bass as bass
import concourse.mybir as mybir
from concourse.bass_utils import run_bass_kernel_spmd

F32 = mybir.dt.float32
BF16 = mybir.dt.bfloat16
AF = mybir.ActivationFunctionType
ALU = mybir.AluOpType
AX = mybir.AxisListType
BF16NP = ml_dtypes.bfloat16

NCORES = 8
BPC = 4
S = 2048
C = 1024
NT = C // 128
CH = 512
NCH = S // CH
NH = 16
HD = 64
A1COLS = NH * 16 * 65



class Prog:
    ENGS = ("tensor", "vector", "scalar", "gpsimd", "sync")

    def __init__(self):
        self.nc = bass.Bass("TRN2", target_bir_lowering=False)
        self.ops = {e: [] for e in self.ENGS}
        self.ctx = ExitStack()
        self.ctrs = {}
        for e in self.ENGS:
            self._mkctr(e)
        for d in ("gdma", "sdma", "odma", "xdma", "c8d"):
            self._mkctr(d)
        self.ring_busy = {}

    def _mkctr(self, name):
        h = self.ctx.enter_context(self.nc.semaphore(name))
        self.ctrs[name] = [h, 0]

    def ctr(self, name):
        if name not in self.ctrs:
            self._mkctr(name)
        return name

    def sbuf(self, name, shape, dt):
        return self.ctx.enter_context(self.nc.sbuf_tensor(name, shape, dt))

    def psum(self, name, shape, dt=F32):
        return self.ctx.enter_context(self.nc.psum_tensor(name, shape, dt))

    def add(self, eng, build, deps=(), ctr=None, n=1, fifo=False):
        cname = ctr or eng
        h, cnt = self.ctrs[cname]
        waits = {}
        for d in deps:
            if d is None:
                continue
            dc, dv = d
            if dv <= 0 or (fifo and dc == eng):
                continue
            waits[dc] = max(waits.get(dc, 0), dv)
        wl = [(self.ctrs[dc][0], dv) for dc, dv in waits.items()]
        self.ctrs[cname][1] = cnt + n
        self.ops[eng].append((wl, build, (h, n)))
        return (cname, cnt + n)

    def realize(self):
        finals = [(self.ctrs[n][0], self.ctrs[n][1]) for n in self.ctrs
                  if n not in self.ENGS and self.ctrs[n][1] > 0]
        blk = self.ctx.enter_context(self.nc.Block())
        for eng in self.ENGS:
            ops = self.ops[eng]

            def body(e, ops=ops, eng=eng):
                for waits, build, inc in ops:
                    for (h, v) in waits:
                        e.wait_ge(h, v)
                    r = build(e)
                    if r is not None:
                        r.then_inc(*inc)
                if eng == "sync":
                    for h, v in finals:
                        e.wait_ge(h, v)
            getattr(blk, eng)(body)
        return self.nc


def joined(*tok_lists):
    best = {}
    for tl in tok_lists:
        for t in tl:
            if t is None:
                continue
            c, v = t
            best[c] = max(best.get(c, 0), v)
    return [(c, v) for c, v in best.items()]


class PsumRot:
    def __init__(self, views):
        self.views = views
        self.free = [None] * len(views)
        self.i = 0

    def next(self):
        i = self.i
        self.i = (self.i + 1) % len(self.views)
        return i, self.views[i], ([self.free[i]] if self.free[i] else [])

    def done(self, i, tok):
        self.free[i] = tok



def host_consts():
    cs = {}
    cs["c_idn"] = np.eye(128, dtype=BF16NP)
    cs["c_ones2"] = np.ones((128, 128), dtype=BF16NP)
    cs["c_onesc"] = np.ones((128, 1), dtype=BF16NP)
    cs["c_onesf"] = np.ones((1, 128), dtype=np.float32)
    cs["c_onescf"] = np.ones((128, 1), dtype=np.float32)
    kp = np.arange(128)[:, None]
    q = np.arange(512)[None, :]
    m = np.stack([np.where(q >= 128 * r + kp, 0.0, -1e30) for r in range(4)], 1)
    cs["c_cmskt"] = np.ascontiguousarray(m.reshape(128, 2048)).astype(BF16NP)
    es = np.zeros((16, 16, 128), dtype=np.float32)
    for e in range(16):
        es[e, e, :] = 1.0
    cs["c_esel"] = np.ascontiguousarray(es.transpose(0, 1, 2).reshape(16, 2048)
                                        ).astype(BF16NP)
    R = np.eye(C, dtype=np.float64)
    inv = 10000.0 ** (-np.arange(0, 16, 2) / 64.0)
    for h in range(NH):
        for j in range(8):
            ct, st = np.cos(h * inv[j]), np.sin(h * inv[j])
            a, b = 64 * h + 2 * j, 64 * h + 2 * j + 1
            R[a, a] = ct
            R[a, b] = -st
            R[b, a] = st
            R[b, b] = ct
    cs["c_rt"] = np.ascontiguousarray(R.T).astype(np.float32)
    return cs


def featvec(v):
    return np.ascontiguousarray(np.asarray(v, np.float32).reshape(NT, 128).T)



def make_bufs(P):
    b = {}
    b["X"] = P.sbuf("X", [128, NT * S], F32)
    b["A1"] = P.sbuf("A1", [128, A1COLS], BF16)
    b["A2"] = P.sbuf("A2", [128, NT * S], BF16)
    b["A3"] = P.sbuf("A3", [128, NT * S], BF16)
    b["A4"] = P.sbuf("A4", [128, NT * S], BF16)
    b["F32A"] = P.sbuf("F32A", [128, 1024], F32)
    b["ROWS"] = P.sbuf("ROWS", [1, 1536], F32)
    b["C8B"] = P.sbuf("C8B", [128, 2048], BF16)
    b["COL"] = P.sbuf("COL", [128, 40], F32)
    b["CBC"] = P.sbuf("CBC", [128, 16], F32)
    b["IDN"] = P.sbuf("IDN", [128, 128], BF16)
    b["ONESC"] = P.sbuf("ONESC", [128, 1], BF16)
    b["ONESF"] = P.sbuf("ONESF", [1, 128], F32)
    b["ONESCF"] = P.sbuf("ONESCF", [128, 1], F32)
    b["NORMW"] = P.sbuf("NORMW", [128, 24], F32)
    b["EPSC"] = P.sbuf("EPSC", [1, 2], F32)
    b["PSA"] = P.psum("PSA", [128, 2048], F32)
    b["PSB"] = P.psum("PSB", [128, 512], F32)
    b["PSC"] = P.psum("PSC", [128, 512], F32)
    b["PSD"] = P.psum("PSD", [128, 512], F32)
    b["PTP"] = P.psum("PTP", [128, 512], BF16)
    return b


def xt(b, j):
    return b["X"][:, j * S:(j + 1) * S]


def emit_const_loads(P, b, d, extra=()):
    def g(e):
        sem = P.ctrs["sdma"][0]
        e.dma_start(b["IDN"][:, :], d["c_idn"]).then_inc(sem, 16)
        e.dma_start(b["ONESC"][:, :], d["c_onesc"]).then_inc(sem, 16)
        e.dma_start(b["ONESF"][:, :], d["c_onesf"]).then_inc(sem, 16)
        e.dma_start(b["ONESCF"][:, :], d["c_onescf"]).then_inc(sem, 16)
        for dst, src in extra:
            e.dma_start(dst, src).then_inc(sem, 16)
        return None
    t1 = P.add("sync", g, ctr="sdma", n=16 * (4 + len(extra)))

    def ge(e):
        e.memset(b["EPSC"][0:1, 0:1], 1e-5)
        return e.memset(b["EPSC"][0:1, 1:2], 1e-6)
    t2 = P.add("vector", ge)
    return [t1, t2]


def emit_load_x(P, b, x_ap, bi, deps=()):
    src = x_ap[bi].rearrange("(j p) t -> p j t", p=128)
    dst = b["X"][:, :].rearrange("p (j t) -> p j t", t=S)
    return [P.add("sync", lambda e: e.dma_start(dst, src), deps=deps,
                  ctr="xdma", n=16)]


def emit_store_x(P, b, xo_ap, bi, deps=()):
    dst = xo_ap[bi].rearrange("(j p) t -> p j t", p=128)
    src = b["X"][:, :].rearrange("p (j t) -> p j t", t=S)
    return [P.add("sync", lambda e: e.dma_start(dst, src), deps=deps,
                  ctr="odma", n=16)]



class WRing:

    def __init__(self, P, arena, base_col, nslots, cols, tag="w"):
        self.P = P
        self.tag = tag
        self.slots = [arena[:, base_col + i * cols: base_col + (i + 1) * cols]
                      for i in range(nslots)]
        self.n = nslots
        self.i = 0

    def load(self, dram_slice, ktiles, deps=()):
        P = self.P
        i = self.i
        self.i = (self.i + 1) % self.n
        sl = self.slots[i]
        cast = dram_slice.tensor.dtype != BF16
        cname = P.ctr(f"{self.tag}{'g' if cast else 's'}{i}")
        key = (self.tag, i)
        d = list(deps) + [P.ring_busy.get(key)]
        mcols = dram_slice.shape[-1]
        src = dram_slice.rearrange("(kt p) m -> p kt m", p=128)
        dst = sl[:, 0:ktiles * mcols].rearrange("p (kt m) -> p kt m", kt=ktiles)
        eng = "gpsimd" if cast else "sync"
        tok = P.add(eng, (lambda e: e.dma_start(dst, src)), deps=d,
                    ctr=cname, n=16)
        return sl, key, tok

    def used(self, key, tok):
        self.P.ring_busy[key] = tok



def emit_proj(P, b, wring, wt_dram, src_buf, kdim, mdim, prot, evac,
              deps_src=(), src_stride=S, src_col0=0, chunks=None, wdeps=()):
    ktiles = kdim // 128
    mtiles = mdim // 128
    chunks = list(range(NCH)) if chunks is None else chunks
    etoks = []
    mmtok = None
    for mi in range(mtiles):
        wsl, wi, wtok = wring.load(wt_dram[:, mi * 128:(mi + 1) * 128], ktiles,
                                   deps=wdeps)
        for ci in chunks:
            si, pv, ftoks = prot.next()

            def g(e, wsl=wsl, pv=pv, ci=ci, ktiles=ktiles):
                last = None
                for kt in range(ktiles):
                    c0 = src_col0 + kt * src_stride + ci * CH
                    last = e.matmul(pv, wsl[:, kt * 128:(kt + 1) * 128],
                                    src_buf[:, c0:c0 + CH],
                                    start=(kt == 0), stop=(kt == ktiles - 1))
                return last
            mmtok = P.add("tensor", g, deps=[wtok] + list(deps_src) + ftoks,
                          fifo=True)
            et = evac(mi, ci, pv, [mmtok])
            prot.done(si, et)
            etoks.append(et)
        wring.used(wi, mmtok)
    return etoks


def mk_evac_to(P, dst_buf, eng="scalar", dst_stride=S, dst_col0=0):
    def evac(mi, ci, pv, deps):
        c0 = dst_col0 + mi * dst_stride + ci * CH
        dst = dst_buf[:, c0:c0 + CH]
        if eng == "scalar":
            return P.add("scalar", lambda e: e.activation(dst, pv, AF.Copy),
                         deps=deps)
        return P.add("vector", lambda e: e.tensor_copy(dst, pv), deps=deps)
    return evac


def mk_evac_resid(P, b, scale=None, extra=()):
    def evac(mi, ci, pv, deps):
        dst = b["X"][:, mi * S + ci * CH: mi * S + ci * CH + CH]
        if scale is None:
            return P.add("vector",
                         lambda e: e.tensor_tensor(dst, dst, pv, ALU.add),
                         deps=list(deps) + list(extra))
        return P.add("vector",
                     lambda e: e.scalar_tensor_tensor(dst, pv, float(scale),
                                                      dst, ALU.mult, ALU.add),
                     deps=list(deps) + list(extra))
    return evac



def emit_layernorm(P, b, dst_buf, arena, deps, eps=1e-5):
    xq = [arena[:, 1024:1536], arena[:, 1536:2048]]
    out = []
    sum_free = []
    row_free = []
    rowvar_free = {}
    xq_free = {0: None, 1: None}
    for ci in range(NCH):
        negmu = b["ROWS"][0:1, 0:512]
        invr = b["ROWS"][0:1, 512:1024]
        msh = b["ROWS"][0:1, 1024:1536]
        sumx = b["PSA"][0:1, 0:512]
        sumq = b["PSA"][0:1, 512:1024]
        mmx = mmq = None
        for j in range(NT):
            sl = j % 2
            mmx = P.add("tensor",
                        (lambda e, j=j, ci=ci:
                         e.matmul(sumx, b["ONESCF"][:, 0:1],
                                  xt(b, j)[:, ci * CH:ci * CH + CH],
                                  start=(j == 0), stop=(j == NT - 1))),
                        deps=list(deps) + sum_free, fifo=True)
            sq = P.add("scalar",
                       (lambda e, sl=sl, j=j, ci=ci:
                        e.activation(xq[sl],
                                     xt(b, j)[:, ci * CH:ci * CH + CH],
                                     AF.Square)),
                       deps=list(deps) + [xq_free[sl]] + sum_free)
            mmq = P.add("tensor",
                        (lambda e, sl=sl, j=j:
                         e.matmul(sumq, b["ONESC"][:, 0:1], xq[sl],
                                  start=(j == 0), stop=(j == NT - 1))),
                        deps=[sq] + sum_free, fifo=True)
            xq_free[sl] = mmq
        t1 = P.add("vector",
                   (lambda e, negmu=negmu:
                    e.tensor_scalar(negmu, sumx, -1.0 / C, None, ALU.mult)),
                   deps=[mmx] + [rowvar_free.get(ci % 2)])
        t2 = P.add("vector",
                   (lambda e, negmu=negmu, msh=msh:
                    e.tensor_tensor(msh, negmu, negmu, ALU.mult)),
                   deps=[t1])
        t3 = P.add("vector",
                   (lambda e, invr=invr, msh=msh:
                    e.scalar_tensor_tensor(invr, sumq, 1.0 / C, msh,
                                           ALU.mult, ALU.subtract)),
                   deps=[t2, mmq])
        sum_free = [t1, t3]
        t4 = P.add("scalar",
                   (lambda e, invr=invr, msh=msh:
                    e.activation(msh, invr, AF.Ln, bias=b["EPSC"][0:1, 0:1])),
                   deps=[t3])
        t5 = P.add("scalar",
                   (lambda e, invr=invr, msh=msh:
                    e.activation(invr, msh, AF.Exp, scale=-0.5)),
                   deps=[t4])
        t6 = P.add("vector",
                   (lambda e, negmu=negmu, invr=invr, msh=msh:
                    e.tensor_tensor(msh, negmu, invr, ALU.mult)),
                   deps=[t5])
        bcA = P.add("tensor",
                    (lambda e, invr=invr:
                     e.matmul(b["PSB"][:, 0:512], b["ONESF"][0:1, :],
                              invr, start=True, stop=True)),
                    deps=[t5] + row_free)
        bcB = P.add("tensor",
                    (lambda e, msh=msh:
                     e.matmul(b["PSC"][:, 0:512], b["ONESF"][0:1, :],
                              msh, start=True, stop=True)),
                    deps=[t6] + row_free)
        rowvar_free[ci % 2] = bcB
        rf = []
        for j in range(NT):
            tt1 = P.add("vector",
                        (lambda e, j=j, ci=ci:
                         e.tensor_tensor(b["F32A"][:, 512:1024],
                                         xt(b, j)[:, ci * CH:ci * CH + CH],
                                         b["PSB"][:, 0:512], ALU.mult)),
                        deps=[bcA] + ([rf[-1]] if rf else []))
            tt2 = P.add("vector",
                        (lambda e, j=j, ci=ci:
                         e.tensor_tensor(
                             dst_buf[:, j * S + ci * CH:j * S + ci * CH + CH],
                             b["F32A"][:, 512:1024], b["PSC"][:, 0:512],
                             ALU.add)),
                        deps=[bcB, tt1])
            rf.append(tt2)
            out.append(tt2)
        row_free = [rf[-1]]
    return out


def emit_rmsnorm(P, b, src_buf, dst_buf, nw_col, arena, deps, eps=1e-6):
    xq = [arena[:, 0:512], arena[:, 512:1024]]
    out = []
    sum_free = []
    row_free = []
    rowvar_free = {}
    xq_free = {0: None, 1: None}
    for ci in range(NCH):
        invr = b["ROWS"][0:1, 512:1024]
        msh = b["ROWS"][0:1, 1024:1536]
        sumq = b["PSA"][0:1, 512:1024]
        mmq = None
        for j in range(NT):
            sl = j % 2
            sq = P.add("scalar",
                       (lambda e, j=j, ci=ci, sl=sl:
                        e.activation(
                            xq[sl],
                            src_buf[:, j * S + ci * CH:j * S + ci * CH + CH],
                            AF.Square)),
                       deps=list(deps) + [xq_free[sl]] + sum_free)
            mmq = P.add("tensor",
                        (lambda e, sl=sl, j=j:
                         e.matmul(sumq, b["ONESC"][:, 0:1], xq[sl],
                                  start=(j == 0), stop=(j == NT - 1))),
                        deps=[sq] + sum_free, fifo=True)
            xq_free[sl] = mmq
        t1 = P.add("vector",
                   (lambda e, invr=invr:
                    e.tensor_scalar(invr, sumq, 1.0 / C, None, ALU.mult)),
                   deps=[mmq] + [rowvar_free.get(ci % 2)])
        sum_free = [t1]
        t2 = P.add("scalar",
                   (lambda e, invr=invr, msh=msh:
                    e.activation(msh, invr, AF.Ln, bias=b["EPSC"][0:1, 1:2])),
                   deps=[t1])
        t3 = P.add("scalar",
                   (lambda e, invr=invr, msh=msh:
                    e.activation(invr, msh, AF.Exp, scale=-0.5)),
                   deps=[t2])
        bcA = P.add("tensor",
                    (lambda e, invr=invr:
                     e.matmul(b["PSC"][:, 0:512], b["ONESF"][0:1, :],
                              invr, start=True, stop=True)),
                    deps=[t3] + row_free)
        rowvar_free[ci % 2] = bcA
        rf = []
        for j in range(NT):
            tt = P.add("vector",
                       (lambda e, j=j, ci=ci:
                        e.scalar_tensor_tensor(
                            dst_buf[:, j * S + ci * CH:j * S + ci * CH + CH],
                            src_buf[:, j * S + ci * CH:j * S + ci * CH + CH],
                            nw_col(j), b["PSC"][:, 0:512],
                            ALU.mult, ALU.mult)),
                       deps=[bcA] + ([rf[-1]] if rf else []))
            rf.append(tt)
            out.append(tt)
        row_free = [rf[-1]]
    return out



def mk_evac_vtok(P, b, vt_buf, pair_heads, vs_view, ts_view=None):
    state = {"ptp_free": None, "vs_free": None}

    def evac(mi, ci, pv, deps):
        hA, hB = pair_heads(mi)
        cp = P.add("scalar", lambda e: e.activation(vs_view, pv, AF.Copy),
                   deps=list(deps) + [state["vs_free"]])
        trs = []
        for blk in range(4):
            tr = P.add("tensor",
                       (lambda e, blk=blk:
                        e.transpose(b["PTP"][:, blk * 128:blk * 128 + 128],
                                    vs_view[:, blk * 128:blk * 128 + 128],
                                    b["IDN"][:, :])),
                       deps=[cp, state["ptp_free"]], fifo=True)
            trs.append(tr)
        state["vs_free"] = trs[-1]
        kt0 = ci * 4
        last = None
        for blk in range(4):
            kt = kt0 + blk
            cA = 65 * (hA * 16 + kt)
            cB = 65 * (hB * 16 + kt)

            def gv(e, cA=cA, cB=cB, blk=blk):
                e.tensor_copy(vt_buf[:, cA:cA + 64],
                              b["PTP"][:, blk * 128:blk * 128 + 64])
                return e.tensor_copy(vt_buf[:, cB:cB + 64],
                                     b["PTP"][:, blk * 128 + 64:blk * 128 + 128])
            last = P.add("vector", gv, deps=[trs[3]])
        state["ptp_free"] = last
        return last
    return evac


def emit_attn(P, b, q_buf, k_buf, vt_buf, o_buf, qslot, kslot, vblk, causal,
              deps, arena, cmskt=None):
    sqst = arena[:, 0:2048]
    ptst = arena[:, 2048:4096]
    rcrow = b["ROWS"][0:1, 512:1024]
    srot = PsumRot([b["PSA"][:, i * 512:(i + 1) * 512] for i in range(4)])
    orot = PsumRot([b["PSB"][:, 0:512], b["PSD"][:, 0:512]])
    out = []
    rows = [b["PSC"][0:1, 0:512], b["PSD"][0:1, 0:512]]
    rowtok = [[], []]
    sq_free = []
    kvs = {}
    for h in range(NH):
        kvs.setdefault(kslot(h), []).append(h)
    kcol = {}
    acc = []
    prev_kx = []
    for kv, heads in kvs.items():
        kt_, kp_ = kv
        ksl = k_buf[64 * kp_:64 * kp_ + 64, kt_ * S:(kt_ + 1) * S]
        sqk = sqst[64 * kp_:64 * kp_ + 64, 0:2048]
        sq1 = P.add("scalar",
                    (lambda e, sqk=sqk, ksl=ksl:
                     e.activation(sqk, ksl, AF.Square)),
                    deps=list(deps) + sq_free)
        kms = []
        cell = b["COL"][0:1, 1 + len(kcol):2 + len(kcol)]
        for ci in range(NCH):
            rv = rows[ci % 2]
            km = P.add("tensor",
                       (lambda e, ci=ci, sqk=sqk, kp_=kp_, rv=rv:
                        e.matmul(rv, b["ONESC"][64 * kp_:64 * kp_ + 64, 0:1],
                                 sqk[:, ci * 512:ci * 512 + 512],
                                 start=True, stop=True)),
                       deps=[sq1] + rowtok[ci % 2])
            kc = P.add("vector",
                       (lambda e, ci=ci, rv=rv:
                        e.tensor_reduce(b["COL"][0:1, 20 + ci:21 + ci], rv,
                                        AX.X, ALU.max)),
                       deps=[km] + prev_kx)
            rowtok[ci % 2] = [kc]
            kms.append(kc)
        kx = P.add("vector",
                   (lambda e, cell=cell:
                    e.tensor_reduce(cell, b["COL"][0:1, 20:24], AX.X, ALU.max)),
                   deps=kms)
        kcol[kv] = (cell, kx)
        sq_free = kms
        prev_kx = [kx]
        acc.append(kx)
    qcell = b["COL"][0:1, 0:1]
    cbc_toks = {}
    for h in range(NH):
        qt, qp = qslot(h)
        qsl = q_buf[64 * qp:64 * qp + 64, qt * S:(qt + 1) * S]
        sqq = sqst[64 * qp:64 * qp + 64, 0:2048]
        sq2 = P.add("scalar",
                    (lambda e, sqq=sqq, qsl=qsl:
                     e.activation(sqq, qsl, AF.Square)),
                    deps=list(deps) + sq_free)
        qms = []
        for ci in range(NCH):
            rv = rows[ci % 2]
            qm = P.add("tensor",
                       (lambda e, ci=ci, sqq=sqq, qp=qp, rv=rv:
                        e.matmul(rv, b["ONESC"][64 * qp:64 * qp + 64, 0:1],
                                 sqq[:, ci * 512:ci * 512 + 512],
                                 start=True, stop=True)),
                       deps=[sq2] + rowtok[ci % 2])
            qc = P.add("vector",
                       (lambda e, ci=ci, rv=rv:
                        e.tensor_reduce(b["COL"][0:1, 24 + ci:25 + ci], rv,
                                        AX.X, ALU.max)),
                       deps=[qm, cbc_toks.get(h - 1)])
            rowtok[ci % 2] = [qc]
            qms.append(qc)
        sq_free = qms
        kcell, kxt = kcol[kslot(h)]
        t1 = P.add("vector",
                   (lambda e: e.tensor_reduce(qcell, b["COL"][0:1, 24:28],
                                              AX.X, ALU.max)),
                   deps=qms + [cbc_toks.get(h - 1)])
        t2 = P.add("vector",
                   (lambda e, kcell=kcell:
                    e.tensor_tensor(qcell, qcell, kcell, ALU.mult)),
                   deps=[t1, kxt])
        t3 = P.add("scalar",
                   lambda e: e.activation(qcell, qcell, AF.Sqrt),
                   deps=[t2])
        t4 = P.add("vector",
                   (lambda e: e.tensor_scalar(qcell, qcell, -0.125, None,
                                              ALU.mult)),
                   deps=[t3])
        t5 = P.add("tensor",
                   (lambda e: e.matmul(b["PSC"][:, 0:1], b["ONESF"][0:1, :],
                                       qcell, start=True, stop=True)),
                   deps=[t4] + rowtok[0] + rowtok[1])
        cb = P.add("vector",
                   (lambda e, h=h: e.tensor_copy(b["CBC"][:, h:h + 1],
                                                 b["PSC"][:, 0:1])),
                   deps=[t5])
        rowtok[0] = rowtok[1] = [cb]
        cbc_toks[h] = cb
        acc.append(cb)
    prep = joined(acc, [cbc_toks[NH - 1]])
    pend = None
    pt_free = {0: [], 1: []}

    def emit_pv(prevp):
        ex, kts, pp, h, ov, nkt, ofree = prevp
        last = None
        for i, kt in enumerate(kts):
            vcol = 65 * (vblk(h) * 16 + kt)
            pt = ptst[:, pp * 1024 + i * 512: pp * 1024 + i * 512 + 512]
            last = P.add("tensor",
                         (lambda e, kt=kt, nkt=nkt, vcol=vcol, ov=ov, pt=pt:
                          e.matmul(ov[0:65, :], vt_buf[:, vcol:vcol + 65],
                                   pt, start=(kt == 0),
                                   stop=(kt == nkt - 1))),
                         deps=[ex] + (ofree if kt == 0 else []), fifo=True)
        pt_free[pp] = [last]
        return last

    norm_state = {"prev": []}

    def emit_norm(pend):
        ov, oi, qt, qp, ci, _pv, _nf = pend
        rc = P.add("vector",
                   (lambda e, ov=ov: e.reciprocal(rcrow, ov[64:65, :])),
                   deps=[pend[5]] + pend[6] + norm_state["prev"])
        bc = P.add("tensor",
                   lambda e: e.matmul(b["PSC"][0:64, 0:512],
                                      b["ONESF"][0:1, 0:64], rcrow,
                                      start=True, stop=True),
                   deps=[rc])
        cbe = P.add("vector",
                    lambda e: e.tensor_copy(b["F32A"][0:64, 0:512],
                                            b["PSC"][0:64, 0:512]),
                    deps=[bc] + norm_state["prev"])
        oc0 = qt * S + ci * CH
        fin = P.add("vector",
                    (lambda e, ov=ov, oc0=oc0, qp=qp:
                     e.tensor_tensor(
                         o_buf[64 * qp:64 * qp + 64, oc0:oc0 + CH],
                         ov[0:64, :], b["F32A"][0:64, 0:512], ALU.mult)),
                    deps=[cbe])
        orot.done(oi, fin)
        out.append(fin)
        norm_state["prev"] = [rc, fin]
        return [cbe, fin]

    nfree = []
    for h in range(NH):
        qt, qp = qslot(h)
        kt_, kp_ = kslot(h)
        qsl = q_buf[64 * qp:64 * qp + 64, qt * S:(qt + 1) * S]
        ksl = k_buf[64 * kp_:64 * kp_ + 64, kt_ * S:(kt_ + 1) * S]
        for ci in range(NCH):
            nkt = (4 * ci + 4) if causal else 16
            oi, ov, ofree = orot.next()
            pv_last = None
            prevp = None
            for p in range(nkt // 2):
                kts = (2 * p, 2 * p + 1)
                mdep = []
                svs = []
                s0 = None
                for kt in kts:
                    si, sv, sfree = srot.next()
                    if s0 is None:
                        s0 = (si, sv)
                    svs.append(si)

                    def gmm(e, sv=sv, kt=kt, ci=ci, qsl=qsl, ksl=ksl):
                        return e.matmul(sv, ksl[:, kt * 128:kt * 128 + 128],
                                        qsl[:, ci * CH:ci * CH + CH],
                                        start=True, stop=True)
                    mm = P.add("tensor", gmm, deps=sfree + list(deps) + prep,
                               fifo=True)
                    if causal and kt >= 4 * ci:
                        r = kt - 4 * ci
                        mk = P.add("vector",
                                   (lambda e, sv=sv, r=r:
                                    e.tensor_tensor(
                                        sv, sv,
                                        cmskt[:, r * 512:r * 512 + 512],
                                        ALU.add)),
                                   deps=[mm])
                        mdep.append(mk)
                    else:
                        mdep.append(mm)
                pp = p % 2
                pt2 = ptst[:, pp * 1024:pp * 1024 + 1024]
                sv2 = b["PSA"][:, s0[0] * 512:s0[0] * 512 + 1024]
                ex = P.add("scalar",
                           (lambda e, sv2=sv2, pt2=pt2, h=h:
                            e.activation(pt2, sv2, AF.Exp, scale=0.125,
                                         bias=b["CBC"][:, h:h + 1])),
                           deps=mdep + pt_free[pp] + prep)
                for si in svs:
                    srot.done(si, ex)
                if prevp is not None:
                    pv_last = emit_pv(prevp)
                prevp = (ex, kts, pp, h, ov, nkt, ofree)
            pv_last = emit_pv(prevp)
            this = (ov, oi, qt, qp, ci, pv_last, nfree)
            if pend is not None:
                nfree = emit_norm(pend)
            pend = this
    if pend is not None:
        emit_norm(pend)
    return out



def emit_gate8(P, b, f_buf, gw_view, gsc, cmb_view, combt_view, deps,
               scale=0.5):
    out = []
    L = gsc[:, 0:8]
    T = gsc[:, 8:16]
    E = gsc[:, 16:24]
    m1 = b["COL"][:, 1:2]
    nm1 = b["COL"][:, 2:3]
    m2 = b["COL"][:, 3:4]
    den = b["COL"][:, 4:5]
    rc = b["COL"][:, 5:6]
    prev = []
    for tc in range(16):
        def gmm(e, tc=tc):
            last = None
            for j in range(NT):
                last = e.matmul(b["PSA"][:, 1024:1040][:, 0:8],
                                f_buf[:, j * S + tc * 128:j * S + tc * 128 + 128],
                                gw_view[:, j * 8:(j + 1) * 8],
                                start=(j == 0), stop=(j == NT - 1))
            return last
        mm = P.add("tensor", gmm, deps=list(deps) + prev, fifo=True)
        t0 = P.add("vector",
                   lambda e: e.tensor_copy(L, b["PSA"][:, 1024:1032]),
                   deps=[mm])
        t1 = P.add("vector", lambda e: e.tensor_reduce(m1, L, AX.X, ALU.max),
                   deps=[t0])
        t1b = P.add("vector",
                    lambda e: e.tensor_scalar(nm1, m1, -1.0, None, ALU.mult),
                    deps=[t1])
        t2 = P.add("vector",
                   lambda e: e.tensor_scalar(T, L, m1[0:128, 0:1], None,
                                             ALU.is_equal),
                   deps=[t1])
        t3 = P.add("vector",
                   lambda e: e.scalar_tensor_tensor(T, T, -1e30, L, ALU.mult,
                                                    ALU.add),
                   deps=[t2])
        t4 = P.add("vector", lambda e: e.tensor_reduce(m2, T, AX.X, ALU.max),
                   deps=[t3])
        t5 = P.add("vector",
                   lambda e: e.tensor_scalar(T, L, m2[0:128, 0:1], None,
                                             ALU.is_ge),
                   deps=[t4])
        t6 = P.add("scalar",
                   lambda e: e.activation(E, L, AF.Exp, bias=nm1[0:128, 0:1]),
                   deps=[t1b, t0])
        t7 = P.add("vector", lambda e: e.tensor_tensor(E, E, T, ALU.mult),
                   deps=[t5, t6])
        t8 = P.add("vector", lambda e: e.tensor_reduce(den, E, AX.X, ALU.add),
                   deps=[t7])
        t9 = P.add("vector", lambda e: e.reciprocal(rc, den), deps=[t8])
        t10 = P.add("vector",
                    (lambda e: e.tensor_scalar(cmb_view[:, 0:8], E,
                                               rc[0:128, 0:1], float(scale),
                                               ALU.mult, ALU.mult)),
                    deps=[t9])
        tr = P.add("tensor",
                   (lambda e: e.transpose(b["PTP"][0:8, 0:128],
                                          cmb_view[:, 0:8], b["IDN"][:, :])),
                   deps=[t10] + prev)
        ev = P.add("vector",
                   (lambda e, tc=tc:
                    e.tensor_copy(combt_view[0:8, tc * 128:tc * 128 + 128],
                                  b["PTP"][0:8, 0:128])),
                   deps=[tr])
        prev = [ev]
        out.append(ev)
    return out


def emit_gate16(P, b, f_buf, gw_view, gbc_view, gsc, cmb_view, combt_view,
                deps, route_scale=2.5, sub_scale=0.5):
    out = []
    L = gsc[:, 0:16]
    Bv = gsc[:, 16:32]
    T = gsc[:, 32:48]
    E = gsc[:, 48:64]
    mx = b["COL"][:, 1:2]
    den = b["COL"][:, 4:5]
    rc = b["COL"][:, 5:6]
    prev = []
    for tc in range(16):
        def gmm(e, tc=tc):
            last = None
            for j in range(NT):
                last = e.matmul(b["PSA"][:, 1024:1040],
                                f_buf[:, j * S + tc * 128:j * S + tc * 128 + 128],
                                gw_view[:, j * 16:(j + 1) * 16],
                                start=(j == 0), stop=(j == NT - 1))
            return last
        mm = P.add("tensor", gmm, deps=list(deps) + prev, fifo=True)
        t0 = P.add("scalar",
                   lambda e: e.activation(L, b["PSA"][:, 1024:1040], AF.Sigmoid),
                   deps=[mm])
        t1 = P.add("vector", lambda e: e.tensor_tensor(Bv, L, gbc_view, ALU.add),
                   deps=[t0])
        tsel = t1
        for r in range(4):
            ta = P.add("vector",
                       lambda e: e.tensor_reduce(mx, Bv, AX.X, ALU.max),
                       deps=[tsel])
            if r < 3:
                tb = P.add("vector",
                           (lambda e:
                            e.tensor_scalar(T, Bv, mx[0:128, 0:1], None,
                                            ALU.is_equal)),
                           deps=[ta])
                tsel = P.add("vector",
                             (lambda e:
                              e.scalar_tensor_tensor(Bv, T, -1e30, Bv,
                                                     ALU.mult, ALU.add)),
                             deps=[tb])
            else:
                tsel = ta
        t2 = P.add("vector", lambda e: e.tensor_tensor(Bv, L, gbc_view, ALU.add),
                   deps=[tsel])
        t3 = P.add("vector",
                   lambda e: e.tensor_scalar(T, Bv, mx[0:128, 0:1], None,
                                             ALU.is_ge),
                   deps=[t2])
        t4 = P.add("vector", lambda e: e.tensor_tensor(E, L, T, ALU.mult),
                   deps=[t3])
        t5 = P.add("vector", lambda e: e.tensor_reduce(den, E, AX.X, ALU.add),
                   deps=[t4])
        t6 = P.add("vector", lambda e: e.reciprocal(rc, den), deps=[t5])
        t7 = P.add("vector",
                   (lambda e:
                    e.tensor_scalar(cmb_view[:, 0:16], E, rc[0:128, 0:1],
                                    float(route_scale * sub_scale),
                                    ALU.mult, ALU.mult)),
                   deps=[t6])
        tr = P.add("tensor",
                   (lambda e: e.transpose(b["PTP"][0:16, 0:128],
                                          cmb_view[:, 0:16], b["IDN"][:, :])),
                   deps=[t7] + prev)
        ev = P.add("vector",
                   (lambda e, tc=tc:
                    e.tensor_copy(combt_view[0:16, tc * 128:tc * 128 + 128],
                                  b["PTP"][0:16, 0:128])),
                   deps=[tr])
        prev = [ev]
        out.append(ev)
    return out


def emit_comb_bc(P, b, esel_view, combt_view, nexp, e_idx, cbe4_view, deps,
                 ps_view=None):
    pv = b["PSD"][:, 0:512] if ps_view is None else ps_view
    toks = []
    for ci in range(NCH):
        mm = P.add("tensor",
                   (lambda e, ci=ci:
                    e.matmul(pv,
                             esel_view[0:nexp, e_idx * 128:e_idx * 128 + 128],
                             combt_view[0:nexp, ci * CH:ci * CH + CH],
                             start=True, stop=True)),
                   deps=list(deps) + toks[-1:])
        ev = P.add("scalar",
                   (lambda e, ci=ci:
                    e.activation(cbe4_view[:, ci * CH:ci * CH + CH],
                                 pv, AF.Copy)),
                   deps=[mm])
        toks.append(ev)
    return toks


def emit_moe8(P, b, f_buf, expw_ap, esel_view, combt_view, cbe4_view, wring,
              deps, gate_deps, fe_buf=None, wdeps=()):
    out = []
    orot = PsumRot([b["PSB"][:, 0:512], b["PSC"][:, 0:512]])
    t1 = b["F32A"][:, 512:1024]
    cb_tok = []
    prev_t1 = []
    cbe4_readers = []
    for e_ in range(8):
        cb_tok = emit_comb_bc(P, b, esel_view, combt_view, 8, e_, cbe4_view,
                              joined(gate_deps, cb_tok, cbe4_readers))
        cbe4_readers = []
        for mo in range(NT):
            wsl, wi, wtok = wring.load(expw_ap[e_][:, mo * 128:(mo + 1) * 128],
                                       NT, deps=wdeps)
            last = None
            for ci in range(NCH):
                si, pv, sfree = orot.next()

                def gmm(e, wsl=wsl, pv=pv, ci=ci):
                    r = None
                    for kt in range(NT):
                        r = e.matmul(pv, wsl[:, kt * 128:(kt + 1) * 128],
                                     f_buf[:, kt * S + ci * CH:
                                           kt * S + ci * CH + CH],
                                     start=(kt == 0), stop=(kt == NT - 1))
                    return r
                mm = P.add("tensor", gmm,
                           deps=[wtok] + list(deps) + sfree, fifo=True)
                tta = P.add("vector",
                            (lambda e, pv=pv, ci=ci:
                             e.tensor_tensor(t1, pv,
                                             cbe4_view[:, ci * CH:ci * CH + CH],
                                             ALU.mult)),
                            deps=[mm] + prev_t1 + cb_tok)
                ttb = P.add("vector",
                            (lambda e, mo=mo, ci=ci:
                             e.tensor_tensor(
                                 b["X"][:, mo * S + ci * CH:mo * S + ci * CH + CH],
                                 b["X"][:, mo * S + ci * CH:mo * S + ci * CH + CH],
                                 t1, ALU.add)),
                            deps=[tta])
                prev_t1 = [ttb]
                cbe4_readers = [tta]
                orot.done(si, tta)
                last = ttb
                out.append(ttb)
            wring.used(wi, last)
    return out


def emit_moe16(P, b, y_buf, w13_ap, w2_ap, w13s_ap, w2s_ap, esel_view,
               combt_view, cbe4_view, sg_view, h_buf, wring, w2ring, deps,
               gate_deps):
    out = []
    t1 = b["F32A"][:, 512:1024]
    grot = PsumRot([b["PSB"][:, 0:512], b["PSC"][:, 0:512],
                    b["PSD"][:, 0:512], b["PSA"][:, 1536:2048]])
    orot = PsumRot([b["PSA"][:, 0:512], b["PSA"][:, 512:1024]])
    prev_t1 = []
    prev_sg = {0: [], 1: []}
    cbe4_readers = []
    groups = [(2 * g, 2 * g + 1) for g in range(8)] + [(16,)]
    for grp in groups:
        for slot, e_ in enumerate(grp):
            if e_ < 16:
                cb = emit_comb_bc(P, b, esel_view, combt_view, 16, e_,
                                  cbe4_view,
                                  joined(gate_deps, cbe4_readers),
                                  ps_view=b["PSA"][:, 1024:1536])
            else:
                cb = [P.add("vector",
                            lambda e: e.memset(cbe4_view[:, 0:2048], 0.5),
                            deps=joined(gate_deps, cbe4_readers))]
            cbe4_readers = []
            w13 = w13_ap[e_] if e_ < 16 else w13s_ap
            for mg in range(4):
                wg, wgi, wgt = wring.load(w13[:, mg * 128:(mg + 1) * 128], NT)
                wu, wui, wut = wring.load(
                    w13[:, (mg + 4) * 128:(mg + 5) * 128], NT)
                lastg = lastu = None
                for ci in range(NCH):
                    gi, gv, gfree = grot.next()
                    ui, uv, ufree = grot.next()

                    def gmm(e, wsl=wg, pv=gv, ci=ci):
                        r = None
                        for kt in range(NT):
                            r = e.matmul(pv, wsl[:, kt * 128:(kt + 1) * 128],
                                         y_buf[:, kt * S + ci * CH:
                                               kt * S + ci * CH + CH],
                                         start=(kt == 0), stop=(kt == NT - 1))
                        return r
                    mmg = P.add("tensor", gmm, deps=[wgt] + list(deps) + gfree,
                                fifo=True)

                    def umm(e, wsl=wu, pv=uv, ci=ci):
                        r = None
                        for kt in range(NT):
                            r = e.matmul(pv, wsl[:, kt * 128:(kt + 1) * 128],
                                         y_buf[:, kt * S + ci * CH:
                                               kt * S + ci * CH + CH],
                                         start=(kt == 0), stop=(kt == NT - 1))
                        return r
                    mmu = P.add("tensor", umm, deps=[wut] + list(deps) + ufree,
                                fifo=True)
                    sgv = sg_view[:, (ci % 2) * 512:(ci % 2) * 512 + 512]
                    sg = P.add("scalar",
                               (lambda e, gv=gv, sgv=sgv:
                                e.activation(sgv, gv, AF.Silu)),
                               deps=[mmg] + prev_sg[ci % 2])
                    up = P.add("vector",
                               (lambda e, uv=uv, ci=ci:
                                e.tensor_tensor(
                                    t1, uv,
                                    cbe4_view[:, ci * CH:ci * CH + CH],
                                    ALU.mult)),
                               deps=[mmu] + prev_t1 + cb)
                    hc0 = slot * (4 * S) + mg * S + ci * CH
                    hh = P.add("vector",
                               (lambda e, hc0=hc0, sgv=sgv:
                                e.tensor_tensor(h_buf[:, hc0:hc0 + CH], t1,
                                                sgv, ALU.mult)),
                               deps=[up, sg])
                    prev_t1 = [hh]
                    prev_sg[ci % 2] = [hh]
                    cbe4_readers.append(up)
                    grot.done(gi, sg)
                    grot.done(ui, up)
                    lastg, lastu = hh, hh
                wring.used(wgi, lastg)
                wring.used(wui, lastu)
        w2s = []
        for slot, e_ in enumerate(grp):
            w2s.append(w2_ap[e_] if e_ < 16 else w2s_ap)
        for mo in range(NT):
            wts = []
            for slot, e_ in enumerate(grp):
                wsl, wi, wtok = w2ring.load(w2s[slot][:, mo * 128:(mo + 1) * 128],
                                            4)
                wts.append((wsl, wi, wtok))
            last = None
            for ci in range(NCH):
                oi, pv, ofree = orot.next()

                def gmm(e, pv=pv, ci=ci, wts=wts, grp=grp):
                    r = None
                    n = len(grp) * 4
                    i = 0
                    for slot in range(len(grp)):
                        wsl = wts[slot][0]
                        for kt in range(4):
                            hc0 = slot * (4 * S) + kt * S + ci * CH
                            r = e.matmul(pv, wsl[:, kt * 128:(kt + 1) * 128],
                                         h_buf[:, hc0:hc0 + CH],
                                         start=(i == 0), stop=(i == n - 1))
                            i += 1
                    return r
                hdeps = [prev_t1[0]] if prev_t1 else []
                mm = P.add("tensor", gmm,
                           deps=[w[2] for w in wts] + hdeps + ofree, fifo=True)
                ev = P.add("vector",
                           (lambda e, mo=mo, ci=ci, pv=pv:
                            e.tensor_tensor(
                                b["X"][:, mo * S + ci * CH:mo * S + ci * CH + CH],
                                b["X"][:, mo * S + ci * CH:mo * S + ci * CH + CH],
                                pv, ALU.add)),
                           deps=[mm])
                orot.done(oi, ev)
                last = ev
                out.append(ev)
            for wsl, wi, wtok in wts:
                w2ring.used(wi, last)
        prev_t1 = [last]
        prev_sg = {0: [last], 1: [last]}
    return out



def emit_rwkv(P, b, y_buf, wk_ap, wv_ap, wr_ap, td_view, r_buf, rs_buf,
              wring, wbc_view, deps):
    out = []
    wcol = b["COL"][:, 8:16]
    t0 = P.add("scalar", lambda e: e.activation(wcol, td_view, AF.Exp),
               deps=list(deps))
    t1 = P.add("vector",
               lambda e: e.tensor_scalar(wcol, wcol, -1.0, None, ALU.mult),
               deps=[t0])
    lastc = b["COL"][:, 6:7]
    rs_toks = []
    for j in range(NT):
        wsl, wi, wtok = wring.load(wk_ap[:, j * 128:(j + 1) * 128], NT)

        def gk(e, wsl=wsl):
            r = None
            for ci in range(NCH):
                for kt in range(NT):
                    r = e.matmul(b["PSA"][:, ci * CH:ci * CH + CH],
                                 wsl[:, kt * 128:(kt + 1) * 128],
                                 y_buf[:, kt * S + ci * CH:kt * S + ci * CH + CH],
                                 start=(kt == 0), stop=(kt == NT - 1))
            return r
        kdep = [rs_toks[-1]] if rs_toks else []
        kmm = P.add("tensor", gk, deps=[wtok] + list(deps) + kdep, fifo=True)
        wring.used(wi, kmm)
        wb0 = P.add("vector", lambda e: e.memset(wbc_view, 1.0),
                    deps=kdep)
        wb1 = P.add("vector",
                    (lambda e, j=j:
                     e.tensor_scalar(wbc_view, wbc_view, wcol[:, j:j + 1],
                                     None, ALU.mult)),
                    deps=[wb0, t1])
        wsl2, wi2, wtok2 = wring.load(wr_ap[:, j * 128:(j + 1) * 128], NT)
        rtoks = []
        for ci in range(NCH):
            def gr(e, wsl2=wsl2, ci=ci):
                r = None
                for kt in range(NT):
                    r = e.matmul(b["PSD"][:, 0:512],
                                 wsl2[:, kt * 128:(kt + 1) * 128],
                                 y_buf[:, kt * S + ci * CH:kt * S + ci * CH + CH],
                                 start=(kt == 0), stop=(kt == NT - 1))
                return r
            rdep = rtoks[-1:] if rtoks else (rs_toks[-1:] if rs_toks else [])
            rmm = P.add("tensor", gr, deps=[wtok2] + list(deps) + rdep,
                        fifo=True)
            rt = P.add("scalar",
                       (lambda e, j=j, ci=ci:
                        e.activation(r_buf[:, j * S + ci * CH:j * S + ci * CH + CH],
                                     b["PSD"][:, 0:512], AF.Sigmoid)),
                       deps=[rmm])
            rtoks.append(rt)
        wring.used(wi2, rtoks[-1])
        sc_prev = None
        for ci in range(NCH):
            sv = b["PSB"] if ci % 2 == 0 else b["PSC"]
            init = 0.0 if ci == 0 else lastc[0:128, 0:1]
            sc = P.add("vector",
                       (lambda e, sv=sv, ci=ci, init=init:
                        e.tensor_tensor_scan(sv[:, 0:512], wbc_view,
                                             b["PSA"][:, ci * CH:ci * CH + CH],
                                             init, ALU.mult, ALU.add)),
                       deps=[kmm, wb1] + ([sc_prev] if sc_prev else [])
                       + (rs_toks[-2:] if rs_toks else []))
            cp = P.add("vector",
                       (lambda e, sv=sv:
                        e.tensor_copy(lastc, sv[:, 511:512])),
                       deps=[sc])
            rs = P.add("vector",
                       (lambda e, sv=sv, j=j, ci=ci:
                        e.tensor_tensor(
                            rs_buf[:, j * S + ci * CH:j * S + ci * CH + CH],
                            sv[:, 0:512],
                            r_buf[:, j * S + ci * CH:j * S + ci * CH + CH],
                            ALU.mult)),
                       deps=[sc, rtoks[ci], cp])
            sc_prev = cp
            rs_toks.append(rs)
    orot = PsumRot([b["PSA"][:, i * 512:(i + 1) * 512] for i in range(4)])
    for mo in range(NT):
        wsl, wi, wtok = wring.load(wv_ap[:, mo * 128:(mo + 1) * 128], NT)
        last = None
        for ci in range(NCH):
            oi, pv, ofree = orot.next()

            def gmm(e, wsl=wsl, pv=pv, ci=ci):
                r = None
                for kt in range(NT):
                    r = e.matmul(pv, wsl[:, kt * 128:(kt + 1) * 128],
                                 rs_buf[:, kt * S + ci * CH:kt * S + ci * CH + CH],
                                 start=(kt == 0), stop=(kt == NT - 1))
                return r
            mm = P.add("tensor", gmm,
                       deps=[wtok] + rs_toks[-1:] + ofree, fifo=True)
            ev = P.add("vector",
                       (lambda e, mo=mo, ci=ci, pv=pv:
                        e.tensor_tensor(
                            b["X"][:, mo * S + ci * CH:mo * S + ci * CH + CH],
                            b["X"][:, mo * S + ci * CH:mo * S + ci * CH + CH],
                            pv, ALU.add)),
                       deps=[mm])
            orot.done(oi, ev)
            last = ev
            out.append(ev)
        wring.used(wi, last)
    return out



def emit_fold(P, b, w_dram, rtb_view, fst_view, scratch_ap, itiles, wring,
              deps):
    toks = []
    for i in range(itiles):
        wsl, wi, wtok = wring.load(w_dram[:, i * 128:(i + 1) * 128], NT)
        evs = []
        for n in range(2):
            ps = b["PSB"] if n == 0 else b["PSC"]

            def gmm(e, wsl=wsl, n=n, ps=ps):
                r = None
                for kt in range(NT):
                    r = e.matmul(ps[:, 0:512],
                                 wsl[:, kt * 128:(kt + 1) * 128],
                                 rtb_view[:, kt * 1024 + n * 512:
                                          kt * 1024 + n * 512 + 512],
                                 start=(kt == 0), stop=(kt == NT - 1))
                return r
            mm = P.add("tensor", gmm, deps=[wtok] + list(deps) + toks[-1:]
                       + evs[-2:-1], fifo=True)
            ev = P.add("scalar",
                       (lambda e, n=n, ps=ps:
                        e.activation(fst_view[:, n * 512:n * 512 + 512],
                                     ps[:, 0:512], AF.Copy)),
                       deps=[mm])
            evs.append(ev)
        st = P.add("sync",
                   (lambda e, i=i:
                    e.dma_start(scratch_ap[i * 128:(i + 1) * 128, :],
                                fst_view[:, 0:1024])),
                   deps=evs, ctr=P.ctr("fsd"), n=16)
        toks.append(st)
        wring.used(wi, st)
    return toks



def emit_sl_l0ffn(P, b, dr, pfx, deps_in):
    def qslot(h):
        return ((h % 4) + 4 * (h // 8), (h // 4) % 2)

    def kslot(h):
        return ((h // 4) // 2, (h // 4) % 2)

    def vblk(h):
        return h // 4

    def pair_heads(mi):
        return (2 * mi, 2 * mi + 1)

    ln = emit_layernorm(P, b, b["A4"], b["A1"][:, 0:2048], deps_in)
    rm1 = emit_rmsnorm(P, b, b["A4"], b["A1"],
                       lambda j: b["NORMW"][:, j:j + 1],
                       b["A3"][:, 12352:13376], joined(ln, deps_in))
    gw = b["A3"][:, 13888:13952]
    gwt = P.add("gpsimd",
                (lambda e: e.dma_start(
                    gw.rearrange("p (j m) -> p j m", j=NT),
                    dr[pfx + "gwT"].rearrange("(j p) m -> p j m", p=128))),
                deps=deps_in, ctr=P.ctr("gwd"), n=16)
    wring = WRing(P, b["A3"], 8256, 4, 1024, tag="wA")
    pr = PsumRot([b["PSB"][:, 0:512], b["PSC"][:, 0:512], b["PSD"][:, 0:512]])
    ktoks = emit_proj(P, b, wring, dr[pfx + "wkT"], b["A1"], C, 256, pr,
                      mk_evac_to(P, b["A3"], "scalar"), deps_src=rm1,
                      wdeps=deps_in)
    vt1 = P.add("vector", lambda e: e.memset(b["A3"][:, 4096:8256], 1.0),
                deps=joined(rm1, deps_in))
    vtoks = emit_proj(P, b, wring, dr[pfx + "wvT"], b["A1"], C, 256, pr,
                      mk_evac_vtok(P, b, b["A3"][:, 4096:8256], pair_heads,
                                   b["A3"][:, 13376:13888],
                                   b["A3"][:, 14336:14848]),
                      deps_src=joined(rm1, [vt1]), wdeps=deps_in)
    qtoks = emit_proj(P, b, wring, dr[pfx + "wqT"], b["A1"], C, C, pr,
                      mk_evac_to(P, b["A2"], "scalar"), deps_src=rm1,
                      wdeps=deps_in)
    a_dead = joined(ktoks, vtoks, qtoks)
    cmskt = b["A1"][:, 4096:6144]
    cmt = P.add("gpsimd", lambda e: e.dma_start(cmskt, dr["c_cmskt"]),
                deps=a_dead, ctr=P.ctr("cmd"), n=16)
    at = emit_attn(P, b, b["A2"], b["A3"], b["A3"][:, 4096:8256], b["A2"],
                   qslot, kslot, vblk, True, joined(a_dead, [cmt]),
                   b["A1"][:, 0:4096], cmskt=cmskt)
    wring2 = WRing(P, b["A3"], 8256, 4, 1024, tag="wA")

    def evac_h(mi, ci, pv, deps):
        c0 = mi * S + ci * CH
        e1 = P.add("vector",
                   (lambda e, c0=c0, pv=pv:
                    e.tensor_tensor(b["A4"][:, c0:c0 + CH],
                                    b["A4"][:, c0:c0 + CH], pv, ALU.add)),
                   deps=deps)
        e2 = P.add("vector",
                   (lambda e, c0=c0:
                    e.scalar_tensor_tensor(b["X"][:, c0:c0 + CH],
                                           b["A4"][:, c0:c0 + CH], 0.5,
                                           b["X"][:, c0:c0 + CH],
                                           ALU.mult, ALU.add)),
                   deps=[e1])
        return e2
    pr2 = PsumRot([b["PSB"][:, 0:512], b["PSC"][:, 0:512], b["PSD"][:, 0:512]])
    htoks = emit_proj(P, b, wring2, dr[pfx + "woT"], b["A2"], C, C, pr2,
                      evac_h, deps_src=at, wdeps=joined(at))
    rm2 = emit_rmsnorm(P, b, b["A4"], b["A2"],
                       lambda j: b["NORMW"][:, 8 + j:9 + j],
                       b["A1"][:, 0:1024], joined(htoks))
    esel = b["A1"][:, 8448:10496]
    cbe4 = b["A1"][:, 10496:12544]
    combt = b["A1"][:, 6400:8448]
    cmb = b["A1"][:, 6144:6160]
    est = P.add("gpsimd", lambda e: e.dma_start(esel[0:16, :], dr["c_esel"]),
                deps=joined(htoks), ctr=P.ctr("esd"), n=16)
    g8 = emit_gate8(P, b, b["A2"], gw, b["F32A"][:, 0:64], cmb, combt,
                    joined(rm2, [gwt]))
    wring3 = WRing(P, b["A1"], 0, 4, 1024, tag="wM")
    m8 = emit_moe8(P, b, b["A2"], dr[pfx + "expwT"], esel, combt, cbe4,
                   wring3, joined(rm2), joined(g8, [est]),
                   fe_buf=b["A3"], wdeps=joined(rm2))
    return joined(m8, htoks)


def emit_sl_l0attn(P, b, dr, scr_q, scr_kb, deps_in):
    def slot(h):
        return (h // 2, h % 2)

    def vblk(h):
        return h

    def pair_heads(mi):
        return (2 * mi, 2 * mi + 1)

    ln = emit_layernorm(P, b, b["A4"], b["A3"][:, 0:2048], deps_in)
    wring = WRing(P, b["C8B"], 0, 2, 1024, tag="wC")
    pr = PsumRot([b["PSB"][:, 0:512], b["PSC"][:, 0:512], b["PSD"][:, 0:512]])
    t2t = emit_proj(P, b, wring, dr["l0a_wvaT"], b["A4"], C, 256, pr,
                    mk_evac_to(P, b["A2"], "scalar"), deps_src=ln,
                    wdeps=deps_in)
    vt1 = P.add("vector", lambda e: e.memset(b["A1"][:, 0:A1COLS], 1.0),
                deps=joined(ln, deps_in))
    vtt = emit_proj(P, b, wring, dr["l0a_wvbT"], b["A2"], 256, C, pr,
                    mk_evac_vtok(P, b, b["A1"], pair_heads,
                                 b["A2"][:, 8192:8704],
                                 b["A2"][:, 8704:9216]),
                    deps_src=joined(t2t, [vt1]), wdeps=deps_in)
    t1t = emit_proj(P, b, wring, dr["l0a_wkaT"], b["A4"], C, 256, pr,
                    mk_evac_to(P, b["A2"], "scalar", dst_col0=4096),
                    deps_src=ln, wdeps=deps_in)
    ktt = emit_proj(P, b, wring, scr_kb, b["A2"], 256, C, pr,
                    mk_evac_to(P, b["A3"], "scalar"), deps_src=t1t,
                    src_col0=4096, wdeps=joined(ln, deps_in))
    qtt = emit_proj(P, b, wring, scr_q, b["A4"], C, C, pr,
                    mk_evac_to(P, b["A2"], "scalar"),
                    deps_src=joined(ln, ktt, vtt), wdeps=deps_in)
    at = emit_attn(P, b, b["A2"], b["A3"], b["A1"], b["A2"],
                   slot, slot, vblk, False, joined(qtt, ktt, vtt),
                   b["A4"][:, 0:4096])
    wring2 = WRing(P, b["C8B"], 0, 2, 1024, tag="wC")
    pr2 = PsumRot([b["PSB"][:, 0:512], b["PSC"][:, 0:512], b["PSD"][:, 0:512]])
    ot = emit_proj(P, b, wring2, dr["l0a_woT"], b["A2"], C, C, pr2,
                   mk_evac_resid(P, b), deps_src=at, wdeps=joined(at))
    return joined(ot)


def emit_sl_l1ffn(P, b, dr, gbcb, deps_in):
    esel = b["A1"][:, 6144:8192]
    combt = b["A1"][:, 8192:10240]
    cbe4 = b["A1"][:, 10240:12288]
    sg = b["A1"][:, 12288:13312]
    gw = b["A1"][:, 13312:13440]
    cmb = b["A1"][:, 13440:13456]
    ln = emit_layernorm(P, b, b["A4"], b["A2"][:, 0:2048], deps_in)
    est = P.add("gpsimd", lambda e: e.dma_start(esel[0:16, :], dr["c_esel"]),
                deps=deps_in, ctr=P.ctr("esd"), n=16)
    gwt = P.add("gpsimd",
                (lambda e: e.dma_start(
                    gw.rearrange("p (j m) -> p j m", j=NT),
                    dr["l1f_gwT"].rearrange("(j p) m -> p j m", p=128))),
                deps=deps_in, ctr=P.ctr("gwd"), n=16)
    g16 = emit_gate16(P, b, b["A4"], gw, gbcb[:, 0:16], b["F32A"][:, 0:64],
                      cmb, combt, joined(ln, [gwt, est]))
    wring = WRing(P, b["A1"], 0, 4, 1024, tag="wA")
    w2ring = WRing(P, b["A1"], 4096, 4, 512, tag="wB")
    m16 = emit_moe16(P, b, b["A4"], dr["l1f_w13T"], dr["l1f_w2T"],
                     dr["l1f_w13sT"], dr["l1f_w2sT"], esel, combt, cbe4, sg,
                     b["A2"], wring, w2ring, joined(ln, deps_in), joined(g16))
    return joined(m16)


def emit_sl_l1attn(P, b, dr, deps_in):
    ln = emit_layernorm(P, b, b["A4"], b["A2"][:, 0:2048], deps_in)
    wring = WRing(P, b["A1"], 0, 4, 1024, tag="wA")
    rw = emit_rwkv(P, b, b["A4"], dr["l1a_wkT"], dr["l1a_wvT"],
                   dr["l1a_wrT"], b["NORMW"][:, 16:24], b["A3"], b["A2"],
                   wring, b["F32A"][:, 0:512], joined(ln, deps_in))
    return joined(rw)



W_SHAPES = {
    "l0f_wqT": (C, C), "l0f_wkT": (C, 256), "l0f_wvT": (C, 256),
    "l0f_woT": (C, C), "l0f_gwT": (C, 8), "l0f_expwT": (8, C, C),
    "l0f_nw": (128, 16),
    "l0a_wq": (C, C), "l0a_wkaT": (C, 256), "l0a_wkb": (C, 256),
    "l0a_wvaT": (C, 256), "l0a_wvbT": (256, C), "l0a_woT": (C, C),
    "l1f_gwT": (C, 16), "l1f_gbc": (128, 16), "l1f_w13T": (16, C, C),
    "l1f_w2T": (16, 512, C), "l1f_w13sT": (C, C), "l1f_w2sT": (512, C),
    "l1a_wkT": (C, C), "l1a_wvT": (C, C), "l1a_wrT": (C, C),
    "l1a_td": (128, 8),
}
CONST_NAMES = ("c_idn", "c_onesc", "c_onesf", "c_onescf", "c_cmskt",
               "c_esel", "c_rt")


def _declare(P, cs, kinds):
    nc = P.nc
    dr = {}
    dr["x"] = nc.dram_tensor("x", [BPC, C, S], F32, kind="ExternalInput").ap()
    dr["xo"] = nc.dram_tensor("xo", [BPC, C, S], F32,
                              kind="ExternalOutput").ap()
    names = ["x"]
    for nm, shp in W_SHAPES.items():
        if nm.split("_")[0] in kinds:
            dr[nm] = nc.dram_tensor(nm, list(shp), F32,
                                    kind="ExternalInput").ap()
            names.append(nm)
    need_c = {"l0f": ("c_idn", "c_onesc", "c_onesf", "c_onescf", "c_cmskt",
                      "c_esel"),
              "l0a": ("c_idn", "c_onesc", "c_onesf", "c_onescf", "c_rt"),
              "l1f": ("c_idn", "c_onesc", "c_onesf", "c_onescf", "c_esel"),
              "l1a": ("c_idn", "c_onesc", "c_onesf", "c_onescf")}
    cset = []
    for k in kinds:
        cset += list(need_c[k])
    for nm in CONST_NAMES:
        if nm in cset:
            dr[nm] = nc.dram_tensor(nm, list(cs[nm].shape),
                                    BF16 if cs[nm].dtype == BF16NP else F32,
                                    kind="ExternalInput").ap()
            names.append(nm)
    return dr, names


def _prologues(P, b, dr, kinds):
    st = {}
    toks = []
    if "l0f" in kinds:
        t = P.add("gpsimd",
                  lambda e: e.dma_start(b["NORMW"][:, 0:16], dr["l0f_nw"]),
                  ctr=P.ctr("nwd"), n=16)
        toks.append(t)
    if "l1a" in kinds:
        t = P.add("gpsimd",
                  lambda e: e.dma_start(b["NORMW"][:, 16:24], dr["l1a_td"]),
                  ctr=P.ctr("tdd"), n=16)
        toks.append(t)
    if "l1f" in kinds:
        st["gbcb"] = P.sbuf("GBC", [128, 16], F32)
        t = P.add("gpsimd",
                  lambda e: e.dma_start(st["gbcb"][:, :], dr["l1f_gbc"]),
                  ctr=P.ctr("gbd"), n=16)
        toks.append(t)
    if "l0a" in kinds:
        nc = P.nc
        st["scr_q"] = nc.dram_tensor("scr_q", [C, C], BF16,
                                     kind="Internal").ap()
        st["scr_kb"] = nc.dram_tensor("scr_kb", [256, C], BF16,
                                      kind="Internal").ap()
        rtb = b["A3"][:, 0:8192]
        rtt = P.add("gpsimd",
                    (lambda e: e.dma_start(
                        rtb.rearrange("p (kt m) -> p kt m", kt=NT),
                        dr["c_rt"].rearrange("(kt p) m -> p kt m", p=128))),
                    ctr=P.ctr("rtd"), n=16)
        wringf = WRing(P, b["A1"], 0, 2, 1024, tag="wF")
        fst = b["A1"][:, 2048:3072]
        f1 = emit_fold(P, b, dr["l0a_wq"], rtb, fst, st["scr_q"], NT, wringf,
                       [rtt])
        f2 = emit_fold(P, b, dr["l0a_wkb"], rtb, fst, st["scr_kb"], 2, wringf,
                       [rtt] + f1[-1:])
        toks += f1 + f2
    st["toks"] = toks
    return st


def build_prog(cs, seq):
    P = Prog()
    kinds = sorted(set(seq))
    dr, names = _declare(P, cs, kinds)
    b = make_bufs(P)
    ct = emit_const_loads(P, b, dr)
    st = _prologues(P, b, dr, kinds)
    prev_store = []
    for bi in range(BPC):
        lx = emit_load_x(P, b, dr["x"], bi,
                         deps=joined(ct, prev_store, st["toks"]))
        t = joined(lx, ct, st["toks"], prev_store)
        for kind in seq:
            if kind == "l0f":
                t = emit_sl_l0ffn(P, b, dr, "l0f_", t)
            elif kind == "l0a":
                t = emit_sl_l0attn(P, b, dr, st["scr_q"], st["scr_kb"], t)
            elif kind == "l1f":
                t = emit_sl_l1ffn(P, b, dr, st["gbcb"], t)
            elif kind == "l1a":
                t = emit_sl_l1attn(P, b, dr, t)
        prev_store = emit_store_x(P, b, dr["xo"], bi, deps=t)
    P.realize()
    return P.nc, names



def _t(a):
    return np.ascontiguousarray(np.asarray(a, np.float32).T)


def _qperm_cols():
    perm = np.zeros(C, dtype=np.int64)
    for h in range(NH):
        tile = (h % 4) + 4 * (h // 8)
        par = (h // 4) % 2
        slot = 128 * tile + 64 * par
        perm[slot:slot + 64] = np.arange(64 * h, 64 * h + 64)
    return perm


def host_prep(inputs):
    _ = inputs["L1_ffn_input_ids"]
    _ = inputs["L1_attn_time_first"]
    cs = host_consts()
    qp = _qperm_cols()
    w = {
        "l0f_wqT": np.ascontiguousarray(_t(inputs["L0_ffn_wq"])[:, qp]),
        "l0f_wkT": _t(inputs["L0_ffn_wk"]),
        "l0f_wvT": _t(inputs["L0_ffn_wv"]),
        "l0f_woT": np.ascontiguousarray(_t(inputs["L0_ffn_wo"])[qp, :]),
        "l0f_gwT": _t(inputs["L0_ffn_gate_w"]),
        "l0f_expwT": np.ascontiguousarray(
            np.asarray(inputs["L0_ffn_expert_w"],
                       np.float32).transpose(0, 2, 1)),
        "l0f_nw": np.concatenate([featvec(inputs["L0_ffn_attn_norm_w"]),
                                  featvec(inputs["L0_ffn_ffn_norm_w"])],
                                 axis=1),
        "l0a_wq": np.ascontiguousarray(
            np.asarray(inputs["L0_attn_wq"], np.float32)),
        "l0a_wkaT": _t(inputs["L0_attn_wk_a"]),
        "l0a_wkb": np.ascontiguousarray(
            np.asarray(inputs["L0_attn_wk_b"], np.float32)),
        "l0a_wvaT": _t(inputs["L0_attn_wv_a"]),
        "l0a_wvbT": _t(inputs["L0_attn_wv_b"]),
        "l0a_woT": _t(inputs["L0_attn_wo"]),
        "l1f_gwT": _t(inputs["L1_ffn_gate_w"]),
        "l1f_gbc": np.ascontiguousarray(
            np.broadcast_to(np.asarray(inputs["L1_ffn_gate_b"], np.float32),
                            (128, 16))),
        "l1f_w13T": np.ascontiguousarray(
            np.asarray(inputs["L1_ffn_w13"], np.float32).transpose(0, 2, 1)),
        "l1f_w2T": np.ascontiguousarray(
            np.asarray(inputs["L1_ffn_w2"], np.float32).transpose(0, 2, 1)),
        "l1f_w13sT": _t(inputs["L1_ffn_w13_s"]),
        "l1f_w2sT": _t(inputs["L1_ffn_w2_s"]),
        "l1a_wkT": _t(inputs["L1_attn_Wk"]),
        "l1a_wvT": _t(inputs["L1_attn_Wv"]),
        "l1a_wrT": _t(inputs["L1_attn_Wr"]),
        "l1a_td": featvec(inputs["L1_attn_time_decay"]),
    }
    return cs, w


FUSED = True


def kernel(**inputs):
    cs, w = host_prep(inputs)
    x = np.ascontiguousarray(
        np.asarray(inputs["L0_attn_x"], np.float32).transpose(0, 2, 1))
    shards = [np.ascontiguousarray(x[c * BPC:(c + 1) * BPC])
              for c in range(NCORES)]

    def run(nc, names, xs):
        in_maps = []
        for c in range(NCORES):
            m = {"x": xs[c]}
            for nm in names:
                if nm == "x":
                    continue
                m[nm] = w[nm] if nm in w else cs[nm]
            in_maps.append(m)
        res = run_bass_kernel_spmd(nc, in_maps, core_ids=list(range(NCORES)))
        return [r["xo"] for r in res.results]

    if FUSED:
        nc, names = build_prog(cs, ["l0f", "l0a", "l0f", "l1f", "l1a", "l1f"])
        shards = run(nc, names, shards)
    else:
        progs = {k: build_prog(cs, [k]) for k in ("l0f", "l0a", "l1f", "l1a")}
        for kind in ["l0f", "l0a", "l0f", "l1f", "l1a", "l1f"]:
            nc, names = progs[kind]
            shards = run(nc, names, shards)

    out = np.concatenate(shards, axis=0)
    return np.ascontiguousarray(out.transpose(0, 2, 1)).astype(np.float32)


# version 14
# speedup vs baseline: 1.0493x; 1.0042x over previous
import numpy as np
import ml_dtypes
from contextlib import ExitStack

import concourse.bass as bass
import concourse.mybir as mybir
from concourse.bass_utils import run_bass_kernel_spmd

F32 = mybir.dt.float32
BF16 = mybir.dt.bfloat16
AF = mybir.ActivationFunctionType
ALU = mybir.AluOpType
AX = mybir.AxisListType
BF16NP = ml_dtypes.bfloat16

NCORES = 8
BPC = 4
S = 2048
C = 1024
NT = C // 128
CH = 512
NCH = S // CH
NH = 16
HD = 64
A1COLS = NH * 16 * 65



class Prog:
    ENGS = ("tensor", "vector", "scalar", "gpsimd", "sync")

    def __init__(self):
        self.nc = bass.Bass("TRN2", target_bir_lowering=False)
        self.ops = {e: [] for e in self.ENGS}
        self.ctx = ExitStack()
        self.ctrs = {}
        for e in self.ENGS:
            self._mkctr(e)
        for d in ("gdma", "sdma", "odma", "xdma", "c8d"):
            self._mkctr(d)
        self.ring_busy = {}

    def _mkctr(self, name):
        h = self.ctx.enter_context(self.nc.semaphore(name))
        self.ctrs[name] = [h, 0]

    def ctr(self, name):
        if name not in self.ctrs:
            self._mkctr(name)
        return name

    def sbuf(self, name, shape, dt):
        return self.ctx.enter_context(self.nc.sbuf_tensor(name, shape, dt))

    def psum(self, name, shape, dt=F32):
        return self.ctx.enter_context(self.nc.psum_tensor(name, shape, dt))

    def add(self, eng, build, deps=(), ctr=None, n=1, fifo=False):
        cname = ctr or eng
        h, cnt = self.ctrs[cname]
        waits = {}
        for d in deps:
            if d is None:
                continue
            dc, dv = d
            if dv <= 0 or (fifo and dc == eng):
                continue
            waits[dc] = max(waits.get(dc, 0), dv)
        wl = [(self.ctrs[dc][0], dv) for dc, dv in waits.items()]
        self.ctrs[cname][1] = cnt + n
        self.ops[eng].append((wl, build, (h, n)))
        return (cname, cnt + n)

    def realize(self):
        finals = [(self.ctrs[n][0], self.ctrs[n][1]) for n in self.ctrs
                  if n not in self.ENGS and self.ctrs[n][1] > 0]
        blk = self.ctx.enter_context(self.nc.Block())
        for eng in self.ENGS:
            ops = self.ops[eng]

            def body(e, ops=ops, eng=eng):
                for waits, build, inc in ops:
                    for (h, v) in waits:
                        e.wait_ge(h, v)
                    r = build(e)
                    if r is not None:
                        r.then_inc(*inc)
                if eng == "sync":
                    for h, v in finals:
                        e.wait_ge(h, v)
            getattr(blk, eng)(body)
        return self.nc


def joined(*tok_lists):
    best = {}
    for tl in tok_lists:
        for t in tl:
            if t is None:
                continue
            c, v = t
            best[c] = max(best.get(c, 0), v)
    return [(c, v) for c, v in best.items()]


class PsumRot:
    def __init__(self, views):
        self.views = views
        self.free = [None] * len(views)
        self.i = 0

    def next(self):
        i = self.i
        self.i = (self.i + 1) % len(self.views)
        return i, self.views[i], ([self.free[i]] if self.free[i] else [])

    def done(self, i, tok):
        self.free[i] = tok



def host_consts():
    cs = {}
    cs["c_idn"] = np.eye(128, dtype=BF16NP)
    cs["c_ones2"] = np.ones((128, 128), dtype=BF16NP)
    cs["c_onesc"] = np.ones((128, 1), dtype=BF16NP)
    cs["c_onesf"] = np.ones((1, 128), dtype=np.float32)
    cs["c_onescf"] = np.ones((128, 1), dtype=np.float32)
    kp = np.arange(128)[:, None]
    q = np.arange(512)[None, :]
    m = np.stack([np.where(q >= 128 * r + kp, 0.0, -1e30) for r in range(4)], 1)
    cs["c_cmskt"] = np.ascontiguousarray(m.reshape(128, 2048)).astype(BF16NP)
    es = np.zeros((16, 16, 128), dtype=np.float32)
    for e in range(16):
        es[e, e, :] = 1.0
    cs["c_esel"] = np.ascontiguousarray(es.transpose(0, 1, 2).reshape(16, 2048)
                                        ).astype(BF16NP)
    R = np.eye(C, dtype=np.float64)
    inv = 10000.0 ** (-np.arange(0, 16, 2) / 64.0)
    for h in range(NH):
        for j in range(8):
            ct, st = np.cos(h * inv[j]), np.sin(h * inv[j])
            a, b = 64 * h + 2 * j, 64 * h + 2 * j + 1
            R[a, a] = ct
            R[a, b] = -st
            R[b, a] = st
            R[b, b] = ct
    cs["c_rt"] = np.ascontiguousarray(R.T).astype(np.float32)
    return cs


def featvec(v):
    return np.ascontiguousarray(np.asarray(v, np.float32).reshape(NT, 128).T)



def make_bufs(P):
    b = {}
    b["X"] = P.sbuf("X", [128, NT * S], F32)
    b["A1"] = P.sbuf("A1", [128, A1COLS], BF16)
    b["A2"] = P.sbuf("A2", [128, NT * S], BF16)
    b["A3"] = P.sbuf("A3", [128, NT * S], BF16)
    b["A4"] = P.sbuf("A4", [128, NT * S], BF16)
    b["F32A"] = P.sbuf("F32A", [128, 1024], F32)
    b["ROWS"] = P.sbuf("ROWS", [1, 1536], F32)
    b["C8B"] = P.sbuf("C8B", [128, 2048], BF16)
    b["COL"] = P.sbuf("COL", [128, 40], F32)
    b["CBC"] = P.sbuf("CBC", [128, 16], F32)
    b["IDN"] = P.sbuf("IDN", [128, 128], BF16)
    b["ONESC"] = P.sbuf("ONESC", [128, 1], BF16)
    b["ONESF"] = P.sbuf("ONESF", [1, 128], F32)
    b["ONESCF"] = P.sbuf("ONESCF", [128, 1], F32)
    b["NORMW"] = P.sbuf("NORMW", [128, 24], F32)
    b["EPSC"] = P.sbuf("EPSC", [1, 2], F32)
    b["PSA"] = P.psum("PSA", [128, 2048], F32)
    b["PSB"] = P.psum("PSB", [128, 512], F32)
    b["PSC"] = P.psum("PSC", [128, 512], F32)
    b["PSD"] = P.psum("PSD", [128, 512], F32)
    b["PTP"] = P.psum("PTP", [128, 512], BF16)
    return b


def xt(b, j):
    return b["X"][:, j * S:(j + 1) * S]


def emit_const_loads(P, b, d, extra=()):
    def g(e):
        sem = P.ctrs["sdma"][0]
        e.dma_start(b["IDN"][:, :], d["c_idn"]).then_inc(sem, 16)
        e.dma_start(b["ONESC"][:, :], d["c_onesc"]).then_inc(sem, 16)
        e.dma_start(b["ONESF"][:, :], d["c_onesf"]).then_inc(sem, 16)
        e.dma_start(b["ONESCF"][:, :], d["c_onescf"]).then_inc(sem, 16)
        for dst, src in extra:
            e.dma_start(dst, src).then_inc(sem, 16)
        return None
    t1 = P.add("sync", g, ctr="sdma", n=16 * (4 + len(extra)))

    def ge(e):
        e.memset(b["EPSC"][0:1, 0:1], 1e-5)
        return e.memset(b["EPSC"][0:1, 1:2], 1e-6)
    t2 = P.add("vector", ge)
    return [t1, t2]


def emit_load_x(P, b, x_ap, bi, deps=()):
    src = x_ap[bi].rearrange("(j p) t -> p j t", p=128)
    dst = b["X"][:, :].rearrange("p (j t) -> p j t", t=S)
    return [P.add("sync", lambda e: e.dma_start(dst, src), deps=deps,
                  ctr="xdma", n=16)]


def emit_store_x(P, b, xo_ap, bi, deps=()):
    dst = xo_ap[bi].rearrange("(j p) t -> p j t", p=128)
    src = b["X"][:, :].rearrange("p (j t) -> p j t", t=S)
    return [P.add("sync", lambda e: e.dma_start(dst, src), deps=deps,
                  ctr="odma", n=16)]



class WRing:

    def __init__(self, P, arena, base_col, nslots, cols, tag="w"):
        self.P = P
        self.tag = tag
        self.slots = [arena[:, base_col + i * cols: base_col + (i + 1) * cols]
                      for i in range(nslots)]
        self.n = nslots
        self.i = 0

    def load(self, dram_slice, ktiles, deps=()):
        P = self.P
        i = self.i
        self.i = (self.i + 1) % self.n
        sl = self.slots[i]
        cast = dram_slice.tensor.dtype != BF16
        cname = P.ctr(f"{self.tag}{'g' if cast else 's'}{i}")
        key = (self.tag, i)
        d = list(deps) + [P.ring_busy.get(key)]
        mcols = dram_slice.shape[-1]
        src = dram_slice.rearrange("(kt p) m -> p kt m", p=128)
        dst = sl[:, 0:ktiles * mcols].rearrange("p (kt m) -> p kt m", kt=ktiles)
        eng = "gpsimd" if cast else "sync"
        tok = P.add(eng, (lambda e: e.dma_start(dst, src)), deps=d,
                    ctr=cname, n=16)
        return sl, key, tok

    def used(self, key, tok):
        self.P.ring_busy[key] = tok



def emit_proj(P, b, wring, wt_dram, src_buf, kdim, mdim, prot, evac,
              deps_src=(), src_stride=S, src_col0=0, chunks=None, wdeps=()):
    ktiles = kdim // 128
    mtiles = mdim // 128
    chunks = list(range(NCH)) if chunks is None else chunks
    etoks = []
    mmtok = None
    for mi in range(mtiles):
        wsl, wi, wtok = wring.load(wt_dram[:, mi * 128:(mi + 1) * 128], ktiles,
                                   deps=wdeps)
        for ci in chunks:
            si, pv, ftoks = prot.next()

            def g(e, wsl=wsl, pv=pv, ci=ci, ktiles=ktiles):
                last = None
                for kt in range(ktiles):
                    c0 = src_col0 + kt * src_stride + ci * CH
                    last = e.matmul(pv, wsl[:, kt * 128:(kt + 1) * 128],
                                    src_buf[:, c0:c0 + CH],
                                    start=(kt == 0), stop=(kt == ktiles - 1))
                return last
            mmtok = P.add("tensor", g, deps=[wtok] + list(deps_src) + ftoks,
                          fifo=True)
            et = evac(mi, ci, pv, [mmtok])
            prot.done(si, et)
            etoks.append(et)
        wring.used(wi, mmtok)
    return etoks


def mk_evac_to(P, dst_buf, eng="scalar", dst_stride=S, dst_col0=0):
    def evac(mi, ci, pv, deps):
        c0 = dst_col0 + mi * dst_stride + ci * CH
        dst = dst_buf[:, c0:c0 + CH]
        if eng == "scalar":
            return P.add("scalar", lambda e: e.activation(dst, pv, AF.Copy),
                         deps=deps)
        return P.add("vector", lambda e: e.tensor_copy(dst, pv), deps=deps)
    return evac


def mk_evac_resid(P, b, scale=None, extra=()):
    def evac(mi, ci, pv, deps):
        dst = b["X"][:, mi * S + ci * CH: mi * S + ci * CH + CH]
        if scale is None:
            return P.add("vector",
                         lambda e: e.tensor_tensor(dst, dst, pv, ALU.add),
                         deps=list(deps) + list(extra))
        return P.add("vector",
                     lambda e: e.scalar_tensor_tensor(dst, pv, float(scale),
                                                      dst, ALU.mult, ALU.add),
                     deps=list(deps) + list(extra))
    return evac



def emit_layernorm(P, b, dst_buf, arena, deps, eps=1e-5):
    xq = [arena[:, 1024:1536], arena[:, 1536:2048]]
    out = []
    sum_free = []
    row_free = []
    rowvar_free = {}
    xq_free = {0: None, 1: None}
    for ci in range(NCH):
        negmu = b["ROWS"][0:1, 0:512]
        invr = b["ROWS"][0:1, 512:1024]
        msh = b["ROWS"][0:1, 1024:1536]
        sumx = b["PSA"][0:1, 0:512]
        sumq = b["PSA"][0:1, 512:1024]
        mmx = mmq = None
        for j in range(NT):
            sl = j % 2
            mmx = P.add("tensor",
                        (lambda e, j=j, ci=ci:
                         e.matmul(sumx, b["ONESCF"][:, 0:1],
                                  xt(b, j)[:, ci * CH:ci * CH + CH],
                                  start=(j == 0), stop=(j == NT - 1))),
                        deps=list(deps) + sum_free, fifo=True)
            sq = P.add("scalar",
                       (lambda e, sl=sl, j=j, ci=ci:
                        e.activation(xq[sl],
                                     xt(b, j)[:, ci * CH:ci * CH + CH],
                                     AF.Square)),
                       deps=list(deps) + [xq_free[sl]] + sum_free)
            mmq = P.add("tensor",
                        (lambda e, sl=sl, j=j:
                         e.matmul(sumq, b["ONESC"][:, 0:1], xq[sl],
                                  start=(j == 0), stop=(j == NT - 1))),
                        deps=[sq] + sum_free, fifo=True)
            xq_free[sl] = mmq
        t1 = P.add("vector",
                   (lambda e, negmu=negmu:
                    e.tensor_scalar(negmu, sumx, -1.0 / C, None, ALU.mult)),
                   deps=[mmx] + [rowvar_free.get(ci % 2)])
        t2 = P.add("vector",
                   (lambda e, negmu=negmu, msh=msh:
                    e.tensor_tensor(msh, negmu, negmu, ALU.mult)),
                   deps=[t1])
        t3 = P.add("vector",
                   (lambda e, invr=invr, msh=msh:
                    e.scalar_tensor_tensor(invr, sumq, 1.0 / C, msh,
                                           ALU.mult, ALU.subtract)),
                   deps=[t2, mmq])
        sum_free = [t1, t3]
        t4 = P.add("scalar",
                   (lambda e, invr=invr, msh=msh:
                    e.activation(msh, invr, AF.Ln, bias=b["EPSC"][0:1, 0:1])),
                   deps=[t3])
        t5 = P.add("scalar",
                   (lambda e, invr=invr, msh=msh:
                    e.activation(invr, msh, AF.Exp, scale=-0.5)),
                   deps=[t4])
        t6 = P.add("vector",
                   (lambda e, negmu=negmu, invr=invr, msh=msh:
                    e.tensor_tensor(msh, negmu, invr, ALU.mult)),
                   deps=[t5])
        bcA = P.add("tensor",
                    (lambda e, invr=invr:
                     e.matmul(b["PSB"][:, 0:512], b["ONESF"][0:1, :],
                              invr, start=True, stop=True)),
                    deps=[t5] + row_free)
        bcB = P.add("tensor",
                    (lambda e, msh=msh:
                     e.matmul(b["PSC"][:, 0:512], b["ONESF"][0:1, :],
                              msh, start=True, stop=True)),
                    deps=[t6] + row_free)
        rowvar_free[ci % 2] = bcB
        rf = []
        for j in range(NT):
            tt1 = P.add("vector",
                        (lambda e, j=j, ci=ci:
                         e.tensor_tensor(b["F32A"][:, 512:1024],
                                         xt(b, j)[:, ci * CH:ci * CH + CH],
                                         b["PSB"][:, 0:512], ALU.mult)),
                        deps=[bcA] + ([rf[-1]] if rf else []))
            tt2 = P.add("vector",
                        (lambda e, j=j, ci=ci:
                         e.tensor_tensor(
                             dst_buf[:, j * S + ci * CH:j * S + ci * CH + CH],
                             b["F32A"][:, 512:1024], b["PSC"][:, 0:512],
                             ALU.add)),
                        deps=[bcB, tt1])
            rf.append(tt2)
            out.append(tt2)
        row_free = [rf[-1]]
    return out


def emit_rmsnorm(P, b, src_buf, dst_buf, nw_col, arena, deps, eps=1e-6):
    xq = [arena[:, 0:512], arena[:, 512:1024]]
    out = []
    sum_free = []
    row_free = []
    rowvar_free = {}
    xq_free = {0: None, 1: None}
    for ci in range(NCH):
        invr = b["ROWS"][0:1, 512:1024]
        msh = b["ROWS"][0:1, 1024:1536]
        sumq = b["PSA"][0:1, 512:1024]
        mmq = None
        for j in range(NT):
            sl = j % 2
            sq = P.add("scalar",
                       (lambda e, j=j, ci=ci, sl=sl:
                        e.activation(
                            xq[sl],
                            src_buf[:, j * S + ci * CH:j * S + ci * CH + CH],
                            AF.Square)),
                       deps=list(deps) + [xq_free[sl]] + sum_free)
            mmq = P.add("tensor",
                        (lambda e, sl=sl, j=j:
                         e.matmul(sumq, b["ONESC"][:, 0:1], xq[sl],
                                  start=(j == 0), stop=(j == NT - 1))),
                        deps=[sq] + sum_free, fifo=True)
            xq_free[sl] = mmq
        t1 = P.add("vector",
                   (lambda e, invr=invr:
                    e.tensor_scalar(invr, sumq, 1.0 / C, None, ALU.mult)),
                   deps=[mmq] + [rowvar_free.get(ci % 2)])
        sum_free = [t1]
        t2 = P.add("scalar",
                   (lambda e, invr=invr, msh=msh:
                    e.activation(msh, invr, AF.Ln, bias=b["EPSC"][0:1, 1:2])),
                   deps=[t1])
        t3 = P.add("scalar",
                   (lambda e, invr=invr, msh=msh:
                    e.activation(invr, msh, AF.Exp, scale=-0.5)),
                   deps=[t2])
        bcA = P.add("tensor",
                    (lambda e, invr=invr:
                     e.matmul(b["PSC"][:, 0:512], b["ONESF"][0:1, :],
                              invr, start=True, stop=True)),
                    deps=[t3] + row_free)
        rowvar_free[ci % 2] = bcA
        rf = []
        for j in range(NT):
            tt = P.add("vector",
                       (lambda e, j=j, ci=ci:
                        e.scalar_tensor_tensor(
                            dst_buf[:, j * S + ci * CH:j * S + ci * CH + CH],
                            src_buf[:, j * S + ci * CH:j * S + ci * CH + CH],
                            nw_col(j), b["PSC"][:, 0:512],
                            ALU.mult, ALU.mult)),
                       deps=[bcA] + ([rf[-1]] if rf else []))
            rf.append(tt)
            out.append(tt)
        row_free = [rf[-1]]
    return out



def mk_evac_vtok(P, b, vt_buf, pair_heads, vs_view, ts_view=None):
    state = {"ptp_free": None, "vs_free": None}

    def evac(mi, ci, pv, deps):
        hA, hB = pair_heads(mi)
        cp = P.add("scalar", lambda e: e.activation(vs_view, pv, AF.Copy),
                   deps=list(deps) + [state["vs_free"]])
        trs = []
        for blk in range(4):
            tr = P.add("tensor",
                       (lambda e, blk=blk:
                        e.transpose(b["PTP"][:, blk * 128:blk * 128 + 128],
                                    vs_view[:, blk * 128:blk * 128 + 128],
                                    b["IDN"][:, :])),
                       deps=[cp, state["ptp_free"]], fifo=True)
            trs.append(tr)
        state["vs_free"] = trs[-1]
        kt0 = ci * 4
        last = None
        for blk in range(4):
            kt = kt0 + blk
            cA = 65 * (hA * 16 + kt)
            cB = 65 * (hB * 16 + kt)

            def gv(e, cA=cA, cB=cB, blk=blk):
                e.tensor_copy(vt_buf[:, cA:cA + 64],
                              b["PTP"][:, blk * 128:blk * 128 + 64])
                return e.tensor_copy(vt_buf[:, cB:cB + 64],
                                     b["PTP"][:, blk * 128 + 64:blk * 128 + 128])
            last = P.add("vector", gv, deps=[trs[3]])
        state["ptp_free"] = last
        return last
    return evac


def emit_attn(P, b, q_buf, k_buf, vt_buf, o_buf, qslot, kslot, vblk, causal,
              deps, arena, cmskt=None):
    sqst = arena[:, 0:2048]
    ptst = arena[:, 2048:5120]
    rcrow = b["ROWS"][0:1, 512:1024]
    srot = PsumRot([b["PSA"][:, i * 512:(i + 1) * 512] for i in range(4)])
    orot = PsumRot([b["PSB"][:, 0:512], b["PSD"][:, 0:512]])
    out = []
    rows = [b["PSC"][0:1, 0:512], b["PSD"][0:1, 0:512]]
    rowtok = [[], []]
    sq_free = []
    kvs = {}
    for h in range(NH):
        kvs.setdefault(kslot(h), []).append(h)
    kcol = {}
    acc = []
    prev_kx = []
    for kv, heads in kvs.items():
        kt_, kp_ = kv
        ksl = k_buf[64 * kp_:64 * kp_ + 64, kt_ * S:(kt_ + 1) * S]
        sqk = sqst[64 * kp_:64 * kp_ + 64, 0:2048]
        sq1 = P.add("scalar",
                    (lambda e, sqk=sqk, ksl=ksl:
                     e.activation(sqk, ksl, AF.Square)),
                    deps=list(deps) + sq_free)
        kms = []
        cell = b["COL"][0:1, 1 + len(kcol):2 + len(kcol)]
        for ci in range(NCH):
            rv = rows[ci % 2]
            km = P.add("tensor",
                       (lambda e, ci=ci, sqk=sqk, kp_=kp_, rv=rv:
                        e.matmul(rv, b["ONESC"][64 * kp_:64 * kp_ + 64, 0:1],
                                 sqk[:, ci * 512:ci * 512 + 512],
                                 start=True, stop=True)),
                       deps=[sq1] + rowtok[ci % 2])
            kc = P.add("vector",
                       (lambda e, ci=ci, rv=rv:
                        e.tensor_reduce(b["COL"][0:1, 20 + ci:21 + ci], rv,
                                        AX.X, ALU.max)),
                       deps=[km] + prev_kx)
            rowtok[ci % 2] = [kc]
            kms.append(kc)
        kx = P.add("vector",
                   (lambda e, cell=cell:
                    e.tensor_reduce(cell, b["COL"][0:1, 20:24], AX.X, ALU.max)),
                   deps=kms)
        kcol[kv] = (cell, kx)
        sq_free = kms
        prev_kx = [kx]
        acc.append(kx)
    qcell = b["COL"][0:1, 0:1]
    cbc_toks = {}
    for h in range(NH):
        qt, qp = qslot(h)
        qsl = q_buf[64 * qp:64 * qp + 64, qt * S:(qt + 1) * S]
        sqq = sqst[64 * qp:64 * qp + 64, 0:2048]
        sq2 = P.add("scalar",
                    (lambda e, sqq=sqq, qsl=qsl:
                     e.activation(sqq, qsl, AF.Square)),
                    deps=list(deps) + sq_free)
        qms = []
        for ci in range(NCH):
            rv = rows[ci % 2]
            qm = P.add("tensor",
                       (lambda e, ci=ci, sqq=sqq, qp=qp, rv=rv:
                        e.matmul(rv, b["ONESC"][64 * qp:64 * qp + 64, 0:1],
                                 sqq[:, ci * 512:ci * 512 + 512],
                                 start=True, stop=True)),
                       deps=[sq2] + rowtok[ci % 2])
            qc = P.add("vector",
                       (lambda e, ci=ci, rv=rv:
                        e.tensor_reduce(b["COL"][0:1, 24 + ci:25 + ci], rv,
                                        AX.X, ALU.max)),
                       deps=[qm, cbc_toks.get(h - 1)])
            rowtok[ci % 2] = [qc]
            qms.append(qc)
        sq_free = qms
        kcell, kxt = kcol[kslot(h)]
        t1 = P.add("vector",
                   (lambda e: e.tensor_reduce(qcell, b["COL"][0:1, 24:28],
                                              AX.X, ALU.max)),
                   deps=qms + [cbc_toks.get(h - 1)])
        t2 = P.add("vector",
                   (lambda e, kcell=kcell:
                    e.tensor_tensor(qcell, qcell, kcell, ALU.mult)),
                   deps=[t1, kxt])
        t3 = P.add("scalar",
                   lambda e: e.activation(qcell, qcell, AF.Sqrt),
                   deps=[t2])
        t4 = P.add("vector",
                   (lambda e: e.tensor_scalar(qcell, qcell, -0.125, None,
                                              ALU.mult)),
                   deps=[t3])
        t5 = P.add("tensor",
                   (lambda e: e.matmul(b["PSC"][:, 0:1], b["ONESF"][0:1, :],
                                       qcell, start=True, stop=True)),
                   deps=[t4] + rowtok[0] + rowtok[1])
        cb = P.add("vector",
                   (lambda e, h=h: e.tensor_copy(b["CBC"][:, h:h + 1],
                                                 b["PSC"][:, 0:1])),
                   deps=[t5])
        rowtok[0] = rowtok[1] = [cb]
        cbc_toks[h] = cb
        acc.append(cb)
    prep = joined(acc, [cbc_toks[NH - 1]])
    pend = None
    pt_free = {0: [], 1: [], 2: []}

    def emit_pv(prevp):
        ex, kts, pp, h, ov, nkt, ofree = prevp
        last = None
        for i, kt in enumerate(kts):
            vcol = 65 * (vblk(h) * 16 + kt)
            pt = ptst[:, pp * 1024 + i * 512: pp * 1024 + i * 512 + 512]
            last = P.add("tensor",
                         (lambda e, kt=kt, nkt=nkt, vcol=vcol, ov=ov, pt=pt:
                          e.matmul(ov[0:65, :], vt_buf[:, vcol:vcol + 65],
                                   pt, start=(kt == 0),
                                   stop=(kt == nkt - 1))),
                         deps=[ex] + (ofree if kt == 0 else []), fifo=True)
        pt_free[pp] = [last]
        return last

    norm_state = {"prev": []}

    def emit_norm(pend):
        ov, oi, qt, qp, ci, _pv, _nf = pend
        rc = P.add("vector",
                   (lambda e, ov=ov: e.reciprocal(rcrow, ov[64:65, :])),
                   deps=[pend[5]] + pend[6] + norm_state["prev"])
        bc = P.add("tensor",
                   lambda e: e.matmul(b["PSC"][0:64, 0:512],
                                      b["ONESF"][0:1, 0:64], rcrow,
                                      start=True, stop=True),
                   deps=[rc])
        cbe = P.add("vector",
                    lambda e: e.tensor_copy(b["F32A"][0:64, 0:512],
                                            b["PSC"][0:64, 0:512]),
                    deps=[bc] + norm_state["prev"])
        oc0 = qt * S + ci * CH
        fin = P.add("vector",
                    (lambda e, ov=ov, oc0=oc0, qp=qp:
                     e.tensor_tensor(
                         o_buf[64 * qp:64 * qp + 64, oc0:oc0 + CH],
                         ov[0:64, :], b["F32A"][0:64, 0:512], ALU.mult)),
                    deps=[cbe])
        orot.done(oi, fin)
        out.append(fin)
        norm_state["prev"] = [rc, fin]
        return [cbe, fin]

    nfree = []
    for h in range(NH):
        qt, qp = qslot(h)
        kt_, kp_ = kslot(h)
        qsl = q_buf[64 * qp:64 * qp + 64, qt * S:(qt + 1) * S]
        ksl = k_buf[64 * kp_:64 * kp_ + 64, kt_ * S:(kt_ + 1) * S]
        for ci in range(NCH):
            nkt = (4 * ci + 4) if causal else 16
            oi, ov, ofree = orot.next()
            pv_last = None
            pendpv = []
            for p in range(nkt // 2):
                kts = (2 * p, 2 * p + 1)
                mdep = []
                svs = []
                s0 = None
                for kt in kts:
                    si, sv, sfree = srot.next()
                    if s0 is None:
                        s0 = (si, sv)
                    svs.append(si)

                    def gmm(e, sv=sv, kt=kt, ci=ci, qsl=qsl, ksl=ksl):
                        return e.matmul(sv, ksl[:, kt * 128:kt * 128 + 128],
                                        qsl[:, ci * CH:ci * CH + CH],
                                        start=True, stop=True)
                    mm = P.add("tensor", gmm, deps=sfree + list(deps) + prep,
                               fifo=True)
                    if causal and kt >= 4 * ci:
                        r = kt - 4 * ci
                        mk = P.add("vector",
                                   (lambda e, sv=sv, r=r:
                                    e.tensor_tensor(
                                        sv, sv,
                                        cmskt[:, r * 512:r * 512 + 512],
                                        ALU.add)),
                                   deps=[mm])
                        mdep.append(mk)
                    else:
                        mdep.append(mm)
                pp = p % 3
                pt2 = ptst[:, pp * 1024:pp * 1024 + 1024]
                sv2 = b["PSA"][:, s0[0] * 512:s0[0] * 512 + 1024]
                ex = P.add("scalar",
                           (lambda e, sv2=sv2, pt2=pt2, h=h:
                            e.activation(pt2, sv2, AF.Exp, scale=0.125,
                                         bias=b["CBC"][:, h:h + 1])),
                           deps=mdep + pt_free[pp] + prep)
                for si in svs:
                    srot.done(si, ex)
                pendpv.append((ex, kts, pp, h, ov, nkt, ofree))
                if len(pendpv) > 2:
                    pv_last = emit_pv(pendpv.pop(0))
            while pendpv:
                pv_last = emit_pv(pendpv.pop(0))
            this = (ov, oi, qt, qp, ci, pv_last, nfree)
            if pend is not None:
                nfree = emit_norm(pend)
            pend = this
    if pend is not None:
        emit_norm(pend)
    return out



def emit_gate8(P, b, f_buf, gw_view, gsc, cmb_view, combt_view, deps,
               scale=0.5):
    out = []
    L = gsc[:, 0:8]
    T = gsc[:, 8:16]
    E = gsc[:, 16:24]
    m1 = b["COL"][:, 1:2]
    nm1 = b["COL"][:, 2:3]
    m2 = b["COL"][:, 3:4]
    den = b["COL"][:, 4:5]
    rc = b["COL"][:, 5:6]
    prev = []
    for tc in range(16):
        def gmm(e, tc=tc):
            last = None
            for j in range(NT):
                last = e.matmul(b["PSA"][:, 1024:1040][:, 0:8],
                                f_buf[:, j * S + tc * 128:j * S + tc * 128 + 128],
                                gw_view[:, j * 8:(j + 1) * 8],
                                start=(j == 0), stop=(j == NT - 1))
            return last
        mm = P.add("tensor", gmm, deps=list(deps) + prev, fifo=True)
        t0 = P.add("vector",
                   lambda e: e.tensor_copy(L, b["PSA"][:, 1024:1032]),
                   deps=[mm])
        t1 = P.add("vector", lambda e: e.tensor_reduce(m1, L, AX.X, ALU.max),
                   deps=[t0])
        t1b = P.add("vector",
                    lambda e: e.tensor_scalar(nm1, m1, -1.0, None, ALU.mult),
                    deps=[t1])
        t2 = P.add("vector",
                   lambda e: e.tensor_scalar(T, L, m1[0:128, 0:1], None,
                                             ALU.is_equal),
                   deps=[t1])
        t3 = P.add("vector",
                   lambda e: e.scalar_tensor_tensor(T, T, -1e30, L, ALU.mult,
                                                    ALU.add),
                   deps=[t2])
        t4 = P.add("vector", lambda e: e.tensor_reduce(m2, T, AX.X, ALU.max),
                   deps=[t3])
        t5 = P.add("vector",
                   lambda e: e.tensor_scalar(T, L, m2[0:128, 0:1], None,
                                             ALU.is_ge),
                   deps=[t4])
        t6 = P.add("scalar",
                   lambda e: e.activation(E, L, AF.Exp, bias=nm1[0:128, 0:1]),
                   deps=[t1b, t0])
        t7 = P.add("vector", lambda e: e.tensor_tensor(E, E, T, ALU.mult),
                   deps=[t5, t6])
        t8 = P.add("vector", lambda e: e.tensor_reduce(den, E, AX.X, ALU.add),
                   deps=[t7])
        t9 = P.add("vector", lambda e: e.reciprocal(rc, den), deps=[t8])
        t10 = P.add("vector",
                    (lambda e: e.tensor_scalar(cmb_view[:, 0:8], E,
                                               rc[0:128, 0:1], float(scale),
                                               ALU.mult, ALU.mult)),
                    deps=[t9])
        tr = P.add("tensor",
                   (lambda e: e.transpose(b["PTP"][0:8, 0:128],
                                          cmb_view[:, 0:8], b["IDN"][:, :])),
                   deps=[t10] + prev)
        ev = P.add("vector",
                   (lambda e, tc=tc:
                    e.tensor_copy(combt_view[0:8, tc * 128:tc * 128 + 128],
                                  b["PTP"][0:8, 0:128])),
                   deps=[tr])
        prev = [ev]
        out.append(ev)
    return out


def emit_gate16(P, b, f_buf, gw_view, gbc_view, gsc, cmb_view, combt_view,
                deps, route_scale=2.5, sub_scale=0.5):
    out = []
    L = gsc[:, 0:16]
    Bv = gsc[:, 16:32]
    T = gsc[:, 32:48]
    E = gsc[:, 48:64]
    mx = b["COL"][:, 1:2]
    den = b["COL"][:, 4:5]
    rc = b["COL"][:, 5:6]
    prev = []
    for tc in range(16):
        def gmm(e, tc=tc):
            last = None
            for j in range(NT):
                last = e.matmul(b["PSA"][:, 1024:1040],
                                f_buf[:, j * S + tc * 128:j * S + tc * 128 + 128],
                                gw_view[:, j * 16:(j + 1) * 16],
                                start=(j == 0), stop=(j == NT - 1))
            return last
        mm = P.add("tensor", gmm, deps=list(deps) + prev, fifo=True)
        t0 = P.add("scalar",
                   lambda e: e.activation(L, b["PSA"][:, 1024:1040], AF.Sigmoid),
                   deps=[mm])
        t1 = P.add("vector", lambda e: e.tensor_tensor(Bv, L, gbc_view, ALU.add),
                   deps=[t0])
        tsel = t1
        for r in range(4):
            ta = P.add("vector",
                       lambda e: e.tensor_reduce(mx, Bv, AX.X, ALU.max),
                       deps=[tsel])
            if r < 3:
                tb = P.add("vector",
                           (lambda e:
                            e.tensor_scalar(T, Bv, mx[0:128, 0:1], None,
                                            ALU.is_equal)),
                           deps=[ta])
                tsel = P.add("vector",
                             (lambda e:
                              e.scalar_tensor_tensor(Bv, T, -1e30, Bv,
                                                     ALU.mult, ALU.add)),
                             deps=[tb])
            else:
                tsel = ta
        t2 = P.add("vector", lambda e: e.tensor_tensor(Bv, L, gbc_view, ALU.add),
                   deps=[tsel])
        t3 = P.add("vector",
                   lambda e: e.tensor_scalar(T, Bv, mx[0:128, 0:1], None,
                                             ALU.is_ge),
                   deps=[t2])
        t4 = P.add("vector", lambda e: e.tensor_tensor(E, L, T, ALU.mult),
                   deps=[t3])
        t5 = P.add("vector", lambda e: e.tensor_reduce(den, E, AX.X, ALU.add),
                   deps=[t4])
        t6 = P.add("vector", lambda e: e.reciprocal(rc, den), deps=[t5])
        t7 = P.add("vector",
                   (lambda e:
                    e.tensor_scalar(cmb_view[:, 0:16], E, rc[0:128, 0:1],
                                    float(route_scale * sub_scale),
                                    ALU.mult, ALU.mult)),
                   deps=[t6])
        tr = P.add("tensor",
                   (lambda e: e.transpose(b["PTP"][0:16, 0:128],
                                          cmb_view[:, 0:16], b["IDN"][:, :])),
                   deps=[t7] + prev)
        ev = P.add("vector",
                   (lambda e, tc=tc:
                    e.tensor_copy(combt_view[0:16, tc * 128:tc * 128 + 128],
                                  b["PTP"][0:16, 0:128])),
                   deps=[tr])
        prev = [ev]
        out.append(ev)
    return out


def emit_comb_bc(P, b, esel_view, combt_view, nexp, e_idx, cbe4_view, deps,
                 ps_view=None):
    pv = b["PSD"][:, 0:512] if ps_view is None else ps_view
    toks = []
    for ci in range(NCH):
        mm = P.add("tensor",
                   (lambda e, ci=ci:
                    e.matmul(pv,
                             esel_view[0:nexp, e_idx * 128:e_idx * 128 + 128],
                             combt_view[0:nexp, ci * CH:ci * CH + CH],
                             start=True, stop=True)),
                   deps=list(deps) + toks[-1:])
        ev = P.add("scalar",
                   (lambda e, ci=ci:
                    e.activation(cbe4_view[:, ci * CH:ci * CH + CH],
                                 pv, AF.Copy)),
                   deps=[mm])
        toks.append(ev)
    return toks


def emit_moe8(P, b, f_buf, expw_ap, esel_view, combt_view, cbe4_view, wring,
              deps, gate_deps, fe_buf=None, wdeps=()):
    out = []
    orot = PsumRot([b["PSB"][:, 0:512], b["PSC"][:, 0:512]])
    t1 = b["F32A"][:, 512:1024]
    cb_tok = []
    prev_t1 = []
    cbe4_readers = []
    for e_ in range(8):
        cb_tok = emit_comb_bc(P, b, esel_view, combt_view, 8, e_, cbe4_view,
                              joined(gate_deps, cb_tok, cbe4_readers))
        cbe4_readers = []
        for mo in range(NT):
            wsl, wi, wtok = wring.load(expw_ap[e_][:, mo * 128:(mo + 1) * 128],
                                       NT, deps=wdeps)
            last = None
            for ci in range(NCH):
                si, pv, sfree = orot.next()

                def gmm(e, wsl=wsl, pv=pv, ci=ci):
                    r = None
                    for kt in range(NT):
                        r = e.matmul(pv, wsl[:, kt * 128:(kt + 1) * 128],
                                     f_buf[:, kt * S + ci * CH:
                                           kt * S + ci * CH + CH],
                                     start=(kt == 0), stop=(kt == NT - 1))
                    return r
                mm = P.add("tensor", gmm,
                           deps=[wtok] + list(deps) + sfree, fifo=True)
                tta = P.add("vector",
                            (lambda e, pv=pv, ci=ci:
                             e.tensor_tensor(t1, pv,
                                             cbe4_view[:, ci * CH:ci * CH + CH],
                                             ALU.mult)),
                            deps=[mm] + prev_t1 + cb_tok)
                ttb = P.add("vector",
                            (lambda e, mo=mo, ci=ci:
                             e.tensor_tensor(
                                 b["X"][:, mo * S + ci * CH:mo * S + ci * CH + CH],
                                 b["X"][:, mo * S + ci * CH:mo * S + ci * CH + CH],
                                 t1, ALU.add)),
                            deps=[tta])
                prev_t1 = [ttb]
                cbe4_readers = [tta]
                orot.done(si, tta)
                last = ttb
                out.append(ttb)
            wring.used(wi, last)
    return out


def emit_moe16(P, b, y_buf, w13_ap, w2_ap, w13s_ap, w2s_ap, esel_view,
               combt_view, cbe4_view, sg_view, h_buf, wring, w2ring, deps,
               gate_deps):
    out = []
    t1 = b["F32A"][:, 512:1024]
    grot = PsumRot([b["PSB"][:, 0:512], b["PSC"][:, 0:512],
                    b["PSD"][:, 0:512], b["PSA"][:, 1536:2048]])
    orot = PsumRot([b["PSA"][:, 0:512], b["PSA"][:, 512:1024]])
    prev_t1 = []
    prev_sg = {0: [], 1: []}
    cbe4_readers = []
    groups = [(2 * g, 2 * g + 1) for g in range(8)] + [(16,)]
    for grp in groups:
        for slot, e_ in enumerate(grp):
            if e_ < 16:
                cb = emit_comb_bc(P, b, esel_view, combt_view, 16, e_,
                                  cbe4_view,
                                  joined(gate_deps, cbe4_readers),
                                  ps_view=b["PSA"][:, 1024:1536])
            else:
                cb = [P.add("vector",
                            lambda e: e.memset(cbe4_view[:, 0:2048], 0.5),
                            deps=joined(gate_deps, cbe4_readers))]
            cbe4_readers = []
            w13 = w13_ap[e_] if e_ < 16 else w13s_ap
            for mg in range(4):
                wg, wgi, wgt = wring.load(w13[:, mg * 128:(mg + 1) * 128], NT)
                wu, wui, wut = wring.load(
                    w13[:, (mg + 4) * 128:(mg + 5) * 128], NT)
                lastg = lastu = None
                for ci in range(NCH):
                    gi, gv, gfree = grot.next()
                    ui, uv, ufree = grot.next()

                    def gmm(e, wsl=wg, pv=gv, ci=ci):
                        r = None
                        for kt in range(NT):
                            r = e.matmul(pv, wsl[:, kt * 128:(kt + 1) * 128],
                                         y_buf[:, kt * S + ci * CH:
                                               kt * S + ci * CH + CH],
                                         start=(kt == 0), stop=(kt == NT - 1))
                        return r
                    mmg = P.add("tensor", gmm, deps=[wgt] + list(deps) + gfree,
                                fifo=True)

                    def umm(e, wsl=wu, pv=uv, ci=ci):
                        r = None
                        for kt in range(NT):
                            r = e.matmul(pv, wsl[:, kt * 128:(kt + 1) * 128],
                                         y_buf[:, kt * S + ci * CH:
                                               kt * S + ci * CH + CH],
                                         start=(kt == 0), stop=(kt == NT - 1))
                        return r
                    mmu = P.add("tensor", umm, deps=[wut] + list(deps) + ufree,
                                fifo=True)
                    sgv = sg_view[:, (ci % 2) * 512:(ci % 2) * 512 + 512]
                    sg = P.add("scalar",
                               (lambda e, gv=gv, sgv=sgv:
                                e.activation(sgv, gv, AF.Silu)),
                               deps=[mmg] + prev_sg[ci % 2])
                    up = P.add("vector",
                               (lambda e, uv=uv, ci=ci:
                                e.tensor_tensor(
                                    t1, uv,
                                    cbe4_view[:, ci * CH:ci * CH + CH],
                                    ALU.mult)),
                               deps=[mmu] + prev_t1 + cb)
                    hc0 = slot * (4 * S) + mg * S + ci * CH
                    hh = P.add("vector",
                               (lambda e, hc0=hc0, sgv=sgv:
                                e.tensor_tensor(h_buf[:, hc0:hc0 + CH], t1,
                                                sgv, ALU.mult)),
                               deps=[up, sg])
                    prev_t1 = [hh]
                    prev_sg[ci % 2] = [hh]
                    cbe4_readers.append(up)
                    grot.done(gi, sg)
                    grot.done(ui, up)
                    lastg, lastu = hh, hh
                wring.used(wgi, lastg)
                wring.used(wui, lastu)
        w2s = []
        for slot, e_ in enumerate(grp):
            w2s.append(w2_ap[e_] if e_ < 16 else w2s_ap)
        for mo in range(NT):
            wts = []
            for slot, e_ in enumerate(grp):
                wsl, wi, wtok = w2ring.load(w2s[slot][:, mo * 128:(mo + 1) * 128],
                                            4)
                wts.append((wsl, wi, wtok))
            last = None
            for ci in range(NCH):
                oi, pv, ofree = orot.next()

                def gmm(e, pv=pv, ci=ci, wts=wts, grp=grp):
                    r = None
                    n = len(grp) * 4
                    i = 0
                    for slot in range(len(grp)):
                        wsl = wts[slot][0]
                        for kt in range(4):
                            hc0 = slot * (4 * S) + kt * S + ci * CH
                            r = e.matmul(pv, wsl[:, kt * 128:(kt + 1) * 128],
                                         h_buf[:, hc0:hc0 + CH],
                                         start=(i == 0), stop=(i == n - 1))
                            i += 1
                    return r
                hdeps = [prev_t1[0]] if prev_t1 else []
                mm = P.add("tensor", gmm,
                           deps=[w[2] for w in wts] + hdeps + ofree, fifo=True)
                ev = P.add("vector",
                           (lambda e, mo=mo, ci=ci, pv=pv:
                            e.tensor_tensor(
                                b["X"][:, mo * S + ci * CH:mo * S + ci * CH + CH],
                                b["X"][:, mo * S + ci * CH:mo * S + ci * CH + CH],
                                pv, ALU.add)),
                           deps=[mm])
                orot.done(oi, ev)
                last = ev
                out.append(ev)
            for wsl, wi, wtok in wts:
                w2ring.used(wi, last)
        prev_t1 = [last]
        prev_sg = {0: [last], 1: [last]}
    return out



def emit_rwkv(P, b, y_buf, wk_ap, wv_ap, wr_ap, td_view, r_buf, rs_buf,
              wring, wbc_view, deps):
    out = []
    wcol = b["COL"][:, 8:16]
    t0 = P.add("scalar", lambda e: e.activation(wcol, td_view, AF.Exp),
               deps=list(deps))
    t1 = P.add("vector",
               lambda e: e.tensor_scalar(wcol, wcol, -1.0, None, ALU.mult),
               deps=[t0])
    lastc = b["COL"][:, 6:7]
    rs_toks = []
    for j in range(NT):
        wsl, wi, wtok = wring.load(wk_ap[:, j * 128:(j + 1) * 128], NT)

        def gk(e, wsl=wsl):
            r = None
            for ci in range(NCH):
                for kt in range(NT):
                    r = e.matmul(b["PSA"][:, ci * CH:ci * CH + CH],
                                 wsl[:, kt * 128:(kt + 1) * 128],
                                 y_buf[:, kt * S + ci * CH:kt * S + ci * CH + CH],
                                 start=(kt == 0), stop=(kt == NT - 1))
            return r
        kdep = [rs_toks[-1]] if rs_toks else []
        kmm = P.add("tensor", gk, deps=[wtok] + list(deps) + kdep, fifo=True)
        wring.used(wi, kmm)
        wb0 = P.add("vector", lambda e: e.memset(wbc_view, 1.0),
                    deps=kdep)
        wb1 = P.add("vector",
                    (lambda e, j=j:
                     e.tensor_scalar(wbc_view, wbc_view, wcol[:, j:j + 1],
                                     None, ALU.mult)),
                    deps=[wb0, t1])
        wsl2, wi2, wtok2 = wring.load(wr_ap[:, j * 128:(j + 1) * 128], NT)
        rtoks = []
        for ci in range(NCH):
            def gr(e, wsl2=wsl2, ci=ci):
                r = None
                for kt in range(NT):
                    r = e.matmul(b["PSD"][:, 0:512],
                                 wsl2[:, kt * 128:(kt + 1) * 128],
                                 y_buf[:, kt * S + ci * CH:kt * S + ci * CH + CH],
                                 start=(kt == 0), stop=(kt == NT - 1))
                return r
            rdep = rtoks[-1:] if rtoks else (rs_toks[-1:] if rs_toks else [])
            rmm = P.add("tensor", gr, deps=[wtok2] + list(deps) + rdep,
                        fifo=True)
            rt = P.add("scalar",
                       (lambda e, j=j, ci=ci:
                        e.activation(r_buf[:, j * S + ci * CH:j * S + ci * CH + CH],
                                     b["PSD"][:, 0:512], AF.Sigmoid)),
                       deps=[rmm])
            rtoks.append(rt)
        wring.used(wi2, rtoks[-1])
        sc_prev = None
        for ci in range(NCH):
            sv = b["PSB"] if ci % 2 == 0 else b["PSC"]
            init = 0.0 if ci == 0 else lastc[0:128, 0:1]
            sc = P.add("vector",
                       (lambda e, sv=sv, ci=ci, init=init:
                        e.tensor_tensor_scan(sv[:, 0:512], wbc_view,
                                             b["PSA"][:, ci * CH:ci * CH + CH],
                                             init, ALU.mult, ALU.add)),
                       deps=[kmm, wb1] + ([sc_prev] if sc_prev else [])
                       + (rs_toks[-2:] if rs_toks else []))
            cp = P.add("vector",
                       (lambda e, sv=sv:
                        e.tensor_copy(lastc, sv[:, 511:512])),
                       deps=[sc])
            rs = P.add("vector",
                       (lambda e, sv=sv, j=j, ci=ci:
                        e.tensor_tensor(
                            rs_buf[:, j * S + ci * CH:j * S + ci * CH + CH],
                            sv[:, 0:512],
                            r_buf[:, j * S + ci * CH:j * S + ci * CH + CH],
                            ALU.mult)),
                       deps=[sc, rtoks[ci], cp])
            sc_prev = cp
            rs_toks.append(rs)
    orot = PsumRot([b["PSA"][:, i * 512:(i + 1) * 512] for i in range(4)])
    for mo in range(NT):
        wsl, wi, wtok = wring.load(wv_ap[:, mo * 128:(mo + 1) * 128], NT)
        last = None
        for ci in range(NCH):
            oi, pv, ofree = orot.next()

            def gmm(e, wsl=wsl, pv=pv, ci=ci):
                r = None
                for kt in range(NT):
                    r = e.matmul(pv, wsl[:, kt * 128:(kt + 1) * 128],
                                 rs_buf[:, kt * S + ci * CH:kt * S + ci * CH + CH],
                                 start=(kt == 0), stop=(kt == NT - 1))
                return r
            mm = P.add("tensor", gmm,
                       deps=[wtok] + rs_toks[-1:] + ofree, fifo=True)
            ev = P.add("vector",
                       (lambda e, mo=mo, ci=ci, pv=pv:
                        e.tensor_tensor(
                            b["X"][:, mo * S + ci * CH:mo * S + ci * CH + CH],
                            b["X"][:, mo * S + ci * CH:mo * S + ci * CH + CH],
                            pv, ALU.add)),
                       deps=[mm])
            orot.done(oi, ev)
            last = ev
            out.append(ev)
        wring.used(wi, last)
    return out



def emit_fold(P, b, w_dram, rtb_view, fst_view, scratch_ap, itiles, wring,
              deps):
    toks = []
    for i in range(itiles):
        wsl, wi, wtok = wring.load(w_dram[:, i * 128:(i + 1) * 128], NT)
        evs = []
        for n in range(2):
            ps = b["PSB"] if n == 0 else b["PSC"]

            def gmm(e, wsl=wsl, n=n, ps=ps):
                r = None
                for kt in range(NT):
                    r = e.matmul(ps[:, 0:512],
                                 wsl[:, kt * 128:(kt + 1) * 128],
                                 rtb_view[:, kt * 1024 + n * 512:
                                          kt * 1024 + n * 512 + 512],
                                 start=(kt == 0), stop=(kt == NT - 1))
                return r
            mm = P.add("tensor", gmm, deps=[wtok] + list(deps) + toks[-1:]
                       + evs[-2:-1], fifo=True)
            ev = P.add("scalar",
                       (lambda e, n=n, ps=ps:
                        e.activation(fst_view[:, n * 512:n * 512 + 512],
                                     ps[:, 0:512], AF.Copy)),
                       deps=[mm])
            evs.append(ev)
        st = P.add("sync",
                   (lambda e, i=i:
                    e.dma_start(scratch_ap[i * 128:(i + 1) * 128, :],
                                fst_view[:, 0:1024])),
                   deps=evs, ctr=P.ctr("fsd"), n=16)
        toks.append(st)
        wring.used(wi, st)
    return toks



def emit_sl_l0ffn(P, b, dr, pfx, deps_in):
    def qslot(h):
        return ((h % 4) + 4 * (h // 8), (h // 4) % 2)

    def kslot(h):
        return ((h // 4) // 2, (h // 4) % 2)

    def vblk(h):
        return h // 4

    def pair_heads(mi):
        return (2 * mi, 2 * mi + 1)

    ln = emit_layernorm(P, b, b["A4"], b["A1"][:, 0:2048], deps_in)
    rm1 = emit_rmsnorm(P, b, b["A4"], b["A1"],
                       lambda j: b["NORMW"][:, j:j + 1],
                       b["A3"][:, 12352:13376], joined(ln, deps_in))
    gw = b["A3"][:, 13888:13952]
    gwt = P.add("gpsimd",
                (lambda e: e.dma_start(
                    gw.rearrange("p (j m) -> p j m", j=NT),
                    dr[pfx + "gwT"].rearrange("(j p) m -> p j m", p=128))),
                deps=deps_in, ctr=P.ctr("gwd"), n=16)
    wring = WRing(P, b["A3"], 8256, 4, 1024, tag="wA")
    pr = PsumRot([b["PSB"][:, 0:512], b["PSC"][:, 0:512], b["PSD"][:, 0:512]])
    ktoks = emit_proj(P, b, wring, dr[pfx + "wkT"], b["A1"], C, 256, pr,
                      mk_evac_to(P, b["A3"], "scalar"), deps_src=rm1,
                      wdeps=deps_in)
    vt1 = P.add("vector", lambda e: e.memset(b["A3"][:, 4096:8256], 1.0),
                deps=joined(rm1, deps_in))
    vtoks = emit_proj(P, b, wring, dr[pfx + "wvT"], b["A1"], C, 256, pr,
                      mk_evac_vtok(P, b, b["A3"][:, 4096:8256], pair_heads,
                                   b["A3"][:, 13376:13888],
                                   b["A3"][:, 14336:14848]),
                      deps_src=joined(rm1, [vt1]), wdeps=deps_in)
    qtoks = emit_proj(P, b, wring, dr[pfx + "wqT"], b["A1"], C, C, pr,
                      mk_evac_to(P, b["A2"], "scalar"), deps_src=rm1,
                      wdeps=deps_in)
    a_dead = joined(ktoks, vtoks, qtoks)
    cmskt = b["A1"][:, 5120:7168]
    cmt = P.add("gpsimd", lambda e: e.dma_start(cmskt, dr["c_cmskt"]),
                deps=a_dead, ctr=P.ctr("cmd"), n=16)
    at = emit_attn(P, b, b["A2"], b["A3"], b["A3"][:, 4096:8256], b["A2"],
                   qslot, kslot, vblk, True, joined(a_dead, [cmt]),
                   b["A1"][:, 0:5120], cmskt=cmskt)
    wring2 = WRing(P, b["A3"], 8256, 4, 1024, tag="wA")

    def evac_h(mi, ci, pv, deps):
        c0 = mi * S + ci * CH
        e1 = P.add("vector",
                   (lambda e, c0=c0, pv=pv:
                    e.tensor_tensor(b["A4"][:, c0:c0 + CH],
                                    b["A4"][:, c0:c0 + CH], pv, ALU.add)),
                   deps=deps)
        e2 = P.add("vector",
                   (lambda e, c0=c0:
                    e.scalar_tensor_tensor(b["X"][:, c0:c0 + CH],
                                           b["A4"][:, c0:c0 + CH], 0.5,
                                           b["X"][:, c0:c0 + CH],
                                           ALU.mult, ALU.add)),
                   deps=[e1])
        return e2
    pr2 = PsumRot([b["PSB"][:, 0:512], b["PSC"][:, 0:512], b["PSD"][:, 0:512]])
    htoks = emit_proj(P, b, wring2, dr[pfx + "woT"], b["A2"], C, C, pr2,
                      evac_h, deps_src=at, wdeps=joined(at))
    rm2 = emit_rmsnorm(P, b, b["A4"], b["A2"],
                       lambda j: b["NORMW"][:, 8 + j:9 + j],
                       b["A1"][:, 0:1024], joined(htoks))
    esel = b["A1"][:, 9472:11520]
    cbe4 = b["A1"][:, 11520:13568]
    combt = b["A1"][:, 7424:9472]
    cmb = b["A1"][:, 7168:7184]
    est = P.add("gpsimd", lambda e: e.dma_start(esel[0:16, :], dr["c_esel"]),
                deps=joined(htoks), ctr=P.ctr("esd"), n=16)
    g8 = emit_gate8(P, b, b["A2"], gw, b["F32A"][:, 0:64], cmb, combt,
                    joined(rm2, [gwt]))
    wring3 = WRing(P, b["A1"], 0, 4, 1024, tag="wM")
    m8 = emit_moe8(P, b, b["A2"], dr[pfx + "expwT"], esel, combt, cbe4,
                   wring3, joined(rm2), joined(g8, [est]),
                   fe_buf=b["A3"], wdeps=joined(rm2))
    return joined(m8, htoks)


def emit_sl_l0attn(P, b, dr, scr_q, scr_kb, deps_in):
    def slot(h):
        return (h // 2, h % 2)

    def vblk(h):
        return h

    def pair_heads(mi):
        return (2 * mi, 2 * mi + 1)

    ln = emit_layernorm(P, b, b["A4"], b["A3"][:, 0:2048], deps_in)
    wring = WRing(P, b["C8B"], 0, 2, 1024, tag="wC")
    pr = PsumRot([b["PSB"][:, 0:512], b["PSC"][:, 0:512], b["PSD"][:, 0:512]])
    t2t = emit_proj(P, b, wring, dr["l0a_wvaT"], b["A4"], C, 256, pr,
                    mk_evac_to(P, b["A2"], "scalar"), deps_src=ln,
                    wdeps=deps_in)
    vt1 = P.add("vector", lambda e: e.memset(b["A1"][:, 0:A1COLS], 1.0),
                deps=joined(ln, deps_in))
    vtt = emit_proj(P, b, wring, dr["l0a_wvbT"], b["A2"], 256, C, pr,
                    mk_evac_vtok(P, b, b["A1"], pair_heads,
                                 b["A2"][:, 8192:8704],
                                 b["A2"][:, 8704:9216]),
                    deps_src=joined(t2t, [vt1]), wdeps=deps_in)
    t1t = emit_proj(P, b, wring, dr["l0a_wkaT"], b["A4"], C, 256, pr,
                    mk_evac_to(P, b["A2"], "scalar", dst_col0=4096),
                    deps_src=ln, wdeps=deps_in)
    ktt = emit_proj(P, b, wring, scr_kb, b["A2"], 256, C, pr,
                    mk_evac_to(P, b["A3"], "scalar"), deps_src=t1t,
                    src_col0=4096, wdeps=joined(ln, deps_in))
    qtt = emit_proj(P, b, wring, scr_q, b["A4"], C, C, pr,
                    mk_evac_to(P, b["A2"], "scalar"),
                    deps_src=joined(ln, ktt, vtt), wdeps=deps_in)
    at = emit_attn(P, b, b["A2"], b["A3"], b["A1"], b["A2"],
                   slot, slot, vblk, False, joined(qtt, ktt, vtt),
                   b["A4"][:, 0:5120])
    wring2 = WRing(P, b["C8B"], 0, 2, 1024, tag="wC")
    pr2 = PsumRot([b["PSB"][:, 0:512], b["PSC"][:, 0:512], b["PSD"][:, 0:512]])
    ot = emit_proj(P, b, wring2, dr["l0a_woT"], b["A2"], C, C, pr2,
                   mk_evac_resid(P, b), deps_src=at, wdeps=joined(at))
    return joined(ot)


def emit_sl_l1ffn(P, b, dr, gbcb, deps_in):
    esel = b["A1"][:, 6144:8192]
    combt = b["A1"][:, 8192:10240]
    cbe4 = b["A1"][:, 10240:12288]
    sg = b["A1"][:, 12288:13312]
    gw = b["A1"][:, 13312:13440]
    cmb = b["A1"][:, 13440:13456]
    ln = emit_layernorm(P, b, b["A4"], b["A2"][:, 0:2048], deps_in)
    est = P.add("gpsimd", lambda e: e.dma_start(esel[0:16, :], dr["c_esel"]),
                deps=deps_in, ctr=P.ctr("esd"), n=16)
    gwt = P.add("gpsimd",
                (lambda e: e.dma_start(
                    gw.rearrange("p (j m) -> p j m", j=NT),
                    dr["l1f_gwT"].rearrange("(j p) m -> p j m", p=128))),
                deps=deps_in, ctr=P.ctr("gwd"), n=16)
    g16 = emit_gate16(P, b, b["A4"], gw, gbcb[:, 0:16], b["F32A"][:, 0:64],
                      cmb, combt, joined(ln, [gwt, est]))
    wring = WRing(P, b["A1"], 0, 4, 1024, tag="wA")
    w2ring = WRing(P, b["A1"], 4096, 4, 512, tag="wB")
    m16 = emit_moe16(P, b, b["A4"], dr["l1f_w13T"], dr["l1f_w2T"],
                     dr["l1f_w13sT"], dr["l1f_w2sT"], esel, combt, cbe4, sg,
                     b["A2"], wring, w2ring, joined(ln, deps_in), joined(g16))
    return joined(m16)


def emit_sl_l1attn(P, b, dr, deps_in):
    ln = emit_layernorm(P, b, b["A4"], b["A2"][:, 0:2048], deps_in)
    wring = WRing(P, b["A1"], 0, 4, 1024, tag="wA")
    rw = emit_rwkv(P, b, b["A4"], dr["l1a_wkT"], dr["l1a_wvT"],
                   dr["l1a_wrT"], b["NORMW"][:, 16:24], b["A3"], b["A2"],
                   wring, b["F32A"][:, 0:512], joined(ln, deps_in))
    return joined(rw)



W_SHAPES = {
    "l0f_wqT": (C, C), "l0f_wkT": (C, 256), "l0f_wvT": (C, 256),
    "l0f_woT": (C, C), "l0f_gwT": (C, 8), "l0f_expwT": (8, C, C),
    "l0f_nw": (128, 16),
    "l0a_wq": (C, C), "l0a_wkaT": (C, 256), "l0a_wkb": (C, 256),
    "l0a_wvaT": (C, 256), "l0a_wvbT": (256, C), "l0a_woT": (C, C),
    "l1f_gwT": (C, 16), "l1f_gbc": (128, 16), "l1f_w13T": (16, C, C),
    "l1f_w2T": (16, 512, C), "l1f_w13sT": (C, C), "l1f_w2sT": (512, C),
    "l1a_wkT": (C, C), "l1a_wvT": (C, C), "l1a_wrT": (C, C),
    "l1a_td": (128, 8),
}
CONST_NAMES = ("c_idn", "c_onesc", "c_onesf", "c_onescf", "c_cmskt",
               "c_esel", "c_rt")


def _declare(P, cs, kinds):
    nc = P.nc
    dr = {}
    dr["x"] = nc.dram_tensor("x", [BPC, C, S], F32, kind="ExternalInput").ap()
    dr["xo"] = nc.dram_tensor("xo", [BPC, C, S], F32,
                              kind="ExternalOutput").ap()
    names = ["x"]
    for nm, shp in W_SHAPES.items():
        if nm.split("_")[0] in kinds:
            dr[nm] = nc.dram_tensor(nm, list(shp), F32,
                                    kind="ExternalInput").ap()
            names.append(nm)
    need_c = {"l0f": ("c_idn", "c_onesc", "c_onesf", "c_onescf", "c_cmskt",
                      "c_esel"),
              "l0a": ("c_idn", "c_onesc", "c_onesf", "c_onescf", "c_rt"),
              "l1f": ("c_idn", "c_onesc", "c_onesf", "c_onescf", "c_esel"),
              "l1a": ("c_idn", "c_onesc", "c_onesf", "c_onescf")}
    cset = []
    for k in kinds:
        cset += list(need_c[k])
    for nm in CONST_NAMES:
        if nm in cset:
            dr[nm] = nc.dram_tensor(nm, list(cs[nm].shape),
                                    BF16 if cs[nm].dtype == BF16NP else F32,
                                    kind="ExternalInput").ap()
            names.append(nm)
    return dr, names


def _prologues(P, b, dr, kinds):
    st = {}
    toks = []
    if "l0f" in kinds:
        t = P.add("gpsimd",
                  lambda e: e.dma_start(b["NORMW"][:, 0:16], dr["l0f_nw"]),
                  ctr=P.ctr("nwd"), n=16)
        toks.append(t)
    if "l1a" in kinds:
        t = P.add("gpsimd",
                  lambda e: e.dma_start(b["NORMW"][:, 16:24], dr["l1a_td"]),
                  ctr=P.ctr("tdd"), n=16)
        toks.append(t)
    if "l1f" in kinds:
        st["gbcb"] = P.sbuf("GBC", [128, 16], F32)
        t = P.add("gpsimd",
                  lambda e: e.dma_start(st["gbcb"][:, :], dr["l1f_gbc"]),
                  ctr=P.ctr("gbd"), n=16)
        toks.append(t)
    if "l0a" in kinds:
        nc = P.nc
        st["scr_q"] = nc.dram_tensor("scr_q", [C, C], BF16,
                                     kind="Internal").ap()
        st["scr_kb"] = nc.dram_tensor("scr_kb", [256, C], BF16,
                                      kind="Internal").ap()
        rtb = b["A3"][:, 0:8192]
        rtt = P.add("gpsimd",
                    (lambda e: e.dma_start(
                        rtb.rearrange("p (kt m) -> p kt m", kt=NT),
                        dr["c_rt"].rearrange("(kt p) m -> p kt m", p=128))),
                    ctr=P.ctr("rtd"), n=16)
        wringf = WRing(P, b["A1"], 0, 2, 1024, tag="wF")
        fst = b["A1"][:, 2048:3072]
        f1 = emit_fold(P, b, dr["l0a_wq"], rtb, fst, st["scr_q"], NT, wringf,
                       [rtt])
        f2 = emit_fold(P, b, dr["l0a_wkb"], rtb, fst, st["scr_kb"], 2, wringf,
                       [rtt] + f1[-1:])
        toks += f1 + f2
    st["toks"] = toks
    return st


def build_prog(cs, seq):
    P = Prog()
    kinds = sorted(set(seq))
    dr, names = _declare(P, cs, kinds)
    b = make_bufs(P)
    ct = emit_const_loads(P, b, dr)
    st = _prologues(P, b, dr, kinds)
    prev_store = []
    for bi in range(BPC):
        lx = emit_load_x(P, b, dr["x"], bi,
                         deps=joined(ct, prev_store, st["toks"]))
        t = joined(lx, ct, st["toks"], prev_store)
        for kind in seq:
            if kind == "l0f":
                t = emit_sl_l0ffn(P, b, dr, "l0f_", t)
            elif kind == "l0a":
                t = emit_sl_l0attn(P, b, dr, st["scr_q"], st["scr_kb"], t)
            elif kind == "l1f":
                t = emit_sl_l1ffn(P, b, dr, st["gbcb"], t)
            elif kind == "l1a":
                t = emit_sl_l1attn(P, b, dr, t)
        prev_store = emit_store_x(P, b, dr["xo"], bi, deps=t)
    P.realize()
    return P.nc, names



def _t(a):
    return np.ascontiguousarray(np.asarray(a, np.float32).T)


def _qperm_cols():
    perm = np.zeros(C, dtype=np.int64)
    for h in range(NH):
        tile = (h % 4) + 4 * (h // 8)
        par = (h // 4) % 2
        slot = 128 * tile + 64 * par
        perm[slot:slot + 64] = np.arange(64 * h, 64 * h + 64)
    return perm


def host_prep(inputs):
    _ = inputs["L1_ffn_input_ids"]
    _ = inputs["L1_attn_time_first"]
    cs = host_consts()
    qp = _qperm_cols()
    w = {
        "l0f_wqT": np.ascontiguousarray(_t(inputs["L0_ffn_wq"])[:, qp]),
        "l0f_wkT": _t(inputs["L0_ffn_wk"]),
        "l0f_wvT": _t(inputs["L0_ffn_wv"]),
        "l0f_woT": np.ascontiguousarray(_t(inputs["L0_ffn_wo"])[qp, :]),
        "l0f_gwT": _t(inputs["L0_ffn_gate_w"]),
        "l0f_expwT": np.ascontiguousarray(
            np.asarray(inputs["L0_ffn_expert_w"],
                       np.float32).transpose(0, 2, 1)),
        "l0f_nw": np.concatenate([featvec(inputs["L0_ffn_attn_norm_w"]),
                                  featvec(inputs["L0_ffn_ffn_norm_w"])],
                                 axis=1),
        "l0a_wq": np.ascontiguousarray(
            np.asarray(inputs["L0_attn_wq"], np.float32)),
        "l0a_wkaT": _t(inputs["L0_attn_wk_a"]),
        "l0a_wkb": np.ascontiguousarray(
            np.asarray(inputs["L0_attn_wk_b"], np.float32)),
        "l0a_wvaT": _t(inputs["L0_attn_wv_a"]),
        "l0a_wvbT": _t(inputs["L0_attn_wv_b"]),
        "l0a_woT": _t(inputs["L0_attn_wo"]),
        "l1f_gwT": _t(inputs["L1_ffn_gate_w"]),
        "l1f_gbc": np.ascontiguousarray(
            np.broadcast_to(np.asarray(inputs["L1_ffn_gate_b"], np.float32),
                            (128, 16))),
        "l1f_w13T": np.ascontiguousarray(
            np.asarray(inputs["L1_ffn_w13"], np.float32).transpose(0, 2, 1)),
        "l1f_w2T": np.ascontiguousarray(
            np.asarray(inputs["L1_ffn_w2"], np.float32).transpose(0, 2, 1)),
        "l1f_w13sT": _t(inputs["L1_ffn_w13_s"]),
        "l1f_w2sT": _t(inputs["L1_ffn_w2_s"]),
        "l1a_wkT": _t(inputs["L1_attn_Wk"]),
        "l1a_wvT": _t(inputs["L1_attn_Wv"]),
        "l1a_wrT": _t(inputs["L1_attn_Wr"]),
        "l1a_td": featvec(inputs["L1_attn_time_decay"]),
    }
    return cs, w


FUSED = True


def kernel(**inputs):
    cs, w = host_prep(inputs)
    x = np.ascontiguousarray(
        np.asarray(inputs["L0_attn_x"], np.float32).transpose(0, 2, 1))
    shards = [np.ascontiguousarray(x[c * BPC:(c + 1) * BPC])
              for c in range(NCORES)]

    def run(nc, names, xs):
        in_maps = []
        for c in range(NCORES):
            m = {"x": xs[c]}
            for nm in names:
                if nm == "x":
                    continue
                m[nm] = w[nm] if nm in w else cs[nm]
            in_maps.append(m)
        res = run_bass_kernel_spmd(nc, in_maps, core_ids=list(range(NCORES)))
        return [r["xo"] for r in res.results]

    if FUSED:
        nc, names = build_prog(cs, ["l0f", "l0a", "l0f", "l1f", "l1a", "l1f"])
        shards = run(nc, names, shards)
    else:
        progs = {k: build_prog(cs, [k]) for k in ("l0f", "l0a", "l1f", "l1a")}
        for kind in ["l0f", "l0a", "l0f", "l1f", "l1a", "l1f"]:
            nc, names = progs[kind]
            shards = run(nc, names, shards)

    out = np.concatenate(shards, axis=0)
    return np.ascontiguousarray(out.transpose(0, 2, 1)).astype(np.float32)
